# version 03
# baseline (speedup 1.0000x reference)
import numpy as np
import ml_dtypes
import concourse.bass as bass
import concourse.mybir as mybir
from concourse.bass_utils import run_bass_kernel_spmd

F32 = mybir.dt.float32
BF16 = mybir.dt.bfloat16
I32 = mybir.dt.int32
AF = mybir.ActivationFunctionType
ALU = mybir.AluOpType
BF = ml_dtypes.bfloat16

NCORES = 8
B, T, D = 4, 4096, 1024
NTOK = B * T
S = NTOK // NCORES
EPS = 1e-5


def ceil_div(a, b):
    return -(-a // b)

_UID = [0]


def _nm(base):
    _UID[0] += 1
    return f"{base}_{_UID[0]}"


def mm(nc, out, lhsT, rhs, start, stop, tile_position=None):
    n = rhs.shape[-1]
    kw = {} if tile_position is None else {"tile_position": tile_position}
    for j in range(ceil_div(n, 512)):
        cols = slice(j * 512, min((j + 1) * 512, n))
        nc.tensor.matmul(out[..., cols], lhsT, rhs[..., cols],
                         start=start, stop=stop, skip_group_check=True, **kw)


class Prog:

    def __init__(self, nc):
        self.nc = nc
        self.dma_sems = {}
        self.dma_counts = {}
        self.cc_uid = 0

    def dma(self, qeng, out, in_, **kw):
        key = qeng.engine.name
        if key not in self.dma_sems:
            self.dma_sems[key] = (qeng, self.nc.alloc_semaphore(f"dmasem_{key}"))
            self.dma_counts[key] = 0
        _, sem = self.dma_sems[key]
        qeng.dma_start(out=out, in_=in_, **kw).then_inc(sem, 16)
        self.dma_counts[key] += 16
        return sem

    def dma_wait_all(self):
        for key, (eng, sem) in self.dma_sems.items():
            eng.wait_ge(sem, self.dma_counts[key])

    def barrier(self):
        self.dma_wait_all()
        self.nc.all_engine_barrier()

    def allgather(self, in_ap2d, out_ap2d):
        self.cc_uid += 1
        sem = self.nc.alloc_semaphore(f"ccsem_{self.cc_uid}")
        self.nc.gpsimd.collective_compute(
            "AllGather", mybir.AluOpType.bypass,
            replica_groups=[list(range(8))],
            ins=[in_ap2d.opt()],
            outs=[out_ap2d.opt()],
        ).then_inc(sem)
        return sem

    def cc_wait(self, sem):
        self.nc.gpsimd.wait_ge(sem, 1)
        self.nc.all_engine_barrier()


def emit_ln_fm(p, xb_out, sq_buf, row_r, row_mr, t_cols, nchunk=8,
               x_src_chunks=None):
    nc = p.nc
    dmodel = 128 * nchunk
    with (
        nc.sbuf_tensor(_nm("ln_ones"), [128, 1], BF16) as ones,
        nc.sbuf_tensor(_nm("ln_m"), [1, t_cols], F32) as row_m,
        nc.sbuf_tensor(_nm("ln_v"), [1, t_cols], F32) as row_v,
        nc.psum_tensor(_nm("ln_ps"), [1, t_cols], F32) as ps_s,
        nc.psum_tensor(_nm("ln_ps2"), [1, t_cols], F32) as ps_q,
    ):
        nc.vector.memset(ones[:, :], 1.0)
        p.barrier()
        ncol = ceil_div(t_cols, 512)
        src = x_src_chunks if x_src_chunks is not None else [
            xb_out[:, c] for c in range(nchunk)]
        for c in range(nchunk):
            nc.scalar.activation(out=sq_buf[:, :], in_=src[c], func=AF.Square)
            p.barrier()
            for j in range(ncol):
                cols = slice(j * 512, min((j + 1) * 512, t_cols))
                nc.tensor.matmul(ps_s[0:1, cols], ones[:, :], src[c][:, cols],
                                 start=(c == 0), stop=(c == nchunk - 1),
                                 skip_group_check=True)
                nc.tensor.matmul(ps_q[0:1, cols], ones[:, :], sq_buf[:, cols],
                                 start=(c == 0), stop=(c == nchunk - 1),
                                 skip_group_check=True)
            p.barrier()
        m = row_m[0:1, :]
        var = row_v[0:1, :]
        nc.vector.tensor_scalar_mul(m, ps_s[0:1, :], 1.0 / dmodel)
        p.barrier()
        nc.vector.tensor_mul(var, m, m)
        p.barrier()
        nc.vector.scalar_tensor_tensor(out=var, in0=ps_q[0:1, :], scalar=1.0 / dmodel,
                                       in1=var, op0=ALU.mult, op1=ALU.subtract)
        p.barrier()
        nc.vector.tensor_scalar_add(var, var, EPS)
        p.barrier()
        nc.scalar.activation(out=var, in_=var, func=AF.Sqrt)
        p.barrier()
        nc.vector.reciprocal(row_r[0:1, :], var)
        p.barrier()
        nc.vector.tensor_mul(row_mr[0:1, :], m, row_r[0:1, :])
        p.barrier()


def emit_bcast_row(p, row_f32, ps_out, t_cols, row_bf):
    nc = p.nc
    with nc.sbuf_tensor(_nm("bc_ones"), [1, 128], BF16) as ones1:
        nc.vector.memset(ones1[:, :], 1.0)
        nc.vector.tensor_copy(out=row_bf[0:1, 0:t_cols], in_=row_f32)
        p.barrier()
        for j in range(ceil_div(t_cols, 512)):
            cols = slice(j * 512, min((j + 1) * 512, t_cols))
            nc.tensor.matmul(ps_out[:, cols], ones1[:, :], row_bf[0:1, cols],
                             start=True, stop=True)
        p.barrier()


def emit_scan(nc, p, pfx, xw_ap, xa_ap, u0, u1, pT, alpha, sh_ap, T_steps=T):
    import contextlib
    N = lambda s: pfx + s
    es = contextlib.ExitStack()
    with es:
        u0b = es.enter_context(nc.sbuf_tensor(N("u0b"), [128, 8, 2048], BF16))
        u1b = es.enter_context(nc.sbuf_tensor(N("u1b"), [128, 8, 2112], BF16))
        stgf = es.enter_context(nc.sbuf_tensor(N("stgf"), [128, 2112], F32))
        pTb = es.enter_context(nc.sbuf_tensor(N("pTb"), [64, 1024], BF16))
        alpr = es.enter_context(nc.sbuf_tensor(N("alpr"), [1, 1024], F32))
        xwrow = es.enter_context(nc.sbuf_tensor(N("xwrow"), [1, 4096], BF16))
        xarow = es.enter_context(nc.sbuf_tensor(N("xarow"), [1, 64], BF16))
        one1 = es.enter_context(nc.sbuf_tensor(N("one1"), [1, 1], BF16))
        ident = es.enter_context(nc.sbuf_tensor(N("ident"), [1, 1], BF16))
        hT = es.enter_context(nc.sbuf_tensor(N("hT"), [128, 8], BF16))
        hb = es.enter_context(nc.sbuf_tensor(N("hb"), [1, 1024], BF16))
        cst = es.enter_context(nc.sbuf_tensor(N("cst"), [1, 1024], F32))
        sif = es.enter_context(nc.sbuf_tensor(N("sif"), [1, 2048], F32))
        sog = es.enter_context(nc.sbuf_tensor(N("sog"), [1, 2048], F32))
        rbb = es.enter_context(nc.sbuf_tensor(N("rbb"), [1, 64], BF16))
        rbT = es.enter_context(nc.sbuf_tensor(N("rbT"), [64, 1], BF16))
        tm1 = es.enter_context(nc.sbuf_tensor(N("tm1"), [1, 1024], F32))
        tm2 = es.enter_context(nc.sbuf_tensor(N("tm2"), [1, 1024], F32))
        tc = es.enter_context(nc.sbuf_tensor(N("tc"), [1, 1024], F32))
        Zp = es.enter_context(nc.psum_tensor(N("Z"), [64, 2112], F32))
        psM = es.enter_context(nc.psum_tensor(N("psM"), [1, 1024], F32))
        pst = es.enter_context(nc.psum_tensor(N("pst"), [128, 16, 2], BF16))
        ld = nc.alloc_semaphore(N("ld"))
        st = nc.alloc_semaphore(N("st"))
        nc.vector.memset(one1[:, :], 1.0)
        nc.vector.memset(ident[:, :], 1.0)
        nc.vector.memset(hT[:, :], 0.0)
        nc.vector.memset(cst[:, :], 0.0)
        p.dma(nc.sync, stgf[0:64, 0:1024], pT[:, :])
        p.barrier()
        nc.vector.tensor_copy(out=pTb[:, :], in_=stgf[0:64, 0:1024])
        p.barrier()
        p.dma(nc.sync, alpr[0:1, :], alpha[:, :])
        p.barrier()
        for k in range(8):
            p.dma(nc.sync, stgf[:, 0:2048], u0[k])
            p.barrier()
            nc.vector.tensor_copy(out=u0b[:, k], in_=stgf[:, 0:2048])
            p.barrier()
            p.dma(nc.sync, stgf[:, 0:2112], u1[k])
            p.barrier()
            nc.vector.tensor_copy(out=u1b[:, k], in_=stgf[:, 0:2112])
            p.barrier()
        p.barrier()

        bar = nc.all_engine_barrier
        with nc.Fori(0, T_steps) as i:
            nc.sync.dma_start(out=xwrow[0:1, :], in_=xw_ap[i]).then_inc(ld, 16)
            nc.sync.dma_start(out=xarow[0:1, :], in_=xa_ap[i]).then_inc(ld, 16)
            nc.sync.wait_ge(ld, (i + 1) * 32)
            bar()
            for k in range(8):
                mm(nc, Zp[0:1, 0:2048], hT[:, k:k + 1], u0b[:, k],
                   start=(k == 0), stop=False, tile_position=(0, 0))
                mm(nc, Zp[32:33, 0:2112], hT[:, k:k + 1], u1b[:, k],
                   start=(k == 0), stop=False, tile_position=(0, 32))
            mm(nc, Zp[0:1, 0:2048], one1[0:1, :], xwrow[0:1, 0:2048],
               start=False, stop=True, tile_position=(0, 0))
            mm(nc, Zp[32:33, 0:2048], one1[0:1, :], xwrow[0:1, 2048:4096],
               start=False, stop=True, tile_position=(0, 32))
            bar()
            nc.scalar.activation(out=sif[0:1, :], in_=Zp[0:1, 0:2048], func=AF.Sigmoid)
            nc.scalar.activation(out=sog[0:1, 0:1024], in_=Zp[32:33, 0:1024],
                                 func=AF.Sigmoid)
            nc.scalar.activation(out=sog[0:1, 1024:2048], in_=Zp[32:33, 1024:2048],
                                 func=AF.Tanh)
            nc.vector.tensor_mul(rbb[0:1, :], xarow[0:1, :], Zp[32:33, 2048:2112])
            bar()
            nc.tensor.transpose(pst[0:64, 15, 0:1], rbb[0:1, :], ident[0:1, 0:1])
            bar()
            nc.vector.tensor_copy(out=rbT[:, 0:1], in_=pst[0:64, 15, 0:1])
            bar()
            mm(nc, psM[0:1, :], rbT[:, 0:1], pTb[:, :], start=True, stop=True)
            nc.vector.tensor_mul(tm1[0:1, :], sif[0:1, 0:1024], sog[0:1, 1024:2048])
            bar()
            nc.vector.tensor_mul(tm2[0:1, :], sif[0:1, 1024:2048], cst[0:1, :])
            bar()
            nc.vector.tensor_add(tm1[0:1, :], tm1[0:1, :], tm2[0:1, :])
            bar()
            nc.vector.tensor_mul(tm1[0:1, :], tm1[0:1, :], alpr[0:1, :])
            bar()
            nc.vector.scalar_tensor_tensor(out=cst[0:1, :], in0=psM[0:1, :],
                                           scalar=0.1, in1=tm1[0:1, :],
                                           op0=ALU.mult, op1=ALU.add)
            bar()
            nc.scalar.activation(out=tc[0:1, :], in_=cst[0:1, :], func=AF.Tanh)
            bar()
            nc.vector.tensor_mul(hb[0:1, :], sog[0:1, 0:1024], tc[0:1, :])
            bar()
            for k in range(8):
                nc.tensor.transpose(pst[:, k, 0:1], hb[0:1, k * 128:(k + 1) * 128],
                                    ident[0:1, 0:1])
            bar()
            nc.vector.tensor_copy(out=hT[:, :], in_=pst[:, 0:8, 0])
            nc.gpsimd.dma_start(out=sh_ap[i], in_=hb[0:1, :]).then_inc(st, 16)
            nc.gpsimd.wait_ge(st, (i + 1) * 16)
            bar()
    p.barrier()


def emit_premid(nc, p, pfx, zw, src_kind, src, wT, b0, b1, xwout, ident_in=None):
    import contextlib
    N = lambda s: pfx + s
    es = contextlib.ExitStack()
    with es:
        xp = es.enter_context(nc.sbuf_tensor(N("xp"), [128, 8, T], BF16))
        if src_kind == "fm":
            xfull = src
            with (
                nc.sbuf_tensor(N("stgi"), [128, S], F32) as stgi,
                nc.sbuf_tensor(N("sq"), [128, S], BF16) as sq,
                nc.sbuf_tensor(N("rowr"), [1, S], F32) as rowr,
                nc.sbuf_tensor(N("rowmr"), [1, S], F32) as rowmr,
                nc.sbuf_tensor(N("rowbf"), [1, S], BF16) as rowbf,
            ):
                for half in range(2):
                    hcols = slice(half * S, (half + 1) * S)
                    for c in range(8):
                        p.dma(nc.sync, stgi[:, :], xfull[c, :, hcols])
                        p.barrier()
                        nc.vector.tensor_copy(out=xp[:, c, hcols], in_=stgi[:, :])
                        p.barrier()
                    emit_ln_fm(p, None, sq, rowr[0:1], rowmr[0:1], S,
                               x_src_chunks=[xp[:, c, hcols] for c in range(8)])
                    with (
                        nc.psum_tensor(_nm("rps"), [128, 512], F32) as rps,
                        nc.psum_tensor(_nm("mrps"), [128, 512], F32) as mrps,
                    ):
                        for tt in range(S // 512):
                            cols = slice(half * S + tt * 512, half * S + (tt + 1) * 512)
                            emit_bcast_row(p, rowr[0:1, tt * 512:(tt + 1) * 512],
                                           rps, 512, rowbf)
                            emit_bcast_row(p, rowmr[0:1, tt * 512:(tt + 1) * 512],
                                           mrps, 512, rowbf)
                            for c in range(8):
                                nc.vector.tensor_mul(xp[:, c, cols], xp[:, c, cols],
                                                     rps[:, :])
                            p.barrier()
                            for c in range(8):
                                nc.vector.tensor_sub(xp[:, c, cols], xp[:, c, cols],
                                                     mrps[:, :])
                            p.barrier()
        else:
            sh_src = src
            with (
                nc.sbuf_tensor(N("stg"), [128, 1024], BF16) as stg,
                nc.sbuf_tensor(N("stgi"), [128, 128], F32) as stgi,
                nc.sbuf_tensor(N("idb"), [128, 128], BF16) as idb,
            ):
                p.dma(nc.sync, stgi[:, 0:128], ident_in[:, :])
                p.barrier()
                nc.vector.tensor_copy(out=idb[:, :], in_=stgi[:, 0:128])
                p.barrier()
                for tt in range(T // 128):
                    p.dma(nc.sync, stg[:, :], sh_src[tt * 128:(tt + 1) * 128, :])
                    p.barrier()
                    with nc.psum_tensor(_nm("tps"), [128, 8, 128], BF16) as tps:
                        for dc in range(8):
                            nc.tensor.transpose(tps[:, dc],
                                                stg[:, dc * 128:(dc + 1) * 128],
                                                idb[:, :])
                        p.barrier()
                        for dc in range(8):
                            nc.vector.tensor_copy(
                                out=xp[:, dc, tt * 128:(tt + 1) * 128],
                                in_=tps[:, dc])
                        p.barrier()
        wTb = es.enter_context(nc.sbuf_tensor(N("wTb"), [128, 8, zw], BF16))
        brf = es.enter_context(nc.sbuf_tensor(N("brf"), [1, 4096], BF16))
        onec = es.enter_context(nc.sbuf_tensor(N("onec"), [1, 128], BF16))
        xwt = es.enter_context(nc.sbuf_tensor(N("xwt"), [128, zw], BF16))
        nc.vector.memset(onec[:, :], 1.0)
        with (
            nc.sbuf_tensor(N("stgw"), [128, zw], F32) as stgw,
            nc.sbuf_tensor(N("b1row"), [1, 4096], F32) as b1row,
        ):
            p.dma(nc.sync, stgw[0:1, 0:4096], b0[:, :])
            p.dma(nc.sync, b1row[0:1, :], b1[:, :])
            p.barrier()
            nc.vector.tensor_add(stgw[0:1, 0:4096], stgw[0:1, 0:4096],
                                 b1row[0:1, :])
            p.barrier()
            nc.vector.tensor_copy(out=brf[0:1, :], in_=stgw[0:1, 0:4096])
            p.barrier()
            for dc in range(8):
                p.dma(nc.sync, stgw[:, 0:zw], wT[dc])
                p.barrier()
                nc.vector.tensor_copy(out=wTb[:, dc], in_=stgw[:, 0:zw])
                p.barrier()
        for tt in range(T // 128):
            ts = slice(tt * 128, (tt + 1) * 128)
            with nc.psum_tensor(_nm("gps"), [128, 8, 512], F32) as ps:
                for zc in range(8):
                    zs = slice(zc * 512, (zc + 1) * 512)
                    for dc in range(8):
                        nc.tensor.matmul(ps[:, zc], xp[:, dc, ts], wTb[:, dc, zs],
                                         start=(dc == 0), stop=False,
                                         skip_group_check=True)
                    nc.tensor.matmul(ps[:, zc], onec[0:1, :], brf[0:1, zs],
                                     start=False, stop=True,
                                     skip_group_check=True)
                p.barrier()
                for zc in range(8):
                    nc.vector.tensor_copy(out=xwt[:, zc * 512:(zc + 1) * 512],
                                          in_=ps[:, zc])
                p.barrier()
            if zw > 4096:
                with nc.psum_tensor(_nm("xps"), [128, zw - 4096], F32) as ps2:
                    for dc in range(8):
                        nc.tensor.matmul(ps2[:, :], xp[:, dc, ts], wTb[:, dc, 4096:zw],
                                         start=(dc == 0), stop=(dc == 7),
                                         skip_group_check=True)
                    p.barrier()
                    nc.vector.tensor_copy(out=xwt[:, 4096:zw], in_=ps2[:, :])
                    p.barrier()
            p.dma(nc.sync, xwout[ts], xwt[:, :])
            p.barrier()


def emit_post(nc, p, pfx, xin, hm_src, ident_in, xout_ap):
    import contextlib
    N = lambda s: pfx + s
    es = contextlib.ExitStack()
    with es:
        ymf = es.enter_context(nc.sbuf_tensor(N("ymf"), [128, 8, S], BF16))
        idb = es.enter_context(nc.sbuf_tensor(N("idb"), [128, 128], BF16))
        stg = es.enter_context(nc.sbuf_tensor(N("stg"), [128, 1024], BF16))
        stf = es.enter_context(nc.sbuf_tensor(N("stf"), [128, 1024], F32))
        sq = es.enter_context(nc.sbuf_tensor(N("sq"), [128, S], BF16))
        rowr = es.enter_context(nc.sbuf_tensor(N("rowr"), [1, S], F32))
        rowmr = es.enter_context(nc.sbuf_tensor(N("rowmr"), [1, S], F32))
        rowbf = es.enter_context(nc.sbuf_tensor(N("rowbf"), [1, S], BF16))
        xst = es.enter_context(nc.sbuf_tensor(N("xst"), [128, S // 2], F32))
        p.dma(nc.sync, stf[:, 0:128], ident_in[:, :])
        p.barrier()
        nc.vector.tensor_copy(out=idb[:, :], in_=stf[:, 0:128])
        p.barrier()
        for tt in range(S // 128):
            p.dma(nc.sync, stg[:, :], hm_src[tt * 128:(tt + 1) * 128, :])
            p.barrier()
            with nc.psum_tensor(_nm("tps"), [128, 8, 128], BF16) as tps:
                for dc in range(8):
                    nc.tensor.transpose(tps[:, dc], stg[:, dc * 128:(dc + 1) * 128],
                                        idb[:, :])
                p.barrier()
                for dc in range(8):
                    nc.vector.tensor_copy(out=ymf[:, dc, tt * 128:(tt + 1) * 128],
                                          in_=tps[:, dc])
                p.barrier()
        emit_ln_fm(p, None, sq, rowr[0:1], rowmr[0:1], S,
                   x_src_chunks=[ymf[:, c] for c in range(8)])
        with (
            nc.psum_tensor(N("rps2"), [128, S // 2], F32) as rps2,
            nc.psum_tensor(N("mrps2"), [128, S // 2], F32) as mrps2,
        ):
            for half in range(2):
                cols = slice(half * (S // 2), (half + 1) * (S // 2))
                emit_bcast_row(p, rowr[0:1, cols], rps2, S // 2, rowbf)
                emit_bcast_row(p, rowmr[0:1, cols], mrps2, S // 2, rowbf)
                with nc.sbuf_tensor(_nm("swo"), [128, S // 2], F32) as tmp:
                    for c in range(8):
                        p.dma(nc.sync, xst[:, :],
                              xin[c].rearrange("p (h t) -> p h t", h=2)[:, half])
                        p.barrier()
                        nc.vector.tensor_mul(tmp[:, :], ymf[:, c, cols], rps2[:, :])
                        p.barrier()
                        nc.vector.tensor_sub(tmp[:, :], tmp[:, :], mrps2[:, :])
                        p.barrier()
                        nc.vector.tensor_add(tmp[:, :], tmp[:, :], xst[:, :])
                        p.barrier()
                        p.dma(nc.sync,
                              xout_ap[c].rearrange("p (h t) -> p h t", h=2)[:, half],
                              tmp[:, :])
                        p.barrier()


def emit_l0attn(nc, p, pfx, xin, wqp, wdn, wkup, wvup, wop, rc2, rs2, yp):
    import contextlib
    N = lambda s: pfx + s
    NT = T // 128
    es = contextlib.ExitStack()
    with es:
        qT = es.enter_context(nc.sbuf_tensor(N("qT"), [128, 4, T], BF16))
        kT = es.enter_context(nc.sbuf_tensor(N("kT"), [128, 4, T], BF16))
        vh = es.enter_context(nc.sbuf_tensor(N("vh"), [128, NT, 8, 65], BF16))
        OT = es.enter_context(nc.sbuf_tensor(N("OT"), [128, 4, T], BF16))
        nb = es.enter_context(nc.sbuf_tensor(N("nb"), [128, 1], BF16))
        onec = es.enter_context(nc.sbuf_tensor(N("onec"), [1, 128], BF16))
        onek = es.enter_context(nc.sbuf_tensor(N("onek"), [128, 1], BF16))
        nc.vector.memset(onec[:, :], 1.0)
        nc.vector.memset(onek[:, :], 1.0)
        nc.vector.memset(vh.ap().rearrange("p a b c -> p (a b c)")[:, :], 0.0)
        nc.vector.memset(
            vh.ap().rearrange("p a b c -> p (a b) c", c=65)[:, :, 64:65], 1.0)
        p.barrier()

        with nc.sbuf_tensor(N("latT"), [128, 4, T], BF16) as latT:
            with (
                nc.sbuf_tensor(N("wqb"), [128, 8, 512], BF16) as wqb,
                nc.sbuf_tensor(N("wdnb"), [128, 8, 512], BF16) as wdnb,
                nc.sbuf_tensor(N("stgf"), [128, 2048], F32) as stgf,
                nc.sbuf_tensor(N("sqb"), [128, 512], BF16) as sqb,
                nc.sbuf_tensor(N("rowr"), [1, 512], F32) as rowr,
                nc.sbuf_tensor(N("rowmr"), [1, 512], F32) as rowmr,
                nc.sbuf_tensor(N("rowbf"), [1, 512], BF16) as rowbf,
            ):
                for wsrc, wdst in ((wqp, wqb), (wdn, wdnb)):
                    for c2 in range(2):
                        p.dma(nc.sync,
                              stgf.ap().rearrange("p (a b) -> p a b", a=4),
                              wsrc[c2 * 4:(c2 + 1) * 4].rearrange("a p b -> p a b"))
                        p.barrier()
                        nc.vector.tensor_copy(
                            out=wdst[:, c2 * 4:(c2 + 1) * 4],
                            in_=stgf.ap().rearrange("p (a b) -> p a b", a=4))
                        p.barrier()
                for tq8 in range(8):
                    tcols = slice(tq8 * 512, (tq8 + 1) * 512)
                    with nc.sbuf_tensor(_nm("xpq"), [128, 8, 512], BF16) as xp:
                        for c in range(8):
                            p.dma(nc.sync, stgf[:, 0:512], xin[c, :, tcols])
                            p.barrier()
                            nc.vector.tensor_copy(out=xp[:, c], in_=stgf[:, 0:512])
                            p.barrier()
                        emit_ln_fm(p, None, sqb, rowr[0:1], rowmr[0:1], 512,
                                   x_src_chunks=[xp[:, c] for c in range(8)])
                        with (
                            nc.psum_tensor(_nm("rps"), [128, 512], F32) as rps,
                            nc.psum_tensor(_nm("mrps"), [128, 512], F32) as mrps,
                        ):
                            emit_bcast_row(p, rowr[0:1, 0:512], rps, 512, rowbf)
                            emit_bcast_row(p, rowmr[0:1, 0:512], mrps, 512, rowbf)
                            for c in range(8):
                                nc.vector.tensor_mul(xp[:, c], xp[:, c], rps[:, :])
                            p.barrier()
                            for c in range(8):
                                nc.vector.tensor_sub(xp[:, c], xp[:, c], mrps[:, :])
                            p.barrier()
                        for wdst, dst in ((wqb, qT), (wdnb, latT)):
                            with nc.psum_tensor(_nm("qps"), [128, 4, 512], F32) as ps:
                                for oc in range(4):
                                    for c in range(8):
                                        nc.tensor.matmul(
                                            ps[:, oc],
                                            wdst[:, c, oc * 128:(oc + 1) * 128],
                                            xp[:, c],
                                            start=(c == 0), stop=(c == 7),
                                            skip_group_check=True)
                                p.barrier()
                                for oc in range(4):
                                    nc.vector.tensor_copy(
                                        out=dst[:, oc, tcols], in_=ps[:, oc])
                                p.barrier()
            with (
                nc.sbuf_tensor(N("wkb"), [128, 4, 512], BF16) as wkb,
                nc.sbuf_tensor(N("wvb"), [128, 4, 512], BF16) as wvb,
                nc.sbuf_tensor(N("stg2"), [128, 2048], F32) as stg2,
            ):
                for wsrc, wdst in ((wkup, wkb), (wvup, wvb)):
                    p.dma(nc.sync,
                          stg2.ap().rearrange("p (a b) -> p a b", a=4),
                          wsrc[:, :, :].rearrange("a p b -> p a b"))
                    p.barrier()
                    nc.vector.tensor_copy(
                        out=wdst[:, :, :],
                        in_=stg2.ap().rearrange("p (a b) -> p a b", a=4))
                    p.barrier()
                with nc.psum_tensor(_nm("kps"), [128, 4, 512], F32) as ps:
                    for oc in range(4):
                        for tt in range(8):
                            cols = slice(tt * 512, (tt + 1) * 512)
                            for c in range(4):
                                nc.tensor.matmul(ps[:, tt % 4],
                                                 wkb[:, c, oc * 128:(oc + 1) * 128],
                                                 latT[:, c, cols],
                                                 start=(c == 0), stop=(c == 3),
                                                 skip_group_check=True)
                            p.barrier()
                            nc.vector.tensor_copy(out=kT[:, oc, cols], in_=ps[:, tt % 4])
                            p.barrier()
                with nc.psum_tensor(_nm("vps"), [128, 512], F32) as ps:
                    for tk in range(NT):
                        cols = slice(tk * 128, (tk + 1) * 128)
                        for c in range(4):
                            nc.tensor.matmul(ps[:, :], latT[:, c, cols], wvb[:, c, :],
                                             start=(c == 0), stop=(c == 3),
                                             skip_group_check=True)
                        p.barrier()
                        nc.vector.tensor_copy(
                            out=vh[:, tk, :, 0:64],
                            in_=ps.ap().rearrange("p (a b) -> p a b", a=8))
                        p.barrier()
        with (
            nc.sbuf_tensor(N("cosb"), [128, T], BF16) as cosb,
            nc.sbuf_tensor(N("sinb"), [128, T], BF16) as sinb,
            nc.sbuf_tensor(N("stg3"), [128, T], F32) as stg3,
            nc.sbuf_tensor(N("rtmp"), [64, 4, 2048], BF16) as rtmp,
        ):
            p.dma(nc.sync, stg3[:, :], rc2[:, :])
            p.barrier()
            nc.vector.tensor_copy(out=cosb[:, :], in_=stg3[:, :])
            p.barrier()
            p.dma(nc.sync, stg3[:, :], rs2[:, :])
            p.barrier()
            nc.vector.tensor_copy(out=sinb[:, :], in_=stg3[:, :])
            p.barrier()
            for buf in (qT, kT):
                for oc in range(4):
                    for th in range(2):
                        cols = slice(th * 2048, (th + 1) * 2048)
                        for hl in range(2):
                            E = buf[64 * hl:64 * hl + 32, oc, cols]
                            O = buf[64 * hl + 32:64 * hl + 64, oc, cols]
                            cE = cosb[64 * hl:64 * hl + 32, cols]
                            cO = cosb[64 * hl + 32:64 * hl + 64, cols]
                            sE = sinb[64 * hl:64 * hl + 32, cols]
                            sO = sinb[64 * hl + 32:64 * hl + 64, cols]
                            nc.vector.tensor_mul(rtmp[0:32, 0], E, cE)
                            nc.vector.tensor_mul(rtmp[0:32, 1], O, sO)
                            nc.vector.tensor_mul(rtmp[0:32, 2], E, sE)
                            nc.vector.tensor_mul(rtmp[0:32, 3], O, cO)
                            p.barrier()
                            nc.vector.tensor_sub(E, rtmp[0:32, 0], rtmp[0:32, 1])
                            nc.vector.tensor_add(O, rtmp[0:32, 2], rtmp[0:32, 3])
                            p.barrier()
        with (
            nc.sbuf_tensor(N("sq4"), [128, T], BF16) as sq4,
            nc.sbuf_tensor(N("srow"), [1, T], F32) as srow,
            nc.sbuf_tensor(N("sml"), [1, 8], F32) as sml,
            nc.sbuf_tensor(N("smlb"), [1, 1], BF16) as smlb,
        ):
          with nc.psum_tensor(_nm("nps"), [1, T], F32) as nps:
              for idx, buf in ((0, qT), (1, kT)):
                  for oc in range(4):
                      nc.scalar.activation(out=sq4[:, :], in_=buf[:, oc], func=AF.Square)
                      p.barrier()
                      for j in range(T // 512):
                          cols = slice(j * 512, (j + 1) * 512)
                          nc.tensor.matmul(nps[0:1, cols], onek[:, 0:1],
                                           sq4[:, cols], start=(oc == 0), stop=(oc == 3),
                                           skip_group_check=True)
                      p.barrier()
                  nc.vector.reduce_max(sml[0:1, idx:idx + 1], nps[0:1, :],
                                       axis=mybir.AxisListType.X)
                  p.barrier()
              nc.scalar.activation(out=sml[0:1, 2:4], in_=sml[0:1, 0:2], func=AF.Sqrt)
              p.barrier()
              nc.vector.tensor_mul(sml[0:1, 4:5], sml[0:1, 2:3], sml[0:1, 3:4])
              p.barrier()
              nc.vector.tensor_scalar(out=sml[0:1, 5:6], in0=sml[0:1, 4:5],
                                      scalar1=-0.125, scalar2=60.0,
                                      op0=ALU.mult, op1=ALU.add)
              p.barrier()
          nc.vector.tensor_copy(out=smlb[0:1, :], in_=sml[0:1, 5:6])
          p.barrier()
          with nc.psum_tensor(_nm("nbp"), [128, 1], F32) as nbp:
              nc.tensor.matmul(nbp[:, :], onec[0:1, :], smlb[0:1, :],
                               start=True, stop=True)
              p.barrier()
              nc.vector.tensor_copy(out=nb[:, :], in_=nbp[:, :])
              p.barrier()

        sem_pe = nc.alloc_semaphore(N("att_pe"))
        sem_act = nc.alloc_semaphore(N("att_act"))
        cnt_pe = 0
        cnt_act = 0
        with (
            nc.sbuf_tensor(N("ebuf"), [128, 2, 32, 512], BF16) as ebuf,
            nc.sbuf_tensor(N("r512"), [1, 512], F32) as r512,
            nc.sbuf_tensor(N("pvs"), [64, 512], F32) as pvs,
            nc.sbuf_tensor(N("one64f"), [1, 64], F32) as one64f,
        ):
            nc.vector.memset(one64f[:, :], 1.0)
            p.barrier()
            for g in range(8):
                ntf = 4 * g + 2
                gc = slice(g * 512, (g + 1) * 512)
                for pair in range(4):
                    with (
                        nc.psum_tensor(_nm("sA"), [128, 2, 512], F32) as sA,
                        nc.psum_tensor(_nm("sB"), [128, 2, 512], F32) as sB,
                    ):
                        for hloc in range(2):
                            for tkb in (ntf, ntf + 1):
                                nc.vector.memset(ebuf[:, hloc, tkb, 0:256], 0.0)
                        p.barrier()
                        for tk in range(ntf + 2):
                            par = tk % 2
                            tkc = slice(tk * 128, (tk + 1) * 128)
                            bnd = tk >= ntf
                            if tk >= 2:
                                nc.tensor.wait_ge(sem_act, cnt_act + tk - 1)
                            nc.tensor.matmul(sA[:, par], kT[0:64, pair, tkc],
                                             qT[0:64, pair, gc], start=True, stop=True,
                                             skip_group_check=True)
                            nc.tensor.matmul(sB[:, par], kT[64:128, pair, tkc],
                                             qT[64:128, pair, gc], start=True, stop=True,
                                             skip_group_check=True,
                                             tile_position=(64, 0)).then_inc(sem_pe, 1)
                            if True:
                                nc.scalar.wait_ge(sem_pe, cnt_pe + tk + 1)
                            ecols = slice(256, 512) if bnd else slice(0, 512)
                            nc.scalar.activation(out=ebuf[:, 0, tk, ecols],
                                                 in_=sA[:, par, ecols], func=AF.Exp,
                                                 bias=nb[:, 0:1], scale=0.125)
                            nc.scalar.activation(out=ebuf[:, 1, tk, ecols],
                                                 in_=sB[:, par, ecols], func=AF.Exp,
                                                 bias=nb[:, 0:1],
                                                 scale=0.125).then_inc(sem_act, 1)
                        cnt_pe += ntf + 2
                        cnt_act += ntf + 2
                        p.barrier()
                    for hloc in range(2):
                        h = pair * 2 + hloc
                        with (
                            nc.psum_tensor(_nm("pv"), [128, 512], F32) as pv,
                            nc.psum_tensor(_nm("rbc"), [64, 512], F32) as rbc,
                        ):
                            for tk in range(ntf + 2):
                                nc.tensor.matmul(pv[0:65, :], vh[:, tk, h],
                                                 ebuf[:, hloc, tk], start=(tk == 0),
                                                 stop=(tk == ntf + 1),
                                                 skip_group_check=True)
                            p.barrier()
                            nc.vector.reciprocal(r512[0:1, :], pv[64:65, :])
                            p.barrier()
                            nc.tensor.matmul(rbc[:, :], one64f[0:1, :], r512[0:1, :],
                                             start=True, stop=True)
                            nc.vector.tensor_copy(out=pvs[:, :], in_=pv[0:64, :])
                            p.barrier()
                            nc.vector.tensor_mul(
                                OT[64 * hloc:64 * (hloc + 1), pair, gc],
                                pvs[:, :], rbc[:, :])
                            p.barrier()
        with (
            nc.sbuf_tensor(N("wob"), [128, 4, D], BF16) as wob,
            nc.sbuf_tensor(N("stg6"), [128, 2048], F32) as stg6,
            nc.sbuf_tensor(N("yst"), [128, 512], F32) as yst,
        ):
            for c2 in range(2):
                p.dma(nc.sync,
                      stg6.ap().rearrange("p (a b) -> p a b", a=2),
                      wop[c2 * 2:(c2 + 1) * 2].rearrange("a p b -> p a b"))
                p.barrier()
                nc.vector.tensor_copy(
                    out=wob[:, c2 * 2:(c2 + 1) * 2],
                    in_=stg6.ap().rearrange("p (a b) -> p a b", a=2))
                p.barrier()
            with nc.psum_tensor(_nm("yps"), [128, 512], F32) as ps:
                for dc in range(8):
                    for tt in range(T // 512):
                        cols = slice(tt * 512, (tt + 1) * 512)
                        for oc in range(4):
                            nc.tensor.matmul(ps[:, :],
                                             wob[:, oc, dc * 128:(dc + 1) * 128],
                                             OT[:, oc, cols], start=(oc == 0),
                                             stop=(oc == 3), skip_group_check=True)
                        p.barrier()
                        nc.vector.tensor_copy(out=yst[:, :], in_=ps[:, :])
                        p.barrier()
                        p.dma(nc.sync, yp[dc, :, cols], yst[:, :])
                        p.barrier()


def emit_comb(nc, p, pfx, xin, ya_chunks, yb_chunks, xout_ap):
    import contextlib
    N = lambda s: pfx + s
    es = contextlib.ExitStack()
    with es:
        ysum = es.enter_context(nc.sbuf_tensor(N("ysum"), [128, 8, S], BF16))
        stga = es.enter_context(nc.sbuf_tensor(N("stga"), [128, S], F32))
        stgb = es.enter_context(nc.sbuf_tensor(N("stgb"), [128, S], F32))
        sq = es.enter_context(nc.sbuf_tensor(N("sq"), [128, S], BF16))
        rowr = es.enter_context(nc.sbuf_tensor(N("rowr"), [1, S], F32))
        rowmr = es.enter_context(nc.sbuf_tensor(N("rowmr"), [1, S], F32))
        rowbf = es.enter_context(nc.sbuf_tensor(N("rowbf"), [1, S], BF16))
        for c in range(8):
            p.dma(nc.sync, stga[:, :], ya_chunks(c))
            p.dma(nc.sync, stgb[:, :], yb_chunks(c))
            p.barrier()
            nc.vector.tensor_add(ysum[:, c], stga[:, :], stgb[:, :])
            p.barrier()
        emit_ln_fm(p, None, sq, rowr[0:1], rowmr[0:1], S,
                   x_src_chunks=[ysum[:, c] for c in range(8)])
        with (
            nc.psum_tensor(N("rps"), [128, S // 2], F32) as rps,
            nc.psum_tensor(N("mrps"), [128, S // 2], F32) as mrps,
        ):
            for half in range(2):
                cols = slice(half * (S // 2), (half + 1) * (S // 2))
                emit_bcast_row(p, rowr[0:1, cols], rps, S // 2, rowbf)
                emit_bcast_row(p, rowmr[0:1, cols], mrps, S // 2, rowbf)
                for c in range(8):
                    p.dma(nc.sync, stga[:, 0:S // 2],
                          xin[c].rearrange("p (h t) -> p h t", h=2)[:, half])
                    p.barrier()
                    nc.vector.tensor_mul(stgb[:, 0:S // 2], ysum[:, c, cols], rps[:, :])
                    p.barrier()
                    nc.vector.tensor_sub(stgb[:, 0:S // 2], stgb[:, 0:S // 2], mrps[:, :])
                    p.barrier()
                    nc.vector.tensor_add(stgb[:, 0:S // 2], stgb[:, 0:S // 2],
                                         stga[:, 0:S // 2])
                    p.barrier()
                    p.dma(nc.sync,
                          xout_ap[c].rearrange("p (h t) -> p h t", h=2)[:, half],
                          stgb[:, 0:S // 2])
                    p.barrier()


def emit_wcast(nc, p, pfx, w1, w2, w1o, w2o):
    N = lambda s: pfx + s
    with (
        nc.sbuf_tensor(N("st"), [128, 4096], F32) as st,
        nc.sbuf_tensor(N("bt"), [128, 4096], BF16) as bt,
    ):
        for c in range(8):
            p.dma(nc.sync, st[:, :], w1[c])
            p.barrier()
            nc.vector.tensor_copy(out=bt[:, :], in_=st[:, :])
            p.barrier()
            p.dma(nc.sync, w1o[c], bt[:, :])
            p.barrier()
        for c in range(32):
            p.dma(nc.sync, st[:, 0:1024], w2[c])
            p.barrier()
            nc.vector.tensor_copy(out=bt[:, 0:1024], in_=st[:, 0:1024])
            p.barrier()
            p.dma(nc.sync, w2o[c], bt[:, 0:1024])
            p.barrier()


def emit_l0fattn(nc, p, pfx, xin, wqs, wks, wvs, bq, bk, bv, l1g, l1b,
                 wop, yp):
    import contextlib
    N = lambda s: pfx + s
    NT = T // 128
    es = contextlib.ExitStack()
    with es:
        qT = es.enter_context(nc.sbuf_tensor(N("qT"), [128, 4, T], BF16))
        kT = es.enter_context(nc.sbuf_tensor(N("kT"), [128, 4, T], BF16))
        vh = es.enter_context(nc.sbuf_tensor(N("vh"), [128, NT, 8, 65], BF16))
        OT = es.enter_context(nc.sbuf_tensor(N("OT"), [128, 4, T], BF16))
        nb = es.enter_context(nc.sbuf_tensor(N("nb"), [128, 1], BF16))
        onec = es.enter_context(nc.sbuf_tensor(N("onec"), [1, 128], BF16))
        onek = es.enter_context(nc.sbuf_tensor(N("onek"), [128, 1], BF16))
        gbg = es.enter_context(nc.sbuf_tensor(N("gbg"), [128, 8], F32))
        gbb = es.enter_context(nc.sbuf_tensor(N("gbb"), [128, 8], F32))
        bqr = es.enter_context(nc.sbuf_tensor(N("bqr"), [128, 4], F32))
        bkr = es.enter_context(nc.sbuf_tensor(N("bkr"), [128, 4], F32))
        bvr = es.enter_context(nc.sbuf_tensor(N("bvr"), [1, 512], BF16))
        nc.vector.memset(onec[:, :], 1.0)
        nc.vector.memset(onek[:, :], 1.0)
        nc.vector.memset(vh.ap().rearrange("p a b c -> p (a b c)")[:, :], 0.0)
        nc.vector.memset(
            vh.ap().rearrange("p a b c -> p (a b) c", c=65)[:, :, 64:65], 1.0)
        p.barrier()
        p.dma(nc.sync, gbg[:, :], l1g[:, :])
        p.dma(nc.sync, gbb[:, :], l1b[:, :])
        p.dma(nc.sync, bqr[:, :], bq[:, :])
        p.dma(nc.sync, bkr[:, :], bk[:, :])
        p.barrier()
        with nc.sbuf_tensor(N("bvf"), [1, 512], F32) as bvf:
            p.dma(nc.sync, bvf[0:1, :], bv[:, :])
            p.barrier()
            nc.vector.tensor_copy(out=bvr[0:1, :], in_=bvf[0:1, :])
            p.barrier()

        for th2 in range(2):
          with (
            nc.sbuf_tensor(_nm("h1"), [128, 8, 2048], BF16) as h1,
            nc.sbuf_tensor(_nm("wstg"), [128, 8, 512], BF16) as wstg,
            nc.sbuf_tensor(_nm("stgf"), [128, 2048], F32) as stgf,
              nc.sbuf_tensor(_nm("sqb"), [128, 512], BF16) as sqb,
              nc.sbuf_tensor(_nm("rowr"), [1, 512], F32) as rowr,
              nc.sbuf_tensor(_nm("rowmr"), [1, 512], F32) as rowmr,
              nc.sbuf_tensor(_nm("rowbf"), [1, 512], BF16) as rowbf,
          ):
              for tq8 in range(th2 * 4, th2 * 4 + 4):
                  tcols = slice(tq8 * 512, (tq8 + 1) * 512)
                  with nc.sbuf_tensor(_nm("xpq"), [128, 8, 512], BF16) as xp:
                      for c in range(8):
                          p.dma(nc.sync, stgf[:, 0:512], xin[c, :, tcols])
                          p.barrier()
                          nc.vector.tensor_copy(out=xp[:, c], in_=stgf[:, 0:512])
                          p.barrier()
                      for rnd in range(2):
                          emit_ln_fm(p, None, sqb, rowr[0:1], rowmr[0:1], 512,
                                     x_src_chunks=[xp[:, c] for c in range(8)])
                          with (
                              nc.psum_tensor(_nm("rps"), [128, 512], F32) as rps,
                              nc.psum_tensor(_nm("mrps"), [128, 512], F32) as mrps,
                          ):
                              emit_bcast_row(p, rowr[0:1, 0:512], rps, 512, rowbf)
                              emit_bcast_row(p, rowmr[0:1, 0:512], mrps, 512, rowbf)
                              for c in range(8):
                                  nc.vector.tensor_mul(xp[:, c], xp[:, c], rps[:, :])
                              p.barrier()
                              for c in range(8):
                                  nc.vector.tensor_sub(xp[:, c], xp[:, c], mrps[:, :])
                              p.barrier()
                              if rnd == 1:
                                  for c in range(8):
                                      nc.vector.tensor_scalar(
                                          out=h1[:, c, tcols.start - th2 * 2048:tcols.stop - th2 * 2048], in0=xp[:, c],
                                          scalar1=gbg[:, c:c + 1], scalar2=gbb[:, c:c + 1],
                                          op0=ALU.mult, op1=ALU.add)
                                  p.barrier()
              for wsrc, dst, bcol in ((wqs, qT, 0), (wks, kT, 1)):
                  for c2 in range(2):
                      p.dma(nc.sync,
                            stgf.ap().rearrange("p (a b) -> p a b", a=4)[:, :, 0:512],
                            wsrc[c2 * 4:(c2 + 1) * 4].rearrange("a p b -> p a b"))
                      p.barrier()
                      nc.vector.tensor_copy(
                          out=wstg[:, c2 * 4:(c2 + 1) * 4],
                          in_=stgf.ap().rearrange("p (a b) -> p a b", a=4)[:, :, 0:512])
                      p.barrier()
                  with nc.psum_tensor(_nm("qps"), [128, 512], F32) as ps:
                      for oc in range(4):
                          for tt in range(th2 * 4, th2 * 4 + 4):
                              cols = slice(tt * 512, (tt + 1) * 512)
                              for c in range(8):
                                  nc.tensor.matmul(ps[:, :],
                                                   wstg[:, c, oc * 128:(oc + 1) * 128],
                                                   h1[:, c, cols.start - th2 * 2048:cols.stop - th2 * 2048],
                                                   start=(c == 0), stop=(c == 7),
                                                   skip_group_check=True)
                              p.barrier()
                              nc.vector.tensor_scalar_add(
                                  dst[:, oc, cols], ps[:, :],
                                  (bqr if bcol == 0 else bkr)[:, oc:oc + 1])
                              p.barrier()
              for c2 in range(2):
                  p.dma(nc.sync,
                        stgf.ap().rearrange("p (a b) -> p a b", a=4)[:, :, 0:512],
                        wvs[c2 * 4:(c2 + 1) * 4].rearrange("a p b -> p a b"))
                  p.barrier()
                  nc.vector.tensor_copy(
                      out=wstg[:, c2 * 4:(c2 + 1) * 4],
                      in_=stgf.ap().rearrange("p (a b) -> p a b", a=4)[:, :, 0:512])
                  p.barrier()
              with nc.psum_tensor(_nm("vps"), [128, 512], F32) as ps:
                  for tk in range(th2 * 16, th2 * 16 + 16):
                      cols = slice(tk * 128 - th2 * 2048, (tk + 1) * 128 - th2 * 2048)
                      for c in range(8):
                          nc.tensor.matmul(ps[:, :], h1[:, c, cols], wstg[:, c, :],
                                           start=(c == 0), stop=False,
                                           skip_group_check=True)
                      nc.tensor.matmul(ps[:, :], onec[0:1, :], bvr[0:1, :],
                                       start=False, stop=True, skip_group_check=True)
                      p.barrier()
                      nc.vector.tensor_copy(
                          out=vh[:, tk, :, 0:64],
                          in_=ps.ap().rearrange("p (a b) -> p a b", a=8))
                      p.barrier()
        with (
            nc.sbuf_tensor(N("sq4"), [128, T], BF16) as sq4,
            nc.sbuf_tensor(N("sml"), [1, 8], F32) as sml,
            nc.sbuf_tensor(N("smlb"), [1, 1], BF16) as smlb,
        ):
            with nc.psum_tensor(_nm("nps"), [1, T], F32) as nps:
                for idx, buf in ((0, qT), (1, kT)):
                    for oc in range(4):
                        nc.scalar.activation(out=sq4[:, :], in_=buf[:, oc], func=AF.Square)
                        p.barrier()
                        for j in range(T // 512):
                            cols = slice(j * 512, (j + 1) * 512)
                            nc.tensor.matmul(nps[0:1, cols], onek[:, 0:1],
                                             sq4[:, cols], start=(oc == 0), stop=(oc == 3),
                                             skip_group_check=True)
                        p.barrier()
                    nc.vector.reduce_max(sml[0:1, idx:idx + 1], nps[0:1, :],
                                         axis=mybir.AxisListType.X)
                    p.barrier()
                nc.scalar.activation(out=sml[0:1, 2:4], in_=sml[0:1, 0:2], func=AF.Sqrt)
                p.barrier()
                nc.vector.tensor_mul(sml[0:1, 4:5], sml[0:1, 2:3], sml[0:1, 3:4])
                p.barrier()
                nc.vector.tensor_scalar(out=sml[0:1, 5:6], in0=sml[0:1, 4:5],
                                        scalar1=-0.125, scalar2=60.0,
                                        op0=ALU.mult, op1=ALU.add)
                p.barrier()
            nc.vector.tensor_copy(out=smlb[0:1, :], in_=sml[0:1, 5:6])
            p.barrier()
            with nc.psum_tensor(_nm("nbp"), [128, 1], F32) as nbp:
                nc.tensor.matmul(nbp[:, :], onec[0:1, :], smlb[0:1, :],
                                 start=True, stop=True)
                p.barrier()
                nc.vector.tensor_copy(out=nb[:, :], in_=nbp[:, :])
                p.barrier()
        sem_pe = nc.alloc_semaphore(N("fat_pe"))
        sem_act = nc.alloc_semaphore(N("fat_act"))
        cnt_pe = 0
        cnt_act = 0
        with (
            nc.sbuf_tensor(N("ebuf"), [128, 2, NT, 512], BF16) as ebuf,
            nc.sbuf_tensor(N("r512"), [1, 512], F32) as r512,
            nc.sbuf_tensor(N("pvs"), [64, 512], F32) as pvs,
            nc.sbuf_tensor(N("one64f"), [1, 64], F32) as one64f,
        ):
            nc.vector.memset(one64f[:, :], 1.0)
            p.barrier()
            for g in range(8):
                gc = slice(g * 512, (g + 1) * 512)
                for pair in range(4):
                    with (
                        nc.psum_tensor(_nm("sA"), [128, 2, 512], F32) as sA,
                        nc.psum_tensor(_nm("sB"), [128, 2, 512], F32) as sB,
                    ):
                        for tk in range(NT):
                            par = tk % 2
                            tkc = slice(tk * 128, (tk + 1) * 128)
                            if tk >= 2:
                                nc.tensor.wait_ge(sem_act, cnt_act + tk - 1)
                            nc.tensor.matmul(sA[:, par], kT[0:64, pair, tkc],
                                             qT[0:64, pair, gc], start=True, stop=True,
                                             skip_group_check=True)
                            nc.tensor.matmul(sB[:, par], kT[64:128, pair, tkc],
                                             qT[64:128, pair, gc], start=True, stop=True,
                                             skip_group_check=True,
                                             tile_position=(64, 0)).then_inc(sem_pe, 1)
                            nc.scalar.wait_ge(sem_pe, cnt_pe + tk + 1)
                            nc.scalar.activation(out=ebuf[:, 0, tk], in_=sA[:, par],
                                                 func=AF.Exp, bias=nb[:, 0:1], scale=0.125)
                            nc.scalar.activation(out=ebuf[:, 1, tk], in_=sB[:, par],
                                                 func=AF.Exp, bias=nb[:, 0:1],
                                                 scale=0.125).then_inc(sem_act, 1)
                        cnt_pe += NT
                        cnt_act += NT
                        p.barrier()
                    for hloc in range(2):
                        h = pair * 2 + hloc
                        with (
                            nc.psum_tensor(_nm("pv"), [128, 512], F32) as pv,
                            nc.psum_tensor(_nm("rbc"), [64, 512], F32) as rbc,
                        ):
                            for tk in range(NT):
                                nc.tensor.matmul(pv[0:65, :], vh[:, tk, h],
                                                 ebuf[:, hloc, tk], start=(tk == 0),
                                                 stop=(tk == NT - 1),
                                                 skip_group_check=True)
                            p.barrier()
                            nc.vector.reciprocal(r512[0:1, :], pv[64:65, :])
                            p.barrier()
                            nc.tensor.matmul(rbc[:, :], one64f[0:1, :], r512[0:1, :],
                                             start=True, stop=True)
                            nc.vector.tensor_copy(out=pvs[:, :], in_=pv[0:64, :])
                            p.barrier()
                            nc.vector.tensor_mul(
                                OT[64 * hloc:64 * (hloc + 1), pair, gc],
                                pvs[:, :], rbc[:, :])
                            p.barrier()
        with (
            nc.sbuf_tensor(N("wob"), [128, 4, D], BF16) as wob,
            nc.sbuf_tensor(N("stg6"), [128, 2048], F32) as stg6,
            nc.sbuf_tensor(N("yst"), [128, 512], F32) as yst,
        ):
            for c2 in range(2):
                p.dma(nc.sync,
                      stg6.ap().rearrange("p (a b) -> p a b", a=2),
                      wop[c2 * 2:(c2 + 1) * 2].rearrange("a p b -> p a b"))
                p.barrier()
                nc.vector.tensor_copy(
                    out=wob[:, c2 * 2:(c2 + 1) * 2],
                    in_=stg6.ap().rearrange("p (a b) -> p a b", a=2))
                p.barrier()
            with nc.psum_tensor(_nm("yps"), [128, 512], F32) as ps:
                for dc in range(8):
                    for tt in range(T // 512):
                        cols = slice(tt * 512, (tt + 1) * 512)
                        for oc in range(4):
                            nc.tensor.matmul(ps[:, :],
                                             wob[:, oc, dc * 128:(dc + 1) * 128],
                                             OT[:, oc, cols], start=(oc == 0),
                                             stop=(oc == 3), skip_group_check=True)
                        p.barrier()
                        nc.vector.tensor_copy(out=yst[:, :], in_=ps[:, :])
                        p.barrier()
                        p.dma(nc.sync, yp[dc, :, cols], yst[:, :])
                        p.barrier()


def emit_l0fmoe(nc, p, pfx, xin, ya_chunks, yb_chunks, bo, l2g, l2b, gw, gbr,
                w1b, w2b, b1c, b2r, ident_in, xout_ap):
    import contextlib
    N = lambda s: pfx + s
    NE, HID, NH = 8, 4096, 32
    es = contextlib.ExitStack()
    with es:
        xa1 = es.enter_context(nc.sbuf_tensor(N("xa1"), [128, 8, S], BF16))
        h2 = es.enter_context(nc.sbuf_tensor(N("h2"), [128, 8, S], BF16))
        ybuf = es.enter_context(nc.sbuf_tensor(N("ybuf"), [128, 8, S], BF16))
        ew = es.enter_context(nc.sbuf_tensor(N("ew"), [128, 16, 8], F32))
        bor = es.enter_context(nc.sbuf_tensor(N("bor"), [128, 8], F32))
        g2r = es.enter_context(nc.sbuf_tensor(N("g2r"), [128, 8], F32))
        b2rr = es.enter_context(nc.sbuf_tensor(N("b2rr"), [128, 8], F32))
        onec = es.enter_context(nc.sbuf_tensor(N("onec"), [1, 128], BF16))
        idb = es.enter_context(nc.sbuf_tensor(N("idb"), [128, 128], BF16))
        stgf = es.enter_context(nc.sbuf_tensor(N("stgf"), [128, 1024], F32))
        stgg = es.enter_context(nc.sbuf_tensor(N("stgg"), [128, 512], F32))
        sqb = es.enter_context(nc.sbuf_tensor(N("sqb"), [128, 512], BF16))
        rowr = es.enter_context(nc.sbuf_tensor(N("rowr"), [1, 512], F32))
        rowmr = es.enter_context(nc.sbuf_tensor(N("rowmr"), [1, 512], F32))
        rowbf = es.enter_context(nc.sbuf_tensor(N("rowbf"), [1, 512], BF16))
        nc.vector.memset(onec[:, :], 1.0)
        p.dma(nc.sync, bor[:, :], bo[:, :])
        p.dma(nc.sync, g2r[:, :], l2g[:, :])
        p.dma(nc.sync, b2rr[:, :], l2b[:, :])
        p.dma(nc.sync, stgf[:, 0:128], ident_in[:, :])
        p.barrier()
        nc.vector.tensor_copy(out=idb[:, :], in_=stgf[:, 0:128])
        p.barrier()

        for tq in range(S // 512):
            tcols = slice(tq * 512, (tq + 1) * 512)
            with nc.sbuf_tensor(_nm("xq"), [128, 8, 512], BF16) as xq:
                for c in range(8):
                    p.dma(nc.sync, stgg[:, :], xin[c, :, tcols])
                    p.barrier()
                    nc.vector.tensor_copy(out=xq[:, c], in_=stgg[:, :])
                    p.barrier()
                emit_ln_fm(p, None, sqb, rowr[0:1], rowmr[0:1], 512,
                           x_src_chunks=[xq[:, c] for c in range(8)])
                with (
                    nc.psum_tensor(_nm("rps"), [128, 512], F32) as rps,
                    nc.psum_tensor(_nm("mrps"), [128, 512], F32) as mrps,
                ):
                    emit_bcast_row(p, rowr[0:1, 0:512], rps, 512, rowbf)
                    emit_bcast_row(p, rowmr[0:1, 0:512], mrps, 512, rowbf)
                    for c in range(8):
                        nc.vector.tensor_mul(xq[:, c], xq[:, c], rps[:, :])
                    p.barrier()
                    for c in range(8):
                        nc.vector.tensor_sub(xq[:, c], xq[:, c], mrps[:, :])
                    p.barrier()
                for c in range(8):
                    p.dma(nc.sync, stgg[:, :], ya_chunks(c)[:, tcols])
                    p.dma(nc.sync, stgf[:, 0:512], yb_chunks(c)[:, tcols])
                    p.barrier()
                    nc.vector.tensor_add(stgg[:, :], stgg[:, :], stgf[:, 0:512])
                    p.barrier()
                    nc.vector.tensor_scalar_add(stgg[:, :], stgg[:, :], bor[:, c:c + 1])
                    p.barrier()
                    nc.vector.tensor_add(xa1[:, c, tcols], xq[:, c], stgg[:, :])
                    p.barrier()
            with nc.sbuf_tensor(_nm("hq"), [128, 8, 512], BF16) as hq:
                for c in range(8):
                    nc.vector.tensor_copy(out=hq[:, c], in_=xa1[:, c, tcols])
                p.barrier()
                emit_ln_fm(p, None, sqb, rowr[0:1], rowmr[0:1], 512,
                           x_src_chunks=[hq[:, c] for c in range(8)])
                with (
                    nc.psum_tensor(_nm("rps"), [128, 512], F32) as rps,
                    nc.psum_tensor(_nm("mrps"), [128, 512], F32) as mrps,
                ):
                    emit_bcast_row(p, rowr[0:1, 0:512], rps, 512, rowbf)
                    emit_bcast_row(p, rowmr[0:1, 0:512], mrps, 512, rowbf)
                    for c in range(8):
                        nc.vector.tensor_mul(hq[:, c], hq[:, c], rps[:, :])
                    p.barrier()
                    for c in range(8):
                        nc.vector.tensor_sub(hq[:, c], hq[:, c], mrps[:, :])
                    p.barrier()
                    for c in range(8):
                        nc.vector.tensor_scalar(out=h2[:, c, tcols], in0=hq[:, c],
                                                scalar1=g2r[:, c:c + 1],
                                                scalar2=b2rr[:, c:c + 1],
                                                op0=ALU.mult, op1=ALU.add)
                    p.barrier()

        with (
            nc.sbuf_tensor(N("gwb"), [128, 8, 8], BF16) as gwb,
            nc.sbuf_tensor(N("gbb"), [1, 8], BF16) as gbbv,
            nc.sbuf_tensor(N("lgt"), [128, 8], F32) as lgt,
            nc.sbuf_tensor(N("m1"), [128, 1], F32) as m1,
            nc.sbuf_tensor(N("m2"), [128, 1], F32) as m2,
            nc.sbuf_tensor(N("eq1"), [128, 8], F32) as eq1,
            nc.sbuf_tensor(N("eq2"), [128, 8], F32) as eq2,
            nc.sbuf_tensor(N("gw1"), [128, 1], F32) as gw1,
            nc.sbuf_tensor(N("gw2"), [128, 1], F32) as gw2,
        ):
            for c in range(8):
                p.dma(nc.sync, stgf[:, 0:8], gw[c])
                p.barrier()
                nc.vector.tensor_copy(out=gwb[:, c], in_=stgf[:, 0:8])
                p.barrier()
            p.dma(nc.sync, stgf[0:1, 0:8], gbr[:, :])
            p.barrier()
            nc.vector.tensor_copy(out=gbbv[0:1, :], in_=stgf[0:1, 0:8])
            p.barrier()
            with nc.psum_tensor(_nm("lps"), [128, 8], F32) as lps:
                for tt in range(16):
                    ts_ = slice(tt * 128, (tt + 1) * 128)
                    for c in range(8):
                        nc.tensor.matmul(lps[:, :], h2[:, c, ts_], gwb[:, c],
                                         start=(c == 0), stop=False,
                                         skip_group_check=True)
                    nc.tensor.matmul(lps[:, :], onec[0:1, :], gbbv[0:1, :],
                                     start=False, stop=True, skip_group_check=True)
                    p.barrier()
                    nc.vector.tensor_copy(out=lgt[:, :], in_=lps[:, :])
                    p.barrier()
                    nc.vector.reduce_max(m1[:, 0:1], lgt[:, :], axis=mybir.AxisListType.X)
                    p.barrier()
                    nc.vector.tensor_scalar(out=eq1[:, :], in0=lgt[:, :],
                                            scalar1=m1[:, 0:1], scalar2=None,
                                            op0=ALU.is_equal)
                    p.barrier()
                    nc.vector.scalar_tensor_tensor(out=eq2[:, :], in0=eq1[:, :],
                                                   scalar=-1e30, in1=lgt[:, :],
                                                   op0=ALU.mult, op1=ALU.add)
                    p.barrier()
                    nc.vector.reduce_max(m2[:, 0:1], eq2[:, :], axis=mybir.AxisListType.X)
                    p.barrier()
                    nc.vector.tensor_sub(gw1[:, 0:1], m1[:, 0:1], m2[:, 0:1])
                    p.barrier()
                    nc.scalar.activation(out=gw1[:, 0:1], in_=gw1[:, 0:1],
                                         func=AF.Sigmoid)
                    p.barrier()
                    nc.vector.tensor_scalar(out=gw2[:, 0:1], in0=gw1[:, 0:1],
                                            scalar1=-1.0, scalar2=1.0,
                                            op0=ALU.mult, op1=ALU.add)
                    p.barrier()
                    nc.vector.tensor_scalar(out=eq2[:, :], in0=eq2[:, :],
                                            scalar1=m2[:, 0:1], scalar2=None,
                                            op0=ALU.is_equal)
                    p.barrier()
                    nc.vector.tensor_scalar_mul(eq1[:, :], eq1[:, :], gw1[:, 0:1])
                    nc.vector.tensor_scalar_mul(eq2[:, :], eq2[:, :], gw2[:, 0:1])
                    p.barrier()
                    nc.vector.tensor_add(ew[:, tt, :], eq1[:, :], eq2[:, :])
                    p.barrier()

        with (
            nc.sbuf_tensor(N("macc"), [128, 4, D], F32) as macc,
            nc.sbuf_tensor(N("hidb"), [128, NH, 512], BF16) as hidb,
            nc.sbuf_tensor(N("w1c"), [128, 8, 512], BF16) as w1c,
            nc.sbuf_tensor(N("w2c"), [128, 4, D], BF16) as w2c,
            nc.sbuf_tensor(N("b1s"), [128, NH], F32) as b1s,
            nc.sbuf_tensor(N("b2s"), [1, D], BF16) as b2s,
            nc.sbuf_tensor(N("mtb"), [128, D], BF16) as mtb,
        ):
            for sup in range(S // 512):
                scols = slice(sup * 512, (sup + 1) * 512)
                nc.vector.memset(macc.ap().rearrange("p a b -> p (a b)")[:, :], 0.0)
                p.barrier()
                for e in range(NE):
                    p.dma(nc.sync, b1s[:, :], b1c[e])
                    p.dma(nc.sync, stgf[0:1, 0:1024], b2r[e])
                    p.barrier()
                    nc.vector.tensor_copy(out=b2s[0:1, :], in_=stgf[0:1, 0:1024])
                    p.barrier()
                    with nc.psum_tensor(_nm("hps"), [128, 512], F32) as hps:
                        for hg in range(NH // 4):
                            p.dma(nc.sync, w1c[:, :, :],
                                  w1b[e, :, :, hg * 512:(hg + 1) * 512]
                                  .rearrange("a p b -> p a b"))
                            p.barrier()
                            for hl in range(4):
                                hc = hg * 4 + hl
                                for c in range(8):
                                    nc.tensor.matmul(hps[:, :],
                                                     w1c[:, c, hl * 128:(hl + 1) * 128],
                                                     h2[:, c, scols],
                                                     start=(c == 0), stop=(c == 7),
                                                     skip_group_check=True)
                                p.barrier()
                                nc.scalar.activation(out=hidb[:, hc], in_=hps[:, :],
                                                     func=AF.Relu, bias=b1s[:, hc:hc + 1])
                                p.barrier()
                    with nc.psum_tensor(_nm("ops"), [128, 4, 2, 512], F32) as ops:
                        for hg in range(NH // 4):
                            p.dma(nc.sync, w2c[:, :, :], w2b[e, hg * 4:(hg + 1) * 4]
                                  .rearrange("a p b -> p a b"))
                            p.barrier()
                            for hl in range(4):
                                hc = hg * 4 + hl
                                for tl in range(4):
                                    lcols = slice(tl * 128, (tl + 1) * 128)
                                    for oc in range(2):
                                        nc.tensor.matmul(ops[:, tl, oc],
                                                         hidb[:, hc, lcols],
                                                         w2c[:, hl, oc * 512:(oc + 1) * 512],
                                                         start=(hc == 0), stop=False,
                                                         skip_group_check=True)
                            p.barrier()
                        for tl in range(4):
                            for oc in range(2):
                                nc.tensor.matmul(ops[:, tl, oc], onec[0:1, 0:128],
                                                 b2s[0:1, oc * 512:(oc + 1) * 512],
                                                 start=False, stop=True,
                                                 skip_group_check=True)
                        p.barrier()
                        for tl in range(4):
                            tg = sup * 4 + tl
                            for oc in range(2):
                                nc.vector.scalar_tensor_tensor(
                                    out=macc[:, tl, oc * 512:(oc + 1) * 512],
                                    in0=ops[:, tl, oc], scalar=ew[:, tg, e:e + 1],
                                    in1=macc[:, tl, oc * 512:(oc + 1) * 512],
                                    op0=ALU.mult, op1=ALU.add)
                        p.barrier()
                for tl in range(4):
                    nc.vector.tensor_copy(out=mtb[:, :], in_=macc[:, tl])
                    p.barrier()
                    with nc.psum_tensor(_nm("tps"), [128, 8, 128], BF16) as tps:
                        for dc in range(8):
                            nc.tensor.transpose(tps[:, dc],
                                                mtb[:, dc * 128:(dc + 1) * 128],
                                                idb[:, :])
                        p.barrier()
                        gcols = slice(sup * 512 + tl * 128, sup * 512 + (tl + 1) * 128)
                        for dc in range(8):
                            nc.vector.tensor_add(ybuf[:, dc, gcols], tps[:, dc],
                                                 xa1[:, dc, gcols])
                        p.barrier()

        with (
            nc.sbuf_tensor(N("fsq"), [128, S], BF16) as fsq,
            nc.sbuf_tensor(N("fr"), [1, S], F32) as fr,
            nc.sbuf_tensor(N("fmr"), [1, S], F32) as fmr,
            nc.sbuf_tensor(N("frbf"), [1, S], BF16) as frbf,
        ):
          emit_ln_fm(p, None, fsq, fr[0:1], fmr[0:1], S,
                     x_src_chunks=[ybuf[:, c] for c in range(8)])
          with (
            nc.psum_tensor(N("rps2"), [128, 512], F32) as rps2,
            nc.psum_tensor(N("mrps2"), [128, 512], F32) as mrps2,
          ):
            for tq in range(S // 512):
                cols = slice(tq * 512, (tq + 1) * 512)
                emit_bcast_row(p, fr[0:1, cols], rps2, 512, frbf)
                emit_bcast_row(p, fmr[0:1, cols], mrps2, 512, frbf)
                for c in range(8):
                    p.dma(nc.sync, stgg[:, :], xin[c, :, cols])
                    p.barrier()
                    nc.vector.tensor_mul(stgf[:, 0:512], ybuf[:, c, cols], rps2[:, :])
                    p.barrier()
                    nc.vector.tensor_sub(stgf[:, 0:512], stgf[:, 0:512], mrps2[:, :])
                    p.barrier()
                    nc.vector.tensor_add(stgf[:, 0:512], stgf[:, 0:512], stgg[:, :])
                    p.barrier()
                    p.dma(nc.sync, xout_ap[c, :, cols], stgf[:, 0:512])
                    p.barrier()


def emit_l1ffn(nc, p, pfx, xin, wgT, wuT, wdT, xout):
    import contextlib
    N = lambda s: pfx + s
    HID = 3072
    NH = HID // 128
    SC = 1.0 / 127.0
    SH = S // 2
    es = contextlib.ExitStack()
    with es:
        xp = es.enter_context(nc.sbuf_tensor(N("xp"), [128, 8, S], BF16))
        ybuf = es.enter_context(nc.sbuf_tensor(N("ybuf"), [128, 8, S], BF16))
        wd = es.enter_context(nc.sbuf_tensor(N("wd"), [128, NH, D], BF16))
        stg = es.enter_context(nc.sbuf_tensor(N("stg"), [128, 8, 128], I32))
        sq = es.enter_context(nc.sbuf_tensor(N("sq"), [128, S], BF16))
        rowr = es.enter_context(nc.sbuf_tensor(N("rowr"), [1, S], F32))
        rowmr = es.enter_context(nc.sbuf_tensor(N("rowmr"), [1, S], F32))
        rowbf = es.enter_context(nc.sbuf_tensor(N("rowbf"), [1, S], BF16))
        xst = es.enter_context(nc.sbuf_tensor(N("xst"), [128, S], F32))
        whc = es.enter_context(nc.sbuf_tensor(N("whc"), [128, 2, 8, 128], BF16))
        gbuf = es.enter_context(nc.sbuf_tensor(N("gbuf"), [128, SH], BF16))
        with nc.sbuf_tensor(N("xb"), [128, 8, S], BF16) as xb:
            for c in range(8):
                p.dma(nc.sync, xst[:, :], xin[c])
                p.barrier()
                nc.vector.tensor_copy(out=xb[:, c], in_=xst[:, :])
                p.barrier()
            emit_ln_fm(p, None, sq, rowr[0:1], rowmr[0:1], S,
                       x_src_chunks=[xb[:, c] for c in range(8)])
            with (
                nc.psum_tensor(N("rps"), [128, 512], F32) as rps,
                nc.psum_tensor(N("mrps"), [128, 512], F32) as mrps,
            ):
                for tt in range(S // 512):
                    cols = slice(tt * 512, (tt + 1) * 512)
                    emit_bcast_row(p, rowr[0:1, cols], rps, 512, rowbf)
                    emit_bcast_row(p, rowmr[0:1, cols], mrps, 512, rowbf)
                    for c in range(8):
                        nc.vector.tensor_mul(xp[:, c, cols], xb[:, c, cols], rps[:, :])
                    p.barrier()
                    for c in range(8):
                        nc.vector.tensor_sub(xp[:, c, cols], xp[:, c, cols], mrps[:, :])
                    p.barrier()

        with nc.sbuf_tensor(N("hbuf"), [128, NH, SH], BF16) as hbuf:
            for hc in range(NH):
                p.dma(nc.sync, stg.ap().rearrange("p a b -> p (a b)")[:, 0:D], wdT[hc])
                p.barrier()
                nc.vector.tensor_copy(out=wd[:, hc],
                                      in_=stg.ap().rearrange("p a b -> p (a b)")[:, 0:D])
                p.barrier()
            for half in range(2):
                cols = slice(half * SH, (half + 1) * SH)
                with (
                    nc.psum_tensor(_nm("gps"), [128, SH], F32) as gps,
                    nc.psum_tensor(_nm("ups"), [128, SH], F32) as ups,
                ):
                    for hc in range(NH):
                        hs = slice(hc * 128, (hc + 1) * 128)
                        p.dma(nc.sync, stg[:, :, :], wgT[:, :, hs].rearrange("a p b -> p a b"))
                        p.barrier()
                        nc.vector.tensor_copy(out=whc[:, 0], in_=stg[:, :, :])
                        p.barrier()
                        p.dma(nc.sync, stg[:, :, :], wuT[:, :, hs].rearrange("a p b -> p a b"))
                        p.barrier()
                        nc.vector.tensor_copy(out=whc[:, 1], in_=stg[:, :, :])
                        p.barrier()
                        for dc in range(8):
                            mm(nc, gps[:, :], whc[:, 0, dc], xp[:, dc, cols],
                               start=(dc == 0), stop=(dc == 7))
                            mm(nc, ups[:, :], whc[:, 1, dc], xp[:, dc, cols],
                               start=(dc == 0), stop=(dc == 7))
                        p.barrier()
                        nc.scalar.activation(out=gbuf[:, :], in_=gps[:, :],
                                             func=AF.Silu, scale=SC)
                        nc.scalar.activation(out=hbuf[:, hc], in_=ups[:, :],
                                             func=AF.Copy, scale=SC * SC)
                        p.barrier()
                        nc.vector.tensor_mul(hbuf[:, hc], gbuf[:, :], hbuf[:, hc])
                    p.barrier()
                for dq in range(2):
                    with nc.psum_tensor(_nm("dps"), [128, 4, SH], F32) as dps:
                        for i in range(4):
                            dc = dq * 4 + i
                            for hc in range(NH):
                                mm(nc, dps[:, i], wd[:, hc, dc * 128:(dc + 1) * 128],
                                   hbuf[:, hc], start=(hc == 0), stop=(hc == NH - 1))
                        p.barrier()
                        for i in range(4):
                            nc.vector.tensor_copy(out=ybuf[:, dq * 4 + i, cols],
                                                  in_=dps[:, i])
                        p.barrier()

        emit_ln_fm(p, None, sq, rowr[0:1], rowmr[0:1], S,
                   x_src_chunks=[ybuf[:, c] for c in range(8)])
        with (
            nc.psum_tensor(N("rps2"), [128, SH], F32) as rps2,
            nc.psum_tensor(N("mrps2"), [128, SH], F32) as mrps2,
        ):
            for half in range(2):
                cols = slice(half * SH, (half + 1) * SH)
                emit_bcast_row(p, rowr[0:1, cols], rps2, SH, rowbf)
                emit_bcast_row(p, rowmr[0:1, cols], mrps2, SH, rowbf)
                with nc.sbuf_tensor(_nm("swo"), [128, SH], F32) as tmp:
                    for c in range(8):
                        p.dma(nc.sync, xst[:, 0:SH],
                              xin[c].rearrange("p (h t) -> p h t", h=2)[:, half])
                        p.barrier()
                        nc.vector.tensor_mul(tmp[:, :], ybuf[:, c, cols], rps2[:, :])
                        p.barrier()
                        nc.vector.tensor_sub(tmp[:, :], tmp[:, :], mrps2[:, :])
                        p.barrier()
                        nc.vector.tensor_add(tmp[:, :], tmp[:, :], xst[:, 0:SH])
                        p.barrier()
                        p.dma(nc.sync,
                              xout[c].rearrange("p (h t) -> p h t", h=2)[:, half],
                              tmp[:, :])
                        p.barrier()
        p.barrier()


def build_full(upto=None):
    nc = bass.Bass("TRN2", target_bir_lowering=False)
    EI, EO = "ExternalInput", "ExternalOutput"
    a_xin = nc.dram_tensor("a_xin", [8, 128, T], F32, kind=EI).ap()
    a_wqp = nc.dram_tensor("a_wqp", [8, 128, 512], F32, kind=EI).ap()
    a_wdn = nc.dram_tensor("a_wdn", [8, 128, 512], F32, kind=EI).ap()
    a_wkup = nc.dram_tensor("a_wkup", [4, 128, 512], F32, kind=EI).ap()
    a_wvup = nc.dram_tensor("a_wvup", [4, 128, 512], F32, kind=EI).ap()
    a_wop = nc.dram_tensor("a_wop", [4, 128, D], F32, kind=EI).ap()
    rc2 = nc.dram_tensor("rc2", [128, T], F32, kind=EI).ap()
    rs2 = nc.dram_tensor("rs2", [128, T], F32, kind=EI).ap()
    x_tok = nc.dram_tensor("x_tok", [8, 128, S], F32, kind=EI).ap()
    cw1 = nc.dram_tensor("cw1", [8, 128, 4096], F32, kind=EI).ap()
    cw2 = nc.dram_tensor("cw2", [32, 128, 1024], F32, kind=EI).ap()
    f_wqs = nc.dram_tensor("f_wqs", [8, 128, 512], F32, kind=EI).ap()
    f_wks = nc.dram_tensor("f_wks", [8, 128, 512], F32, kind=EI).ap()
    f_wvs = nc.dram_tensor("f_wvs", [8, 128, 512], F32, kind=EI).ap()
    f_bq = nc.dram_tensor("f_bq", [128, 4], F32, kind=EI).ap()
    f_bk = nc.dram_tensor("f_bk", [128, 4], F32, kind=EI).ap()
    f_bv = nc.dram_tensor("f_bv", [1, 512], F32, kind=EI).ap()
    f_l1g = nc.dram_tensor("f_l1g", [128, 8], F32, kind=EI).ap()
    f_l1b = nc.dram_tensor("f_l1b", [128, 8], F32, kind=EI).ap()
    f_wop = nc.dram_tensor("f_wop", [4, 128, D], F32, kind=EI).ap()
    m_bo = nc.dram_tensor("m_bo", [128, 8], F32, kind=EI).ap()
    m_l2g = nc.dram_tensor("m_l2g", [128, 8], F32, kind=EI).ap()
    m_l2b = nc.dram_tensor("m_l2b", [128, 8], F32, kind=EI).ap()
    m_gw = nc.dram_tensor("m_gw", [8, 128, 8], F32, kind=EI).ap()
    m_gbr = nc.dram_tensor("m_gbr", [1, 8], F32, kind=EI).ap()
    m_b1c = nc.dram_tensor("m_b1c", [8, 128, 32], F32, kind=EI).ap()
    m_b2r = nc.dram_tensor("m_b2r", [8, 1, D], F32, kind=EI).ap()
    ident = nc.dram_tensor("ident", [128, 128], F32, kind=EI).ap()
    p1_wT = nc.dram_tensor("p1_wT", [8, 128, 4096], F32, kind=EI).ap()
    p1_b0 = nc.dram_tensor("p1_b0", [1, 4096], F32, kind=EI).ap()
    p1_b1 = nc.dram_tensor("p1_b1", [1, 4096], F32, kind=EI).ap()
    p2_wT = nc.dram_tensor("p2_wT", [8, 128, 4160], F32, kind=EI).ap()
    p2_b0 = nc.dram_tensor("p2_b0", [1, 4096], F32, kind=EI).ap()
    p2_b1 = nc.dram_tensor("p2_b1", [1, 4096], F32, kind=EI).ap()
    s1_u0 = nc.dram_tensor("s1_u0", [8, 128, 2048], F32, kind=EI).ap()
    s1_u1 = nc.dram_tensor("s1_u1", [8, 128, 2112], F32, kind=EI).ap()
    s1_pt = nc.dram_tensor("s1_pt", [64, 1024], F32, kind=EI).ap()
    s1_al = nc.dram_tensor("s1_al", [1, 1024], F32, kind=EI).ap()
    s2_u0 = nc.dram_tensor("s2_u0", [8, 128, 2048], F32, kind=EI).ap()
    s2_u1 = nc.dram_tensor("s2_u1", [8, 128, 2112], F32, kind=EI).ap()
    s2_pt = nc.dram_tensor("s2_pt", [64, 1024], F32, kind=EI).ap()
    s2_al = nc.dram_tensor("s2_al", [1, 1024], F32, kind=EI).ap()
    xaz = nc.dram_tensor("xaz", [T, 64], BF16, kind=EI).ap()
    l1_wgT = nc.dram_tensor("l1_wgT", [8, 128, 3072], I32, kind=EI).ap()
    l1_wuT = nc.dram_tensor("l1_wuT", [8, 128, 3072], I32, kind=EI).ap()
    l1_wdT = nc.dram_tensor("l1_wdT", [24, 128, 1024], I32, kind=EI).ap()
    yp1 = nc.dram_tensor("yp1", [8, 128, T], F32).ap()
    yp1g = nc.dram_tensor("yp1g", [8 * 1024, T], F32).ap()
    x1 = nc.dram_tensor("x1", [8, 128, S], F32).ap()
    x1g = nc.dram_tensor("x1g", [8 * 1024, S], F32).ap()
    w1o = nc.dram_tensor("w1o", [8, 128, 4096], BF16).ap()
    w1g = nc.dram_tensor("w1g", [8 * 1024, 4096], BF16).ap()
    w2o = nc.dram_tensor("w2o", [32, 128, 1024], BF16).ap()
    w2g = nc.dram_tensor("w2g", [8 * 4096, 1024], BF16).ap()
    yp2 = nc.dram_tensor("yp2", [8, 128, T], F32).ap()
    yp2g = nc.dram_tensor("yp2g", [8 * 1024, T], F32).ap()
    x2 = nc.dram_tensor("x2", [8, 128, S], F32).ap()
    x2g = nc.dram_tensor("x2g", [8 * 1024, S], F32).ap()
    xw1 = nc.dram_tensor("xw1", [T, 4096], BF16).ap()
    sh1 = nc.dram_tensor("sh1", [T, 1024], BF16).ap()
    xw2 = nc.dram_tensor("xw2", [T, 4160], BF16).ap()
    sh2 = nc.dram_tensor("sh2", [T, 1024], BF16).ap()
    x3 = nc.dram_tensor("x3", [8, 128, S], F32).ap()
    ya1l = nc.dram_tensor("ya1l", [8, 128, S], F32).ap()
    yb1l = nc.dram_tensor("yb1l", [8, 128, S], F32).ap()
    ya2l = nc.dram_tensor("ya2l", [8, 128, S], F32).ap()
    yb2l = nc.dram_tensor("yb2l", [8, 128, S], F32).ap()
    x1full = nc.dram_tensor("x1full", [8, 128, T], F32).ap()
    x2full = nc.dram_tensor("x2full", [8, 128, T], F32).ap()
    hml = nc.dram_tensor("hml", [S, 1024], BF16).ap()
    xout = nc.dram_tensor("xout", [8, 128, S], F32, kind=EO).ap()

    pid = nc.partition_id()
    r0 = (pid >> 1) * 2
    r1 = r0 + 1
    hh = pid & 1
    p = Prog(nc)

    def fin(tag, src, shp, dt):
        dbg = nc.dram_tensor("dbg", shp, dt, kind=EO).ap()
        p.barrier()
        p.dma(nc.sync, dbg, src)
        p.barrier()
        return nc

    emit_wcast(nc, p, "wc_", cw1, cw2, w1o, w2o)
    p.barrier()
    sem_w1 = p.allgather(w1o.rearrange("a p h -> (a p) h"), w1g)
    sem_w2 = p.allgather(w2o.rearrange("a p h -> (a p) h"), w2g)

    emit_l0attn(nc, p, "a_", a_xin, a_wqp, a_wdn, a_wkup, a_wvup, a_wop,
                rc2, rs2, yp1)
    p.barrier()
    sem_y1 = p.allgather(yp1.rearrange("a p t -> (a p) t"), yp1g)
    p.cc_wait(sem_y1)

    yp1g_v = yp1g.rearrange("(r q) (h t) -> r h q t", r=8, h=2)
    p.dma(nc.sync, ya1l.rearrange("a p t -> (a p) t"), yp1g_v[r0, hh])
    p.dma(nc.sync, yb1l.rearrange("a p t -> (a p) t"), yp1g_v[r1, hh])
    p.barrier()
    emit_comb(nc, p, "cb_", x_tok,
              lambda c: ya1l[c], lambda c: yb1l[c], x1)
    p.barrier()
    if upto == "x1":
        return fin("x1", x1.rearrange("a p t -> (a p) t"), [1024, S], F32)
    sem_x1 = p.allgather(x1.rearrange("a p t -> (a p) t"), x1g)
    p.cc_wait(sem_x1)

    x1g_v = x1g.rearrange("(r q) t -> r q t", r=8)
    x1f_v = x1full.rearrange("a p (h t) -> (a p) h t", h=2)
    p.dma(nc.sync, x1f_v[:, 0], x1g_v[r0])
    p.dma(nc.sync, x1f_v[:, 1], x1g_v[r1])
    p.barrier()
    emit_l0fattn(nc, p, "f_", x1full, f_wqs, f_wks, f_wvs, f_bq, f_bk, f_bv,
                 f_l1g, f_l1b, f_wop, yp2)
    p.barrier()
    sem_y2 = p.allgather(yp2.rearrange("a p t -> (a p) t"), yp2g)
    p.cc_wait(sem_y2)
    p.cc_wait(sem_w1)
    p.cc_wait(sem_w2)

    yp2g_v = yp2g.rearrange("(r q) (h t) -> r h q t", r=8, h=2)
    p.dma(nc.sync, ya2l.rearrange("a p t -> (a p) t"), yp2g_v[r0, hh])
    p.dma(nc.sync, yb2l.rearrange("a p t -> (a p) t"), yp2g_v[r1, hh])
    p.barrier()
    w1b_v = w1g.rearrange("(e a p) h -> e a p h", e=8, a=8)
    w2b_v = w2g.rearrange("(e n p) d -> e n p d", e=8, n=32)
    emit_l0fmoe(nc, p, "m_", x1,
                lambda c: ya2l[c], lambda c: yb2l[c],
                m_bo, m_l2g, m_l2b, m_gw, m_gbr, w1b_v, w2b_v, m_b1c, m_b2r,
                ident, x2)
    p.barrier()
    if upto == "x2":
        return fin("x2", x2.rearrange("a p t -> (a p) t"), [1024, S], F32)
    sem_x2 = p.allgather(x2.rearrange("a p t -> (a p) t"), x2g)
    p.cc_wait(sem_x2)

    x2g_v = x2g.rearrange("(r q) t -> r q t", r=8)
    x2f_v = x2full.rearrange("a p (h t) -> (a p) h t", h=2)
    p.dma(nc.sync, x2f_v[:, 0], x2g_v[r0])
    p.dma(nc.sync, x2f_v[:, 1], x2g_v[r1])
    p.barrier()
    emit_premid(nc, p, "p1_", 4096, "fm", x2full, p1_wT, p1_b0, p1_b1, xw1)
    p.barrier()
    if upto == "xw1":
        return fin("xw1", xw1, [T, 4096], BF16)

    emit_scan(nc, p, "s1_", xw1, xaz, s1_u0, s1_u1, s1_pt, s1_al, sh1)
    if upto == "sh1":
        return fin("sh1", sh1, [T, 1024], BF16)

    emit_premid(nc, p, "p2_", 4160, "sh", sh1, p2_wT, p2_b0, p2_b1, xw2,
                ident_in=ident)
    p.barrier()

    emit_scan(nc, p, "s2_", xw2[:, 0:4096], xw2[:, 4096:4160],
              s2_u0, s2_u1, s2_pt, s2_al, sh2)
    if upto == "sh2":
        return fin("sh2", sh2, [T, 1024], BF16)

    sh2_v = sh2.rearrange("(h t) d -> h t d", h=2)
    p.dma(nc.sync, hml, sh2_v[hh])
    p.barrier()
    emit_post(nc, p, "po_", x2, hml, ident, x3)
    p.barrier()
    if upto == "x3":
        return fin("x3", x3.rearrange("a p t -> (a p) t"), [1024, S], F32)

    emit_l1ffn(nc, p, "l1_", x3, l1_wgT, l1_wuT, l1_wdT, xout)
    p.barrier()
    return nc


def build_inmaps(inp, x_fm):
    perm = []
    for h in range(16):
        perm += [64 * h + 2 * i for i in range(32)]
        perm += [64 * h + 2 * i + 1 for i in range(32)]
    perm = np.asarray(perm)
    wq_p = inp["L0_attn_wq"][:, perm]
    wk_p = inp["L0_attn_wk_up"][:, perm]
    inv = 1.0 / 10000.0 ** (np.arange(0, 64, 2, dtype=np.float32) / 64)
    ang = np.outer(np.arange(T, dtype=np.float32), inv)
    cos2 = np.ascontiguousarray(np.concatenate([np.cos(ang).T] * 4, axis=0)).astype(np.float32)
    sin2 = np.ascontiguousarray(np.concatenate([np.sin(ang).T] * 4, axis=0)).astype(np.float32)

    w_qkv, b_qkv = inp["L0_ffn_w_qkv"], inp["L0_ffn_b_qkv"]
    identm = np.eye(128, dtype=np.float32)

    def scan_prep(U, Bm, P, alpha):
        UT = np.ascontiguousarray(U.T)
        u0 = np.ascontiguousarray(UT.reshape(8, 128, 4096)[:, :, 0:2048])
        u1z = UT.reshape(8, 128, 4096)[:, :, 2048:4096]
        BmT = np.ascontiguousarray(Bm.T).reshape(8, 128, 64)
        u1 = np.ascontiguousarray(np.concatenate([u1z, BmT], axis=2))
        return (u0, u1, np.ascontiguousarray(P.T),
                np.ascontiguousarray(alpha.reshape(1, 1024)))

    z64 = np.zeros((64, 1024), np.float32)
    zP = np.zeros((1024, 64), np.float32)
    s1 = scan_prep(inp["L1_attn_Us"], z64, zP, inp["L1_attn_alpha"])
    s2 = scan_prep(inp["L1_attn_Um"], inp["L1_attn_Bm"], inp["L1_attn_P"],
                   np.ones(1024, np.float32))
    p1_wT = np.ascontiguousarray(inp["L1_attn_Ws"].T).reshape(8, 128, 4096)
    p2_wT = np.ascontiguousarray(
        np.concatenate([inp["L1_attn_Wm"].T, inp["L1_attn_A"].T], axis=1)
    ).reshape(8, 128, 4160)
    xaz = np.zeros((T, 64), dtype=BF)
    wgT = np.ascontiguousarray(inp["L1_ffn_w_gate"].T).reshape(8, 128, 3072)
    wuT = np.ascontiguousarray(inp["L1_ffn_w_up"].T).reshape(8, 128, 3072)
    wdT = np.ascontiguousarray(inp["L1_ffn_w_down"].T).reshape(24, 128, 1024)

    maps = []
    for c in range(NCORES):
        b2, hh = c // 2, c % 2
        hcols = slice(512 * hh, 512 * (hh + 1))
        qcols = slice(512 * hh, 512 * (hh + 1))
        kcols = slice(1024 + 512 * hh, 1024 + 512 * (hh + 1))
        vcols = slice(2048 + 512 * hh, 2048 + 512 * (hh + 1))
        m = {
            "a_xin": np.ascontiguousarray(x_fm[:, b2 * T:(b2 + 1) * T]).reshape(8, 128, T),
            "a_wqp": np.ascontiguousarray(wq_p[:, hcols]).reshape(8, 128, 512),
            "a_wdn": np.ascontiguousarray(inp["L0_attn_w_down"]).reshape(8, 128, 512),
            "a_wkup": np.ascontiguousarray(wk_p[:, hcols]).reshape(4, 128, 512),
            "a_wvup": np.ascontiguousarray(
                inp["L0_attn_wv_up"][:, 512 * hh:512 * (hh + 1)]).reshape(4, 128, 512),
            "a_wop": np.ascontiguousarray(
                inp["L0_attn_wo"][512 * hh:512 * (hh + 1), :]).reshape(4, 128, D),
            "rc2": cos2, "rs2": sin2,
            "x_tok": np.ascontiguousarray(x_fm[:, c * S:(c + 1) * S]).reshape(8, 128, S),
            "cw1": np.ascontiguousarray(inp["L0_ffn_W1"][c]).reshape(8, 128, 4096),
            "cw2": np.ascontiguousarray(inp["L0_ffn_W2"][c]).reshape(32, 128, 1024),
            "f_wqs": np.ascontiguousarray(w_qkv[:, qcols]).reshape(8, 128, 512),
            "f_wks": np.ascontiguousarray(w_qkv[:, kcols]).reshape(8, 128, 512),
            "f_wvs": np.ascontiguousarray(w_qkv[:, vcols]).reshape(8, 128, 512),
            "f_bq": np.ascontiguousarray(b_qkv[qcols].reshape(4, 128).T),
            "f_bk": np.ascontiguousarray(b_qkv[kcols].reshape(4, 128).T),
            "f_bv": np.ascontiguousarray(b_qkv[vcols]).reshape(1, 512),
            "f_l1g": np.ascontiguousarray(inp["L0_ffn_ln1_g"].reshape(8, 128).T),
            "f_l1b": np.ascontiguousarray(inp["L0_ffn_ln1_b"].reshape(8, 128).T),
            "f_wop": np.ascontiguousarray(
                inp["L0_ffn_w_o"][512 * hh:512 * (hh + 1), :]).reshape(4, 128, D),
            "m_bo": np.ascontiguousarray(inp["L0_ffn_b_o"].reshape(8, 128).T),
            "m_l2g": np.ascontiguousarray(inp["L0_ffn_ln2_g"].reshape(8, 128).T),
            "m_l2b": np.ascontiguousarray(inp["L0_ffn_ln2_b"].reshape(8, 128).T),
            "m_gw": np.ascontiguousarray(inp["L0_ffn_gate_w"]).reshape(8, 128, 8),
            "m_gbr": np.ascontiguousarray(inp["L0_ffn_gate_b"]).reshape(1, 8),
            "m_b1c": np.ascontiguousarray(
                inp["L0_ffn_b1"].reshape(8, 32, 128).transpose(0, 2, 1)),
            "m_b2r": np.ascontiguousarray(inp["L0_ffn_b2"]).reshape(8, 1, 1024),
            "ident": identm,
            "p1_wT": p1_wT,
            "p1_b0": np.ascontiguousarray(inp["L1_attn_bWs"]).reshape(1, 4096),
            "p1_b1": np.ascontiguousarray(inp["L1_attn_bUs"]).reshape(1, 4096),
            "p2_wT": p2_wT,
            "p2_b0": np.ascontiguousarray(inp["L1_attn_bWm"]).reshape(1, 4096),
            "p2_b1": np.ascontiguousarray(inp["L1_attn_bUm"]).reshape(1, 4096),
            "s1_u0": s1[0], "s1_u1": s1[1], "s1_pt": s1[2], "s1_al": s1[3],
            "s2_u0": s2[0], "s2_u1": s2[1], "s2_pt": s2[2], "s2_al": s2[3],
            "xaz": xaz,
            "l1_wgT": wgT, "l1_wuT": wuT, "l1_wdT": wdT,
        }
        maps.append(m)
    return maps


def kernel(**inputs):
    inp = {k: np.asarray(v) for k, v in inputs.items()}
    x = inp["L0_attn_x"]
    x_fm = np.ascontiguousarray(x.reshape(NTOK, D).T)
    maps = build_inmaps(inp, x_fm)
    nc = build_full()
    res = run_bass_kernel_spmd(nc, maps, core_ids=list(range(NCORES)))
    out_fm = np.concatenate([res.results[c]["xout"].reshape(D, S)
                             for c in range(NCORES)], axis=1)
    return np.ascontiguousarray(out_fm.T).reshape(B, T, D)


# version 04
# speedup vs baseline: 1.3916x; 1.3916x over previous
import numpy as np
import ml_dtypes
import concourse.bass as bass
import concourse.mybir as mybir
from concourse.bass_utils import run_bass_kernel_spmd

F32 = mybir.dt.float32
BF16 = mybir.dt.bfloat16
I32 = mybir.dt.int32
AF = mybir.ActivationFunctionType
ALU = mybir.AluOpType
BF = ml_dtypes.bfloat16

NCORES = 8
B, T, D = 4, 4096, 1024
NTOK = B * T
S = NTOK // NCORES
EPS = 1e-5


def ceil_div(a, b):
    return -(-a // b)

_UID = [0]


def _nm(base):
    _UID[0] += 1
    return f"{base}_{_UID[0]}"


def mm(nc, out, lhsT, rhs, start, stop, tile_position=None):
    n = rhs.shape[-1]
    kw = {} if tile_position is None else {"tile_position": tile_position}
    for j in range(ceil_div(n, 512)):
        cols = slice(j * 512, min((j + 1) * 512, n))
        nc.tensor.matmul(out[..., cols], lhsT, rhs[..., cols],
                         start=start, stop=stop, skip_group_check=True, **kw)


class Prog:

    def __init__(self, nc):
        self.nc = nc
        self.dma_sems = {}
        self.dma_counts = {}
        self.cc_uid = 0

    def dma(self, qeng, out, in_, **kw):
        key = qeng.engine.name
        if key not in self.dma_sems:
            self.dma_sems[key] = (qeng, self.nc.alloc_semaphore(f"dmasem_{key}"))
            self.dma_counts[key] = 0
        _, sem = self.dma_sems[key]
        qeng.dma_start(out=out, in_=in_, **kw).then_inc(sem, 16)
        self.dma_counts[key] += 16
        return sem

    def dma_wait_all(self):
        for key, (eng, sem) in self.dma_sems.items():
            eng.wait_ge(sem, self.dma_counts[key])

    def barrier(self):
        self.dma_wait_all()
        self.nc.all_engine_barrier()

    def allgather(self, in_ap2d, out_ap2d):
        self.cc_uid += 1
        sem = self.nc.alloc_semaphore(f"ccsem_{self.cc_uid}")
        self.nc.gpsimd.collective_compute(
            "AllGather", mybir.AluOpType.bypass,
            replica_groups=[list(range(8))],
            ins=[in_ap2d.opt()],
            outs=[out_ap2d.opt()],
        ).then_inc(sem)
        return sem

    def cc_wait(self, sem):
        self.nc.gpsimd.wait_ge(sem, 1)
        self.nc.all_engine_barrier()


def emit_ln_fm(p, xb_out, sq_buf, row_r, row_mr, t_cols, nchunk=8,
               x_src_chunks=None):
    nc = p.nc
    dmodel = 128 * nchunk
    with (
        nc.sbuf_tensor(_nm("ln_ones"), [128, 1], BF16) as ones,
        nc.sbuf_tensor(_nm("ln_m"), [1, t_cols], F32) as row_m,
        nc.sbuf_tensor(_nm("ln_v"), [1, t_cols], F32) as row_v,
        nc.psum_tensor(_nm("ln_ps"), [1, t_cols], F32) as ps_s,
        nc.psum_tensor(_nm("ln_ps2"), [1, t_cols], F32) as ps_q,
    ):
        nc.vector.memset(ones[:, :], 1.0)
        p.barrier()
        ncol = ceil_div(t_cols, 512)
        src = x_src_chunks if x_src_chunks is not None else [
            xb_out[:, c] for c in range(nchunk)]
        for c in range(nchunk):
            nc.scalar.activation(out=sq_buf[:, :], in_=src[c], func=AF.Square)
            p.barrier()
            for j in range(ncol):
                cols = slice(j * 512, min((j + 1) * 512, t_cols))
                nc.tensor.matmul(ps_s[0:1, cols], ones[:, :], src[c][:, cols],
                                 start=(c == 0), stop=(c == nchunk - 1),
                                 skip_group_check=True)
                nc.tensor.matmul(ps_q[0:1, cols], ones[:, :], sq_buf[:, cols],
                                 start=(c == 0), stop=(c == nchunk - 1),
                                 skip_group_check=True)
            p.barrier()
        m = row_m[0:1, :]
        var = row_v[0:1, :]
        nc.vector.tensor_scalar_mul(m, ps_s[0:1, :], 1.0 / dmodel)
        p.barrier()
        nc.vector.tensor_mul(var, m, m)
        p.barrier()
        nc.vector.scalar_tensor_tensor(out=var, in0=ps_q[0:1, :], scalar=1.0 / dmodel,
                                       in1=var, op0=ALU.mult, op1=ALU.subtract)
        p.barrier()
        nc.vector.tensor_scalar_add(var, var, EPS)
        p.barrier()
        nc.scalar.activation(out=var, in_=var, func=AF.Sqrt)
        p.barrier()
        nc.vector.reciprocal(row_r[0:1, :], var)
        p.barrier()
        nc.vector.tensor_mul(row_mr[0:1, :], m, row_r[0:1, :])
        p.barrier()


def emit_bcast_row(p, row_f32, ps_out, t_cols, row_bf):
    nc = p.nc
    with nc.sbuf_tensor(_nm("bc_ones"), [1, 128], BF16) as ones1:
        nc.vector.memset(ones1[:, :], 1.0)
        nc.vector.tensor_copy(out=row_bf[0:1, 0:t_cols], in_=row_f32)
        p.barrier()
        for j in range(ceil_div(t_cols, 512)):
            cols = slice(j * 512, min((j + 1) * 512, t_cols))
            nc.tensor.matmul(ps_out[:, cols], ones1[:, :], row_bf[0:1, cols],
                             start=True, stop=True)
        p.barrier()


def emit_scan(nc, p, pfx, xw_ap, xa_ap, u0, u1, pT, alpha, sh_ap, T_steps=T):
    import contextlib
    N = lambda s: pfx + s
    es = contextlib.ExitStack()
    with es:
        u0b = es.enter_context(nc.sbuf_tensor(N("u0b"), [128, 8, 2048], BF16))
        u1b = es.enter_context(nc.sbuf_tensor(N("u1b"), [128, 8, 2112], BF16))
        bmb = es.enter_context(nc.sbuf_tensor(N("bmb"), [128, 8, 64], BF16))
        stgf = es.enter_context(nc.sbuf_tensor(N("stgf"), [128, 2112], F32))
        pTb = es.enter_context(nc.sbuf_tensor(N("pTb"), [64, 1024], BF16))
        alpr = es.enter_context(nc.sbuf_tensor(N("alpr"), [1, 1024], F32))
        xwrow = es.enter_context(nc.sbuf_tensor(N("xwrow"), [1, 4096], BF16))
        xac = es.enter_context(nc.sbuf_tensor(N("xac"), [64, 1], BF16))
        one1 = es.enter_context(nc.sbuf_tensor(N("one1"), [1, 1], BF16))
        ident = es.enter_context(nc.sbuf_tensor(N("ident"), [1, 1], BF16))
        hT = es.enter_context(nc.sbuf_tensor(N("hT"), [128, 8], BF16))
        hb = es.enter_context(nc.sbuf_tensor(N("hb"), [1, 1024], BF16))
        cst = es.enter_context(nc.sbuf_tensor(N("cst"), [1, 1024], F32))
        sif = es.enter_context(nc.sbuf_tensor(N("sif"), [1, 2048], F32))
        sog = es.enter_context(nc.sbuf_tensor(N("sog"), [1, 2048], F32))
        rb = es.enter_context(nc.sbuf_tensor(N("rb"), [64, 1], BF16))
        tm1 = es.enter_context(nc.sbuf_tensor(N("tm1"), [1, 1024], F32))
        tm2 = es.enter_context(nc.sbuf_tensor(N("tm2"), [1, 1024], F32))
        tc = es.enter_context(nc.sbuf_tensor(N("tc"), [1, 1024], F32))
        Zp = es.enter_context(nc.psum_tensor(N("Z"), [128, 2048], F32))
        psM = es.enter_context(nc.psum_tensor(N("psM"), [1, 1024], F32))
        pst = es.enter_context(nc.psum_tensor(N("pst"), [128, 8, 2], BF16))
        scr = es.enter_context(nc.psum_tensor(N("scr"), [1, 512], F32))
        ld = nc.alloc_semaphore(N("ld"))
        st = nc.alloc_semaphore(N("st"))
        spe = nc.alloc_semaphore(N("spe"))
        sact = nc.alloc_semaphore(N("sact"))
        sdve = nc.alloc_semaphore(N("sdve"))
        nc.vector.memset(one1[:, :], 1.0)
        nc.vector.memset(ident[:, :], 1.0)
        nc.vector.memset(hT[:, :], 0.0)
        nc.vector.memset(cst[:, :], 0.0)
        p.dma(nc.sync, stgf[0:64, 0:1024], pT[:, :])
        p.barrier()
        nc.vector.tensor_copy(out=pTb[:, :], in_=stgf[0:64, 0:1024])
        p.barrier()
        p.dma(nc.sync, alpr[0:1, :], alpha[:, :])
        p.barrier()
        for k in range(8):
            p.dma(nc.sync, stgf[:, 0:2048], u0[k])
            p.barrier()
            nc.vector.tensor_copy(out=u0b[:, k], in_=stgf[:, 0:2048])
            p.barrier()
            p.dma(nc.sync, stgf[:, 0:2112], u1[k])
            p.barrier()
            nc.vector.tensor_copy(out=u1b[:, k], in_=stgf[:, 0:2112])
            p.barrier()
            nc.vector.tensor_copy(out=bmb[:, k], in_=stgf[:, 2048:2112])
            p.barrier()
        nc.sync.dma_start(out=xwrow[0:1, :], in_=xw_ap[0]).then_inc(ld, 16)
        nc.sync.dma_start(out=xac[:, :], in_=xa_ap[0]).then_inc(ld, 16)
        p.barrier()

        with nc.Fori(0, T_steps) as i:
            nc.sync.wait_ge(spe, i * 4 + 1)
            nc.sync.dma_start(out=xwrow[0:1, :], in_=xw_ap[i + 1]).then_inc(ld, 16)
            nc.sync.wait_ge(sdve, i * 8 + 1)
            nc.sync.dma_start(out=xac[:, :], in_=xa_ap[i + 1]).then_inc(ld, 16)
            nc.tensor.wait_ge(sdve, i * 8)
            nc.tensor.wait_ge(ld, i * 32 + 32)
            for k in range(8):
                mm(nc, Zp[0:1, 0:2048], hT[:, k:k + 1], u0b[:, k],
                   start=(k == 0), stop=False, tile_position=(0, 0))
                mm(nc, Zp[32:33, 0:2048], hT[:, k:k + 1], u1b[:, k, 0:2048],
                   start=(k == 0), stop=False, tile_position=(0, 32))
            mm(nc, Zp[0:1, 0:2048], one1[0:1, :], xwrow[0:1, 0:2048],
               start=False, stop=True, tile_position=(0, 0))
            for j in range(4):
                cols = slice(j * 512, (j + 1) * 512)
                ins = nc.tensor.matmul(Zp[32:33, cols], one1[0:1, :],
                                       xwrow[0:1, 2048 + j * 512:2048 + (j + 1) * 512],
                                       start=False, stop=True,
                                       skip_group_check=True,
                                       tile_position=(0, 32))
            ins.then_inc(spe, 1)
            for k in range(8):
                ins = nc.tensor.matmul(Zp[64:128, 2047:2048], bmb[:, k],
                                       hT[:, k:k + 1], start=(k == 0),
                                       stop=(k == 7), skip_group_check=True,
                                       tile_position=(0, 64))
            ins.then_inc(spe, 1)
            nc.tensor.wait_ge(sdve, i * 8 + 1)
            ins = None
            for j in range(2):
                ins = nc.tensor.matmul(psM[0:1, j * 512:(j + 1) * 512], rb[:, 0:1],
                                       pTb[:, j * 512:(j + 1) * 512],
                                       start=True, stop=True, skip_group_check=True)
            ins.then_inc(spe, 1)
            for dk in range(5):
                mm(nc, scr[0:1, :], hT[:, 0:1], u0b[:, dk, 0:512],
                   start=True, stop=True)
            nc.tensor.wait_ge(sdve, i * 8 + 7)
            for k in range(8):
                ins = nc.tensor.transpose(pst[:, k, 0:1],
                                          hb[0:1, k * 128:(k + 1) * 128],
                                          ident[0:1, 0:1])
            ins.then_inc(spe, 1)
            nc.scalar.wait_ge(spe, i * 4 + 1)
            nc.scalar.activation(out=sif[0:1, :], in_=Zp[0:1, 0:2048],
                                 func=AF.Sigmoid).then_inc(sact, 1)
            nc.scalar.activation(out=sog[0:1, 0:1024], in_=Zp[32:33, 0:1024],
                                 func=AF.Sigmoid)
            nc.scalar.activation(out=sog[0:1, 1024:2048], in_=Zp[32:33, 1024:2048],
                                 func=AF.Tanh).then_inc(sact, 1)
            nc.scalar.wait_ge(sdve, i * 8 + 6)
            nc.scalar.activation(out=tc[0:1, :], in_=cst[0:1, :],
                                 func=AF.Tanh).then_inc(sact, 1)
            nc.vector.wait_ge(spe, i * 4 + 2)
            nc.vector.wait_ge(ld, i * 32 + 32)
            nc.vector.tensor_mul(rb[:, 0:1], xac[:, 0:1],
                                 Zp[64:128, 2047:2048]).then_inc(sdve, 1)
            nc.vector.wait_ge(sact, i * 3 + 1)
            nc.vector.tensor_mul(tm2[0:1, :], sif[0:1, 1024:2048],
                                 cst[0:1, :]).then_inc(sdve, 1)
            nc.vector.wait_ge(sact, i * 3 + 2)
            nc.vector.tensor_mul(tm1[0:1, :], sif[0:1, 0:1024],
                                 sog[0:1, 1024:2048]).then_inc(sdve, 1)
            nc.vector.wait_ge(sdve, i * 8 + 3)
            nc.vector.tensor_add(tm1[0:1, :], tm1[0:1, :],
                                 tm2[0:1, :]).then_inc(sdve, 1)
            nc.vector.wait_ge(sdve, i * 8 + 4)
            nc.vector.tensor_mul(tm1[0:1, :], tm1[0:1, :],
                                 alpr[0:1, :]).then_inc(sdve, 1)
            nc.vector.wait_ge(spe, i * 4 + 3)
            nc.vector.wait_ge(sdve, i * 8 + 5)
            nc.vector.scalar_tensor_tensor(out=cst[0:1, :], in0=psM[0:1, :],
                                           scalar=0.1, in1=tm1[0:1, :],
                                           op0=ALU.mult,
                                           op1=ALU.add).then_inc(sdve, 1)
            nc.vector.wait_ge(sact, i * 3 + 3)
            nc.vector.wait_ge(st, i * 16)
            nc.vector.tensor_mul(hb[0:1, :], sog[0:1, 0:1024],
                                 tc[0:1, :]).then_inc(sdve, 1)
            nc.vector.wait_ge(spe, i * 4 + 4)
            nc.vector.tensor_copy(out=hT[:, :],
                                  in_=pst[:, 0:8, 0]).then_inc(sdve, 1)
            nc.gpsimd.wait_ge(sdve, i * 8 + 7)
            nc.gpsimd.dma_start(out=sh_ap[i], in_=hb[0:1, :]).then_inc(st, 16)
        nc.sync.wait_ge(ld, 32 * (T_steps + 1))
        nc.gpsimd.wait_ge(st, 16 * T_steps)
        nc.tensor.wait_ge(sdve, 8 * T_steps)
        nc.all_engine_barrier()
    p.barrier()


def emit_premid(nc, p, pfx, zw, src_kind, src, wT, b0, b1, xwout, ident_in=None):
    import contextlib
    N = lambda s: pfx + s
    es = contextlib.ExitStack()
    with es:
        xp = es.enter_context(nc.sbuf_tensor(N("xp"), [128, 8, T], BF16))
        if src_kind == "fm":
            xfull = src
            with (
                nc.sbuf_tensor(N("stgi"), [128, S], F32) as stgi,
                nc.sbuf_tensor(N("sq"), [128, S], BF16) as sq,
                nc.sbuf_tensor(N("rowr"), [1, S], F32) as rowr,
                nc.sbuf_tensor(N("rowmr"), [1, S], F32) as rowmr,
                nc.sbuf_tensor(N("rowbf"), [1, S], BF16) as rowbf,
            ):
                for half in range(2):
                    hcols = slice(half * S, (half + 1) * S)
                    for c in range(8):
                        p.dma(nc.sync, stgi[:, :], xfull[c, :, hcols])
                        p.barrier()
                        nc.vector.tensor_copy(out=xp[:, c, hcols], in_=stgi[:, :])
                        p.barrier()
                    emit_ln_fm(p, None, sq, rowr[0:1], rowmr[0:1], S,
                               x_src_chunks=[xp[:, c, hcols] for c in range(8)])
                    with (
                        nc.psum_tensor(_nm("rps"), [128, 512], F32) as rps,
                        nc.psum_tensor(_nm("mrps"), [128, 512], F32) as mrps,
                    ):
                        for tt in range(S // 512):
                            cols = slice(half * S + tt * 512, half * S + (tt + 1) * 512)
                            emit_bcast_row(p, rowr[0:1, tt * 512:(tt + 1) * 512],
                                           rps, 512, rowbf)
                            emit_bcast_row(p, rowmr[0:1, tt * 512:(tt + 1) * 512],
                                           mrps, 512, rowbf)
                            for c in range(8):
                                nc.vector.tensor_mul(xp[:, c, cols], xp[:, c, cols],
                                                     rps[:, :])
                            p.barrier()
                            for c in range(8):
                                nc.vector.tensor_sub(xp[:, c, cols], xp[:, c, cols],
                                                     mrps[:, :])
                            p.barrier()
        else:
            sh_src = src
            with (
                nc.sbuf_tensor(N("stg"), [128, 1024], BF16) as stg,
                nc.sbuf_tensor(N("stgi"), [128, 128], F32) as stgi,
                nc.sbuf_tensor(N("idb"), [128, 128], BF16) as idb,
            ):
                p.dma(nc.sync, stgi[:, 0:128], ident_in[:, :])
                p.barrier()
                nc.vector.tensor_copy(out=idb[:, :], in_=stgi[:, 0:128])
                p.barrier()
                for tt in range(T // 128):
                    p.dma(nc.sync, stg[:, :], sh_src[tt * 128:(tt + 1) * 128, :])
                    p.barrier()
                    with nc.psum_tensor(_nm("tps"), [128, 8, 128], BF16) as tps:
                        for dc in range(8):
                            nc.tensor.transpose(tps[:, dc],
                                                stg[:, dc * 128:(dc + 1) * 128],
                                                idb[:, :])
                        p.barrier()
                        for dc in range(8):
                            nc.vector.tensor_copy(
                                out=xp[:, dc, tt * 128:(tt + 1) * 128],
                                in_=tps[:, dc])
                        p.barrier()
        wTb = es.enter_context(nc.sbuf_tensor(N("wTb"), [128, 8, zw], BF16))
        brf = es.enter_context(nc.sbuf_tensor(N("brf"), [1, 4096], BF16))
        onec = es.enter_context(nc.sbuf_tensor(N("onec"), [1, 128], BF16))
        xwt = es.enter_context(nc.sbuf_tensor(N("xwt"), [128, zw], BF16))
        nc.vector.memset(onec[:, :], 1.0)
        with (
            nc.sbuf_tensor(N("stgw"), [128, zw], F32) as stgw,
            nc.sbuf_tensor(N("b1row"), [1, 4096], F32) as b1row,
        ):
            p.dma(nc.sync, stgw[0:1, 0:4096], b0[:, :])
            p.dma(nc.sync, b1row[0:1, :], b1[:, :])
            p.barrier()
            nc.vector.tensor_add(stgw[0:1, 0:4096], stgw[0:1, 0:4096],
                                 b1row[0:1, :])
            p.barrier()
            nc.vector.tensor_copy(out=brf[0:1, :], in_=stgw[0:1, 0:4096])
            p.barrier()
            for dc in range(8):
                p.dma(nc.sync, stgw[:, 0:zw], wT[dc])
                p.barrier()
                nc.vector.tensor_copy(out=wTb[:, dc], in_=stgw[:, 0:zw])
                p.barrier()
        for tt in range(T // 128):
            ts = slice(tt * 128, (tt + 1) * 128)
            with nc.psum_tensor(_nm("gps"), [128, 8, 512], F32) as ps:
                for zc in range(8):
                    zs = slice(zc * 512, (zc + 1) * 512)
                    for dc in range(8):
                        nc.tensor.matmul(ps[:, zc], xp[:, dc, ts], wTb[:, dc, zs],
                                         start=(dc == 0), stop=False,
                                         skip_group_check=True)
                    nc.tensor.matmul(ps[:, zc], onec[0:1, :], brf[0:1, zs],
                                     start=False, stop=True,
                                     skip_group_check=True)
                p.barrier()
                for zc in range(8):
                    nc.vector.tensor_copy(out=xwt[:, zc * 512:(zc + 1) * 512],
                                          in_=ps[:, zc])
                p.barrier()
            if zw > 4096:
                with nc.psum_tensor(_nm("xps"), [128, zw - 4096], F32) as ps2:
                    for dc in range(8):
                        nc.tensor.matmul(ps2[:, :], xp[:, dc, ts], wTb[:, dc, 4096:zw],
                                         start=(dc == 0), stop=(dc == 7),
                                         skip_group_check=True)
                    p.barrier()
                    nc.vector.tensor_copy(out=xwt[:, 4096:zw], in_=ps2[:, :])
                    p.barrier()
            p.dma(nc.sync, xwout[ts], xwt[:, :])
            p.barrier()


def emit_post(nc, p, pfx, xin, hm_src, ident_in, xout_ap):
    import contextlib
    N = lambda s: pfx + s
    es = contextlib.ExitStack()
    with es:
        ymf = es.enter_context(nc.sbuf_tensor(N("ymf"), [128, 8, S], BF16))
        idb = es.enter_context(nc.sbuf_tensor(N("idb"), [128, 128], BF16))
        stg = es.enter_context(nc.sbuf_tensor(N("stg"), [128, 1024], BF16))
        stf = es.enter_context(nc.sbuf_tensor(N("stf"), [128, 1024], F32))
        sq = es.enter_context(nc.sbuf_tensor(N("sq"), [128, S], BF16))
        rowr = es.enter_context(nc.sbuf_tensor(N("rowr"), [1, S], F32))
        rowmr = es.enter_context(nc.sbuf_tensor(N("rowmr"), [1, S], F32))
        rowbf = es.enter_context(nc.sbuf_tensor(N("rowbf"), [1, S], BF16))
        xst = es.enter_context(nc.sbuf_tensor(N("xst"), [128, S // 2], F32))
        p.dma(nc.sync, stf[:, 0:128], ident_in[:, :])
        p.barrier()
        nc.vector.tensor_copy(out=idb[:, :], in_=stf[:, 0:128])
        p.barrier()
        for tt in range(S // 128):
            p.dma(nc.sync, stg[:, :], hm_src[tt * 128:(tt + 1) * 128, :])
            p.barrier()
            with nc.psum_tensor(_nm("tps"), [128, 8, 128], BF16) as tps:
                for dc in range(8):
                    nc.tensor.transpose(tps[:, dc], stg[:, dc * 128:(dc + 1) * 128],
                                        idb[:, :])
                p.barrier()
                for dc in range(8):
                    nc.vector.tensor_copy(out=ymf[:, dc, tt * 128:(tt + 1) * 128],
                                          in_=tps[:, dc])
                p.barrier()
        emit_ln_fm(p, None, sq, rowr[0:1], rowmr[0:1], S,
                   x_src_chunks=[ymf[:, c] for c in range(8)])
        with (
            nc.psum_tensor(N("rps2"), [128, S // 2], F32) as rps2,
            nc.psum_tensor(N("mrps2"), [128, S // 2], F32) as mrps2,
        ):
            for half in range(2):
                cols = slice(half * (S // 2), (half + 1) * (S // 2))
                emit_bcast_row(p, rowr[0:1, cols], rps2, S // 2, rowbf)
                emit_bcast_row(p, rowmr[0:1, cols], mrps2, S // 2, rowbf)
                with nc.sbuf_tensor(_nm("swo"), [128, S // 2], F32) as tmp:
                    for c in range(8):
                        p.dma(nc.sync, xst[:, :],
                              xin[c].rearrange("p (h t) -> p h t", h=2)[:, half])
                        p.barrier()
                        nc.vector.tensor_mul(tmp[:, :], ymf[:, c, cols], rps2[:, :])
                        p.barrier()
                        nc.vector.tensor_sub(tmp[:, :], tmp[:, :], mrps2[:, :])
                        p.barrier()
                        nc.vector.tensor_add(tmp[:, :], tmp[:, :], xst[:, :])
                        p.barrier()
                        p.dma(nc.sync,
                              xout_ap[c].rearrange("p (h t) -> p h t", h=2)[:, half],
                              tmp[:, :])
                        p.barrier()


def emit_l0attn(nc, p, pfx, xin, wqp, wdn, wkup, wvup, wop, rc2, rs2, yp):
    import contextlib
    N = lambda s: pfx + s
    NT = T // 128
    es = contextlib.ExitStack()
    with es:
        qT = es.enter_context(nc.sbuf_tensor(N("qT"), [128, 4, T], BF16))
        kT = es.enter_context(nc.sbuf_tensor(N("kT"), [128, 4, T], BF16))
        vh = es.enter_context(nc.sbuf_tensor(N("vh"), [128, NT, 8, 65], BF16))
        OT = es.enter_context(nc.sbuf_tensor(N("OT"), [128, 4, T], BF16))
        nb = es.enter_context(nc.sbuf_tensor(N("nb"), [128, 1], BF16))
        onec = es.enter_context(nc.sbuf_tensor(N("onec"), [1, 128], BF16))
        onek = es.enter_context(nc.sbuf_tensor(N("onek"), [128, 1], BF16))
        nc.vector.memset(onec[:, :], 1.0)
        nc.vector.memset(onek[:, :], 1.0)
        nc.vector.memset(vh.ap().rearrange("p a b c -> p (a b c)")[:, :], 0.0)
        nc.vector.memset(
            vh.ap().rearrange("p a b c -> p (a b) c", c=65)[:, :, 64:65], 1.0)
        p.barrier()

        with nc.sbuf_tensor(N("latT"), [128, 4, T], BF16) as latT:
            with (
                nc.sbuf_tensor(N("wqb"), [128, 8, 512], BF16) as wqb,
                nc.sbuf_tensor(N("wdnb"), [128, 8, 512], BF16) as wdnb,
                nc.sbuf_tensor(N("stgf"), [128, 2048], F32) as stgf,
                nc.sbuf_tensor(N("sqb"), [128, 512], BF16) as sqb,
                nc.sbuf_tensor(N("rowr"), [1, 512], F32) as rowr,
                nc.sbuf_tensor(N("rowmr"), [1, 512], F32) as rowmr,
                nc.sbuf_tensor(N("rowbf"), [1, 512], BF16) as rowbf,
            ):
                for wsrc, wdst in ((wqp, wqb), (wdn, wdnb)):
                    for c2 in range(2):
                        p.dma(nc.sync,
                              stgf.ap().rearrange("p (a b) -> p a b", a=4),
                              wsrc[c2 * 4:(c2 + 1) * 4].rearrange("a p b -> p a b"))
                        p.barrier()
                        nc.vector.tensor_copy(
                            out=wdst[:, c2 * 4:(c2 + 1) * 4],
                            in_=stgf.ap().rearrange("p (a b) -> p a b", a=4))
                        p.barrier()
                for tq8 in range(8):
                    tcols = slice(tq8 * 512, (tq8 + 1) * 512)
                    with nc.sbuf_tensor(_nm("xpq"), [128, 8, 512], BF16) as xp:
                        for c in range(8):
                            p.dma(nc.sync, stgf[:, 0:512], xin[c, :, tcols])
                            p.barrier()
                            nc.vector.tensor_copy(out=xp[:, c], in_=stgf[:, 0:512])
                            p.barrier()
                        emit_ln_fm(p, None, sqb, rowr[0:1], rowmr[0:1], 512,
                                   x_src_chunks=[xp[:, c] for c in range(8)])
                        with (
                            nc.psum_tensor(_nm("rps"), [128, 512], F32) as rps,
                            nc.psum_tensor(_nm("mrps"), [128, 512], F32) as mrps,
                        ):
                            emit_bcast_row(p, rowr[0:1, 0:512], rps, 512, rowbf)
                            emit_bcast_row(p, rowmr[0:1, 0:512], mrps, 512, rowbf)
                            for c in range(8):
                                nc.vector.tensor_mul(xp[:, c], xp[:, c], rps[:, :])
                            p.barrier()
                            for c in range(8):
                                nc.vector.tensor_sub(xp[:, c], xp[:, c], mrps[:, :])
                            p.barrier()
                        for wdst, dst in ((wqb, qT), (wdnb, latT)):
                            with nc.psum_tensor(_nm("qps"), [128, 4, 512], F32) as ps:
                                for oc in range(4):
                                    for c in range(8):
                                        nc.tensor.matmul(
                                            ps[:, oc],
                                            wdst[:, c, oc * 128:(oc + 1) * 128],
                                            xp[:, c],
                                            start=(c == 0), stop=(c == 7),
                                            skip_group_check=True)
                                p.barrier()
                                for oc in range(4):
                                    nc.vector.tensor_copy(
                                        out=dst[:, oc, tcols], in_=ps[:, oc])
                                p.barrier()
            with (
                nc.sbuf_tensor(N("wkb"), [128, 4, 512], BF16) as wkb,
                nc.sbuf_tensor(N("wvb"), [128, 4, 512], BF16) as wvb,
                nc.sbuf_tensor(N("stg2"), [128, 2048], F32) as stg2,
            ):
                for wsrc, wdst in ((wkup, wkb), (wvup, wvb)):
                    p.dma(nc.sync,
                          stg2.ap().rearrange("p (a b) -> p a b", a=4),
                          wsrc[:, :, :].rearrange("a p b -> p a b"))
                    p.barrier()
                    nc.vector.tensor_copy(
                        out=wdst[:, :, :],
                        in_=stg2.ap().rearrange("p (a b) -> p a b", a=4))
                    p.barrier()
                with nc.psum_tensor(_nm("kps"), [128, 4, 512], F32) as ps:
                    for oc in range(4):
                        for tt in range(8):
                            cols = slice(tt * 512, (tt + 1) * 512)
                            for c in range(4):
                                nc.tensor.matmul(ps[:, tt % 4],
                                                 wkb[:, c, oc * 128:(oc + 1) * 128],
                                                 latT[:, c, cols],
                                                 start=(c == 0), stop=(c == 3),
                                                 skip_group_check=True)
                            p.barrier()
                            nc.vector.tensor_copy(out=kT[:, oc, cols], in_=ps[:, tt % 4])
                            p.barrier()
                with nc.psum_tensor(_nm("vps"), [128, 512], F32) as ps:
                    for tk in range(NT):
                        cols = slice(tk * 128, (tk + 1) * 128)
                        for c in range(4):
                            nc.tensor.matmul(ps[:, :], latT[:, c, cols], wvb[:, c, :],
                                             start=(c == 0), stop=(c == 3),
                                             skip_group_check=True)
                        p.barrier()
                        nc.vector.tensor_copy(
                            out=vh[:, tk, :, 0:64],
                            in_=ps.ap().rearrange("p (a b) -> p a b", a=8))
                        p.barrier()
        with (
            nc.sbuf_tensor(N("cosb"), [128, T], BF16) as cosb,
            nc.sbuf_tensor(N("sinb"), [128, T], BF16) as sinb,
            nc.sbuf_tensor(N("stg3"), [128, T], F32) as stg3,
            nc.sbuf_tensor(N("rtmp"), [64, 4, 2048], BF16) as rtmp,
        ):
            p.dma(nc.sync, stg3[:, :], rc2[:, :])
            p.barrier()
            nc.vector.tensor_copy(out=cosb[:, :], in_=stg3[:, :])
            p.barrier()
            p.dma(nc.sync, stg3[:, :], rs2[:, :])
            p.barrier()
            nc.vector.tensor_copy(out=sinb[:, :], in_=stg3[:, :])
            p.barrier()
            for buf in (qT, kT):
                for oc in range(4):
                    for th in range(2):
                        cols = slice(th * 2048, (th + 1) * 2048)
                        for hl in range(2):
                            E = buf[64 * hl:64 * hl + 32, oc, cols]
                            O = buf[64 * hl + 32:64 * hl + 64, oc, cols]
                            cE = cosb[64 * hl:64 * hl + 32, cols]
                            cO = cosb[64 * hl + 32:64 * hl + 64, cols]
                            sE = sinb[64 * hl:64 * hl + 32, cols]
                            sO = sinb[64 * hl + 32:64 * hl + 64, cols]
                            nc.vector.tensor_mul(rtmp[0:32, 0], E, cE)
                            nc.vector.tensor_mul(rtmp[0:32, 1], O, sO)
                            nc.vector.tensor_mul(rtmp[0:32, 2], E, sE)
                            nc.vector.tensor_mul(rtmp[0:32, 3], O, cO)
                            p.barrier()
                            nc.vector.tensor_sub(E, rtmp[0:32, 0], rtmp[0:32, 1])
                            nc.vector.tensor_add(O, rtmp[0:32, 2], rtmp[0:32, 3])
                            p.barrier()
        with (
            nc.sbuf_tensor(N("sq4"), [128, T], BF16) as sq4,
            nc.sbuf_tensor(N("srow"), [1, T], F32) as srow,
            nc.sbuf_tensor(N("sml"), [1, 8], F32) as sml,
            nc.sbuf_tensor(N("smlb"), [1, 1], BF16) as smlb,
        ):
          with nc.psum_tensor(_nm("nps"), [1, T], F32) as nps:
              for idx, buf in ((0, qT), (1, kT)):
                  for oc in range(4):
                      nc.scalar.activation(out=sq4[:, :], in_=buf[:, oc], func=AF.Square)
                      p.barrier()
                      for j in range(T // 512):
                          cols = slice(j * 512, (j + 1) * 512)
                          nc.tensor.matmul(nps[0:1, cols], onek[:, 0:1],
                                           sq4[:, cols], start=(oc == 0), stop=(oc == 3),
                                           skip_group_check=True)
                      p.barrier()
                  nc.vector.reduce_max(sml[0:1, idx:idx + 1], nps[0:1, :],
                                       axis=mybir.AxisListType.X)
                  p.barrier()
              nc.scalar.activation(out=sml[0:1, 2:4], in_=sml[0:1, 0:2], func=AF.Sqrt)
              p.barrier()
              nc.vector.tensor_mul(sml[0:1, 4:5], sml[0:1, 2:3], sml[0:1, 3:4])
              p.barrier()
              nc.vector.tensor_scalar(out=sml[0:1, 5:6], in0=sml[0:1, 4:5],
                                      scalar1=-0.125, scalar2=60.0,
                                      op0=ALU.mult, op1=ALU.add)
              p.barrier()
          nc.vector.tensor_copy(out=smlb[0:1, :], in_=sml[0:1, 5:6])
          p.barrier()
          with nc.psum_tensor(_nm("nbp"), [128, 1], F32) as nbp:
              nc.tensor.matmul(nbp[:, :], onec[0:1, :], smlb[0:1, :],
                               start=True, stop=True)
              p.barrier()
              nc.vector.tensor_copy(out=nb[:, :], in_=nbp[:, :])
              p.barrier()

        sem_pe = nc.alloc_semaphore(N("att_pe"))
        sem_act = nc.alloc_semaphore(N("att_act"))
        cnt_pe = 0
        cnt_act = 0
        with (
            nc.sbuf_tensor(N("ebuf"), [128, 2, 32, 512], BF16) as ebuf,
            nc.sbuf_tensor(N("r512"), [1, 512], F32) as r512,
            nc.sbuf_tensor(N("pvs"), [64, 512], F32) as pvs,
            nc.sbuf_tensor(N("one64f"), [1, 64], F32) as one64f,
        ):
            nc.vector.memset(one64f[:, :], 1.0)
            p.barrier()
            for g in range(8):
                ntf = 4 * g + 2
                gc = slice(g * 512, (g + 1) * 512)
                for pair in range(4):
                    with (
                        nc.psum_tensor(_nm("sA"), [128, 2, 512], F32) as sA,
                        nc.psum_tensor(_nm("sB"), [128, 2, 512], F32) as sB,
                    ):
                        for hloc in range(2):
                            for tkb in (ntf, ntf + 1):
                                nc.vector.memset(ebuf[:, hloc, tkb, 0:256], 0.0)
                        p.barrier()
                        for tk in range(ntf + 2):
                            par = tk % 2
                            tkc = slice(tk * 128, (tk + 1) * 128)
                            bnd = tk >= ntf
                            if tk >= 2:
                                nc.tensor.wait_ge(sem_act, cnt_act + tk - 1)
                            nc.tensor.matmul(sA[:, par], kT[0:64, pair, tkc],
                                             qT[0:64, pair, gc], start=True, stop=True,
                                             skip_group_check=True)
                            nc.tensor.matmul(sB[:, par], kT[64:128, pair, tkc],
                                             qT[64:128, pair, gc], start=True, stop=True,
                                             skip_group_check=True,
                                             tile_position=(64, 0)).then_inc(sem_pe, 1)
                            if True:
                                nc.scalar.wait_ge(sem_pe, cnt_pe + tk + 1)
                            ecols = slice(256, 512) if bnd else slice(0, 512)
                            nc.scalar.activation(out=ebuf[:, 0, tk, ecols],
                                                 in_=sA[:, par, ecols], func=AF.Exp,
                                                 bias=nb[:, 0:1], scale=0.125)
                            nc.scalar.activation(out=ebuf[:, 1, tk, ecols],
                                                 in_=sB[:, par, ecols], func=AF.Exp,
                                                 bias=nb[:, 0:1],
                                                 scale=0.125).then_inc(sem_act, 1)
                        cnt_pe += ntf + 2
                        cnt_act += ntf + 2
                        p.barrier()
                    for hloc in range(2):
                        h = pair * 2 + hloc
                        with (
                            nc.psum_tensor(_nm("pv"), [128, 512], F32) as pv,
                            nc.psum_tensor(_nm("rbc"), [64, 512], F32) as rbc,
                        ):
                            for tk in range(ntf + 2):
                                nc.tensor.matmul(pv[0:65, :], vh[:, tk, h],
                                                 ebuf[:, hloc, tk], start=(tk == 0),
                                                 stop=(tk == ntf + 1),
                                                 skip_group_check=True)
                            p.barrier()
                            nc.vector.reciprocal(r512[0:1, :], pv[64:65, :])
                            p.barrier()
                            nc.tensor.matmul(rbc[:, :], one64f[0:1, :], r512[0:1, :],
                                             start=True, stop=True)
                            nc.vector.tensor_copy(out=pvs[:, :], in_=pv[0:64, :])
                            p.barrier()
                            nc.vector.tensor_mul(
                                OT[64 * hloc:64 * (hloc + 1), pair, gc],
                                pvs[:, :], rbc[:, :])
                            p.barrier()
        with (
            nc.sbuf_tensor(N("wob"), [128, 4, D], BF16) as wob,
            nc.sbuf_tensor(N("stg6"), [128, 2048], F32) as stg6,
            nc.sbuf_tensor(N("yst"), [128, 512], F32) as yst,
        ):
            for c2 in range(2):
                p.dma(nc.sync,
                      stg6.ap().rearrange("p (a b) -> p a b", a=2),
                      wop[c2 * 2:(c2 + 1) * 2].rearrange("a p b -> p a b"))
                p.barrier()
                nc.vector.tensor_copy(
                    out=wob[:, c2 * 2:(c2 + 1) * 2],
                    in_=stg6.ap().rearrange("p (a b) -> p a b", a=2))
                p.barrier()
            with nc.psum_tensor(_nm("yps"), [128, 512], F32) as ps:
                for dc in range(8):
                    for tt in range(T // 512):
                        cols = slice(tt * 512, (tt + 1) * 512)
                        for oc in range(4):
                            nc.tensor.matmul(ps[:, :],
                                             wob[:, oc, dc * 128:(dc + 1) * 128],
                                             OT[:, oc, cols], start=(oc == 0),
                                             stop=(oc == 3), skip_group_check=True)
                        p.barrier()
                        nc.vector.tensor_copy(out=yst[:, :], in_=ps[:, :])
                        p.barrier()
                        p.dma(nc.sync, yp[dc, :, cols], yst[:, :])
                        p.barrier()


def emit_comb(nc, p, pfx, xin, ya_chunks, yb_chunks, xout_ap):
    import contextlib
    N = lambda s: pfx + s
    es = contextlib.ExitStack()
    with es:
        ysum = es.enter_context(nc.sbuf_tensor(N("ysum"), [128, 8, S], BF16))
        stga = es.enter_context(nc.sbuf_tensor(N("stga"), [128, S], F32))
        stgb = es.enter_context(nc.sbuf_tensor(N("stgb"), [128, S], F32))
        sq = es.enter_context(nc.sbuf_tensor(N("sq"), [128, S], BF16))
        rowr = es.enter_context(nc.sbuf_tensor(N("rowr"), [1, S], F32))
        rowmr = es.enter_context(nc.sbuf_tensor(N("rowmr"), [1, S], F32))
        rowbf = es.enter_context(nc.sbuf_tensor(N("rowbf"), [1, S], BF16))
        for c in range(8):
            p.dma(nc.sync, stga[:, :], ya_chunks(c))
            p.dma(nc.sync, stgb[:, :], yb_chunks(c))
            p.barrier()
            nc.vector.tensor_add(ysum[:, c], stga[:, :], stgb[:, :])
            p.barrier()
        emit_ln_fm(p, None, sq, rowr[0:1], rowmr[0:1], S,
                   x_src_chunks=[ysum[:, c] for c in range(8)])
        with (
            nc.psum_tensor(N("rps"), [128, S // 2], F32) as rps,
            nc.psum_tensor(N("mrps"), [128, S // 2], F32) as mrps,
        ):
            for half in range(2):
                cols = slice(half * (S // 2), (half + 1) * (S // 2))
                emit_bcast_row(p, rowr[0:1, cols], rps, S // 2, rowbf)
                emit_bcast_row(p, rowmr[0:1, cols], mrps, S // 2, rowbf)
                for c in range(8):
                    p.dma(nc.sync, stga[:, 0:S // 2],
                          xin[c].rearrange("p (h t) -> p h t", h=2)[:, half])
                    p.barrier()
                    nc.vector.tensor_mul(stgb[:, 0:S // 2], ysum[:, c, cols], rps[:, :])
                    p.barrier()
                    nc.vector.tensor_sub(stgb[:, 0:S // 2], stgb[:, 0:S // 2], mrps[:, :])
                    p.barrier()
                    nc.vector.tensor_add(stgb[:, 0:S // 2], stgb[:, 0:S // 2],
                                         stga[:, 0:S // 2])
                    p.barrier()
                    p.dma(nc.sync,
                          xout_ap[c].rearrange("p (h t) -> p h t", h=2)[:, half],
                          stgb[:, 0:S // 2])
                    p.barrier()


def emit_wcast(nc, p, pfx, w1, w2, w1o, w2o):
    N = lambda s: pfx + s
    with (
        nc.sbuf_tensor(N("st"), [128, 4096], F32) as st,
        nc.sbuf_tensor(N("bt"), [128, 4096], BF16) as bt,
    ):
        for c in range(8):
            p.dma(nc.sync, st[:, :], w1[c])
            p.barrier()
            nc.vector.tensor_copy(out=bt[:, :], in_=st[:, :])
            p.barrier()
            p.dma(nc.sync, w1o[c], bt[:, :])
            p.barrier()
        for c in range(32):
            p.dma(nc.sync, st[:, 0:1024], w2[c])
            p.barrier()
            nc.vector.tensor_copy(out=bt[:, 0:1024], in_=st[:, 0:1024])
            p.barrier()
            p.dma(nc.sync, w2o[c], bt[:, 0:1024])
            p.barrier()


def emit_l0fattn(nc, p, pfx, xin, wqs, wks, wvs, bq, bk, bv, l1g, l1b,
                 wop, yp):
    import contextlib
    N = lambda s: pfx + s
    NT = T // 128
    es = contextlib.ExitStack()
    with es:
        qT = es.enter_context(nc.sbuf_tensor(N("qT"), [128, 4, T], BF16))
        kT = es.enter_context(nc.sbuf_tensor(N("kT"), [128, 4, T], BF16))
        vh = es.enter_context(nc.sbuf_tensor(N("vh"), [128, NT, 8, 65], BF16))
        OT = es.enter_context(nc.sbuf_tensor(N("OT"), [128, 4, T], BF16))
        nb = es.enter_context(nc.sbuf_tensor(N("nb"), [128, 1], BF16))
        onec = es.enter_context(nc.sbuf_tensor(N("onec"), [1, 128], BF16))
        onek = es.enter_context(nc.sbuf_tensor(N("onek"), [128, 1], BF16))
        gbg = es.enter_context(nc.sbuf_tensor(N("gbg"), [128, 8], F32))
        gbb = es.enter_context(nc.sbuf_tensor(N("gbb"), [128, 8], F32))
        bqr = es.enter_context(nc.sbuf_tensor(N("bqr"), [128, 4], F32))
        bkr = es.enter_context(nc.sbuf_tensor(N("bkr"), [128, 4], F32))
        bvr = es.enter_context(nc.sbuf_tensor(N("bvr"), [1, 512], BF16))
        nc.vector.memset(onec[:, :], 1.0)
        nc.vector.memset(onek[:, :], 1.0)
        nc.vector.memset(vh.ap().rearrange("p a b c -> p (a b c)")[:, :], 0.0)
        nc.vector.memset(
            vh.ap().rearrange("p a b c -> p (a b) c", c=65)[:, :, 64:65], 1.0)
        p.barrier()
        p.dma(nc.sync, gbg[:, :], l1g[:, :])
        p.dma(nc.sync, gbb[:, :], l1b[:, :])
        p.dma(nc.sync, bqr[:, :], bq[:, :])
        p.dma(nc.sync, bkr[:, :], bk[:, :])
        p.barrier()
        with nc.sbuf_tensor(N("bvf"), [1, 512], F32) as bvf:
            p.dma(nc.sync, bvf[0:1, :], bv[:, :])
            p.barrier()
            nc.vector.tensor_copy(out=bvr[0:1, :], in_=bvf[0:1, :])
            p.barrier()

        for th2 in range(2):
          with (
            nc.sbuf_tensor(_nm("h1"), [128, 8, 2048], BF16) as h1,
            nc.sbuf_tensor(_nm("wstg"), [128, 8, 512], BF16) as wstg,
            nc.sbuf_tensor(_nm("stgf"), [128, 2048], F32) as stgf,
              nc.sbuf_tensor(_nm("sqb"), [128, 512], BF16) as sqb,
              nc.sbuf_tensor(_nm("rowr"), [1, 512], F32) as rowr,
              nc.sbuf_tensor(_nm("rowmr"), [1, 512], F32) as rowmr,
              nc.sbuf_tensor(_nm("rowbf"), [1, 512], BF16) as rowbf,
          ):
              for tq8 in range(th2 * 4, th2 * 4 + 4):
                  tcols = slice(tq8 * 512, (tq8 + 1) * 512)
                  with nc.sbuf_tensor(_nm("xpq"), [128, 8, 512], BF16) as xp:
                      for c in range(8):
                          p.dma(nc.sync, stgf[:, 0:512], xin[c, :, tcols])
                          p.barrier()
                          nc.vector.tensor_copy(out=xp[:, c], in_=stgf[:, 0:512])
                          p.barrier()
                      for rnd in range(2):
                          emit_ln_fm(p, None, sqb, rowr[0:1], rowmr[0:1], 512,
                                     x_src_chunks=[xp[:, c] for c in range(8)])
                          with (
                              nc.psum_tensor(_nm("rps"), [128, 512], F32) as rps,
                              nc.psum_tensor(_nm("mrps"), [128, 512], F32) as mrps,
                          ):
                              emit_bcast_row(p, rowr[0:1, 0:512], rps, 512, rowbf)
                              emit_bcast_row(p, rowmr[0:1, 0:512], mrps, 512, rowbf)
                              for c in range(8):
                                  nc.vector.tensor_mul(xp[:, c], xp[:, c], rps[:, :])
                              p.barrier()
                              for c in range(8):
                                  nc.vector.tensor_sub(xp[:, c], xp[:, c], mrps[:, :])
                              p.barrier()
                              if rnd == 1:
                                  for c in range(8):
                                      nc.vector.tensor_scalar(
                                          out=h1[:, c, tcols.start - th2 * 2048:tcols.stop - th2 * 2048], in0=xp[:, c],
                                          scalar1=gbg[:, c:c + 1], scalar2=gbb[:, c:c + 1],
                                          op0=ALU.mult, op1=ALU.add)
                                  p.barrier()
              for wsrc, dst, bcol in ((wqs, qT, 0), (wks, kT, 1)):
                  for c2 in range(2):
                      p.dma(nc.sync,
                            stgf.ap().rearrange("p (a b) -> p a b", a=4)[:, :, 0:512],
                            wsrc[c2 * 4:(c2 + 1) * 4].rearrange("a p b -> p a b"))
                      p.barrier()
                      nc.vector.tensor_copy(
                          out=wstg[:, c2 * 4:(c2 + 1) * 4],
                          in_=stgf.ap().rearrange("p (a b) -> p a b", a=4)[:, :, 0:512])
                      p.barrier()
                  with nc.psum_tensor(_nm("qps"), [128, 512], F32) as ps:
                      for oc in range(4):
                          for tt in range(th2 * 4, th2 * 4 + 4):
                              cols = slice(tt * 512, (tt + 1) * 512)
                              for c in range(8):
                                  nc.tensor.matmul(ps[:, :],
                                                   wstg[:, c, oc * 128:(oc + 1) * 128],
                                                   h1[:, c, cols.start - th2 * 2048:cols.stop - th2 * 2048],
                                                   start=(c == 0), stop=(c == 7),
                                                   skip_group_check=True)
                              p.barrier()
                              nc.vector.tensor_scalar_add(
                                  dst[:, oc, cols], ps[:, :],
                                  (bqr if bcol == 0 else bkr)[:, oc:oc + 1])
                              p.barrier()
              for c2 in range(2):
                  p.dma(nc.sync,
                        stgf.ap().rearrange("p (a b) -> p a b", a=4)[:, :, 0:512],
                        wvs[c2 * 4:(c2 + 1) * 4].rearrange("a p b -> p a b"))
                  p.barrier()
                  nc.vector.tensor_copy(
                      out=wstg[:, c2 * 4:(c2 + 1) * 4],
                      in_=stgf.ap().rearrange("p (a b) -> p a b", a=4)[:, :, 0:512])
                  p.barrier()
              with nc.psum_tensor(_nm("vps"), [128, 512], F32) as ps:
                  for tk in range(th2 * 16, th2 * 16 + 16):
                      cols = slice(tk * 128 - th2 * 2048, (tk + 1) * 128 - th2 * 2048)
                      for c in range(8):
                          nc.tensor.matmul(ps[:, :], h1[:, c, cols], wstg[:, c, :],
                                           start=(c == 0), stop=False,
                                           skip_group_check=True)
                      nc.tensor.matmul(ps[:, :], onec[0:1, :], bvr[0:1, :],
                                       start=False, stop=True, skip_group_check=True)
                      p.barrier()
                      nc.vector.tensor_copy(
                          out=vh[:, tk, :, 0:64],
                          in_=ps.ap().rearrange("p (a b) -> p a b", a=8))
                      p.barrier()
        with (
            nc.sbuf_tensor(N("sq4"), [128, T], BF16) as sq4,
            nc.sbuf_tensor(N("sml"), [1, 8], F32) as sml,
            nc.sbuf_tensor(N("smlb"), [1, 1], BF16) as smlb,
        ):
            with nc.psum_tensor(_nm("nps"), [1, T], F32) as nps:
                for idx, buf in ((0, qT), (1, kT)):
                    for oc in range(4):
                        nc.scalar.activation(out=sq4[:, :], in_=buf[:, oc], func=AF.Square)
                        p.barrier()
                        for j in range(T // 512):
                            cols = slice(j * 512, (j + 1) * 512)
                            nc.tensor.matmul(nps[0:1, cols], onek[:, 0:1],
                                             sq4[:, cols], start=(oc == 0), stop=(oc == 3),
                                             skip_group_check=True)
                        p.barrier()
                    nc.vector.reduce_max(sml[0:1, idx:idx + 1], nps[0:1, :],
                                         axis=mybir.AxisListType.X)
                    p.barrier()
                nc.scalar.activation(out=sml[0:1, 2:4], in_=sml[0:1, 0:2], func=AF.Sqrt)
                p.barrier()
                nc.vector.tensor_mul(sml[0:1, 4:5], sml[0:1, 2:3], sml[0:1, 3:4])
                p.barrier()
                nc.vector.tensor_scalar(out=sml[0:1, 5:6], in0=sml[0:1, 4:5],
                                        scalar1=-0.125, scalar2=60.0,
                                        op0=ALU.mult, op1=ALU.add)
                p.barrier()
            nc.vector.tensor_copy(out=smlb[0:1, :], in_=sml[0:1, 5:6])
            p.barrier()
            with nc.psum_tensor(_nm("nbp"), [128, 1], F32) as nbp:
                nc.tensor.matmul(nbp[:, :], onec[0:1, :], smlb[0:1, :],
                                 start=True, stop=True)
                p.barrier()
                nc.vector.tensor_copy(out=nb[:, :], in_=nbp[:, :])
                p.barrier()
        sem_pe = nc.alloc_semaphore(N("fat_pe"))
        sem_act = nc.alloc_semaphore(N("fat_act"))
        cnt_pe = 0
        cnt_act = 0
        with (
            nc.sbuf_tensor(N("ebuf"), [128, 2, NT, 512], BF16) as ebuf,
            nc.sbuf_tensor(N("r512"), [1, 512], F32) as r512,
            nc.sbuf_tensor(N("pvs"), [64, 512], F32) as pvs,
            nc.sbuf_tensor(N("one64f"), [1, 64], F32) as one64f,
        ):
            nc.vector.memset(one64f[:, :], 1.0)
            p.barrier()
            for g in range(8):
                gc = slice(g * 512, (g + 1) * 512)
                for pair in range(4):
                    with (
                        nc.psum_tensor(_nm("sA"), [128, 2, 512], F32) as sA,
                        nc.psum_tensor(_nm("sB"), [128, 2, 512], F32) as sB,
                    ):
                        for tk in range(NT):
                            par = tk % 2
                            tkc = slice(tk * 128, (tk + 1) * 128)
                            if tk >= 2:
                                nc.tensor.wait_ge(sem_act, cnt_act + tk - 1)
                            nc.tensor.matmul(sA[:, par], kT[0:64, pair, tkc],
                                             qT[0:64, pair, gc], start=True, stop=True,
                                             skip_group_check=True)
                            nc.tensor.matmul(sB[:, par], kT[64:128, pair, tkc],
                                             qT[64:128, pair, gc], start=True, stop=True,
                                             skip_group_check=True,
                                             tile_position=(64, 0)).then_inc(sem_pe, 1)
                            nc.scalar.wait_ge(sem_pe, cnt_pe + tk + 1)
                            nc.scalar.activation(out=ebuf[:, 0, tk], in_=sA[:, par],
                                                 func=AF.Exp, bias=nb[:, 0:1], scale=0.125)
                            nc.scalar.activation(out=ebuf[:, 1, tk], in_=sB[:, par],
                                                 func=AF.Exp, bias=nb[:, 0:1],
                                                 scale=0.125).then_inc(sem_act, 1)
                        cnt_pe += NT
                        cnt_act += NT
                        p.barrier()
                    for hloc in range(2):
                        h = pair * 2 + hloc
                        with (
                            nc.psum_tensor(_nm("pv"), [128, 512], F32) as pv,
                            nc.psum_tensor(_nm("rbc"), [64, 512], F32) as rbc,
                        ):
                            for tk in range(NT):
                                nc.tensor.matmul(pv[0:65, :], vh[:, tk, h],
                                                 ebuf[:, hloc, tk], start=(tk == 0),
                                                 stop=(tk == NT - 1),
                                                 skip_group_check=True)
                            p.barrier()
                            nc.vector.reciprocal(r512[0:1, :], pv[64:65, :])
                            p.barrier()
                            nc.tensor.matmul(rbc[:, :], one64f[0:1, :], r512[0:1, :],
                                             start=True, stop=True)
                            nc.vector.tensor_copy(out=pvs[:, :], in_=pv[0:64, :])
                            p.barrier()
                            nc.vector.tensor_mul(
                                OT[64 * hloc:64 * (hloc + 1), pair, gc],
                                pvs[:, :], rbc[:, :])
                            p.barrier()
        with (
            nc.sbuf_tensor(N("wob"), [128, 4, D], BF16) as wob,
            nc.sbuf_tensor(N("stg6"), [128, 2048], F32) as stg6,
            nc.sbuf_tensor(N("yst"), [128, 512], F32) as yst,
        ):
            for c2 in range(2):
                p.dma(nc.sync,
                      stg6.ap().rearrange("p (a b) -> p a b", a=2),
                      wop[c2 * 2:(c2 + 1) * 2].rearrange("a p b -> p a b"))
                p.barrier()
                nc.vector.tensor_copy(
                    out=wob[:, c2 * 2:(c2 + 1) * 2],
                    in_=stg6.ap().rearrange("p (a b) -> p a b", a=2))
                p.barrier()
            with nc.psum_tensor(_nm("yps"), [128, 512], F32) as ps:
                for dc in range(8):
                    for tt in range(T // 512):
                        cols = slice(tt * 512, (tt + 1) * 512)
                        for oc in range(4):
                            nc.tensor.matmul(ps[:, :],
                                             wob[:, oc, dc * 128:(dc + 1) * 128],
                                             OT[:, oc, cols], start=(oc == 0),
                                             stop=(oc == 3), skip_group_check=True)
                        p.barrier()
                        nc.vector.tensor_copy(out=yst[:, :], in_=ps[:, :])
                        p.barrier()
                        p.dma(nc.sync, yp[dc, :, cols], yst[:, :])
                        p.barrier()


def emit_l0fmoe(nc, p, pfx, xin, ya_chunks, yb_chunks, bo, l2g, l2b, gw, gbr,
                w1b, w2b, b1c, b2r, ident_in, xout_ap):
    import contextlib
    N = lambda s: pfx + s
    NE, HID, NH = 8, 4096, 32
    es = contextlib.ExitStack()
    with es:
        xa1 = es.enter_context(nc.sbuf_tensor(N("xa1"), [128, 8, S], BF16))
        h2 = es.enter_context(nc.sbuf_tensor(N("h2"), [128, 8, S], BF16))
        ybuf = es.enter_context(nc.sbuf_tensor(N("ybuf"), [128, 8, S], BF16))
        ew = es.enter_context(nc.sbuf_tensor(N("ew"), [128, 16, 8], F32))
        bor = es.enter_context(nc.sbuf_tensor(N("bor"), [128, 8], F32))
        g2r = es.enter_context(nc.sbuf_tensor(N("g2r"), [128, 8], F32))
        b2rr = es.enter_context(nc.sbuf_tensor(N("b2rr"), [128, 8], F32))
        onec = es.enter_context(nc.sbuf_tensor(N("onec"), [1, 128], BF16))
        idb = es.enter_context(nc.sbuf_tensor(N("idb"), [128, 128], BF16))
        stgf = es.enter_context(nc.sbuf_tensor(N("stgf"), [128, 1024], F32))
        stgg = es.enter_context(nc.sbuf_tensor(N("stgg"), [128, 512], F32))
        sqb = es.enter_context(nc.sbuf_tensor(N("sqb"), [128, 512], BF16))
        rowr = es.enter_context(nc.sbuf_tensor(N("rowr"), [1, 512], F32))
        rowmr = es.enter_context(nc.sbuf_tensor(N("rowmr"), [1, 512], F32))
        rowbf = es.enter_context(nc.sbuf_tensor(N("rowbf"), [1, 512], BF16))
        nc.vector.memset(onec[:, :], 1.0)
        p.dma(nc.sync, bor[:, :], bo[:, :])
        p.dma(nc.sync, g2r[:, :], l2g[:, :])
        p.dma(nc.sync, b2rr[:, :], l2b[:, :])
        p.dma(nc.sync, stgf[:, 0:128], ident_in[:, :])
        p.barrier()
        nc.vector.tensor_copy(out=idb[:, :], in_=stgf[:, 0:128])
        p.barrier()

        for tq in range(S // 512):
            tcols = slice(tq * 512, (tq + 1) * 512)
            with nc.sbuf_tensor(_nm("xq"), [128, 8, 512], BF16) as xq:
                for c in range(8):
                    p.dma(nc.sync, stgg[:, :], xin[c, :, tcols])
                    p.barrier()
                    nc.vector.tensor_copy(out=xq[:, c], in_=stgg[:, :])
                    p.barrier()
                emit_ln_fm(p, None, sqb, rowr[0:1], rowmr[0:1], 512,
                           x_src_chunks=[xq[:, c] for c in range(8)])
                with (
                    nc.psum_tensor(_nm("rps"), [128, 512], F32) as rps,
                    nc.psum_tensor(_nm("mrps"), [128, 512], F32) as mrps,
                ):
                    emit_bcast_row(p, rowr[0:1, 0:512], rps, 512, rowbf)
                    emit_bcast_row(p, rowmr[0:1, 0:512], mrps, 512, rowbf)
                    for c in range(8):
                        nc.vector.tensor_mul(xq[:, c], xq[:, c], rps[:, :])
                    p.barrier()
                    for c in range(8):
                        nc.vector.tensor_sub(xq[:, c], xq[:, c], mrps[:, :])
                    p.barrier()
                for c in range(8):
                    p.dma(nc.sync, stgg[:, :], ya_chunks(c)[:, tcols])
                    p.dma(nc.sync, stgf[:, 0:512], yb_chunks(c)[:, tcols])
                    p.barrier()
                    nc.vector.tensor_add(stgg[:, :], stgg[:, :], stgf[:, 0:512])
                    p.barrier()
                    nc.vector.tensor_scalar_add(stgg[:, :], stgg[:, :], bor[:, c:c + 1])
                    p.barrier()
                    nc.vector.tensor_add(xa1[:, c, tcols], xq[:, c], stgg[:, :])
                    p.barrier()
            with nc.sbuf_tensor(_nm("hq"), [128, 8, 512], BF16) as hq:
                for c in range(8):
                    nc.vector.tensor_copy(out=hq[:, c], in_=xa1[:, c, tcols])
                p.barrier()
                emit_ln_fm(p, None, sqb, rowr[0:1], rowmr[0:1], 512,
                           x_src_chunks=[hq[:, c] for c in range(8)])
                with (
                    nc.psum_tensor(_nm("rps"), [128, 512], F32) as rps,
                    nc.psum_tensor(_nm("mrps"), [128, 512], F32) as mrps,
                ):
                    emit_bcast_row(p, rowr[0:1, 0:512], rps, 512, rowbf)
                    emit_bcast_row(p, rowmr[0:1, 0:512], mrps, 512, rowbf)
                    for c in range(8):
                        nc.vector.tensor_mul(hq[:, c], hq[:, c], rps[:, :])
                    p.barrier()
                    for c in range(8):
                        nc.vector.tensor_sub(hq[:, c], hq[:, c], mrps[:, :])
                    p.barrier()
                    for c in range(8):
                        nc.vector.tensor_scalar(out=h2[:, c, tcols], in0=hq[:, c],
                                                scalar1=g2r[:, c:c + 1],
                                                scalar2=b2rr[:, c:c + 1],
                                                op0=ALU.mult, op1=ALU.add)
                    p.barrier()

        with (
            nc.sbuf_tensor(N("gwb"), [128, 8, 8], BF16) as gwb,
            nc.sbuf_tensor(N("gbb"), [1, 8], BF16) as gbbv,
            nc.sbuf_tensor(N("lgt"), [128, 8], F32) as lgt,
            nc.sbuf_tensor(N("m1"), [128, 1], F32) as m1,
            nc.sbuf_tensor(N("m2"), [128, 1], F32) as m2,
            nc.sbuf_tensor(N("eq1"), [128, 8], F32) as eq1,
            nc.sbuf_tensor(N("eq2"), [128, 8], F32) as eq2,
            nc.sbuf_tensor(N("gw1"), [128, 1], F32) as gw1,
            nc.sbuf_tensor(N("gw2"), [128, 1], F32) as gw2,
        ):
            for c in range(8):
                p.dma(nc.sync, stgf[:, 0:8], gw[c])
                p.barrier()
                nc.vector.tensor_copy(out=gwb[:, c], in_=stgf[:, 0:8])
                p.barrier()
            p.dma(nc.sync, stgf[0:1, 0:8], gbr[:, :])
            p.barrier()
            nc.vector.tensor_copy(out=gbbv[0:1, :], in_=stgf[0:1, 0:8])
            p.barrier()
            with nc.psum_tensor(_nm("lps"), [128, 8], F32) as lps:
                for tt in range(16):
                    ts_ = slice(tt * 128, (tt + 1) * 128)
                    for c in range(8):
                        nc.tensor.matmul(lps[:, :], h2[:, c, ts_], gwb[:, c],
                                         start=(c == 0), stop=False,
                                         skip_group_check=True)
                    nc.tensor.matmul(lps[:, :], onec[0:1, :], gbbv[0:1, :],
                                     start=False, stop=True, skip_group_check=True)
                    p.barrier()
                    nc.vector.tensor_copy(out=lgt[:, :], in_=lps[:, :])
                    p.barrier()
                    nc.vector.reduce_max(m1[:, 0:1], lgt[:, :], axis=mybir.AxisListType.X)
                    p.barrier()
                    nc.vector.tensor_scalar(out=eq1[:, :], in0=lgt[:, :],
                                            scalar1=m1[:, 0:1], scalar2=None,
                                            op0=ALU.is_equal)
                    p.barrier()
                    nc.vector.scalar_tensor_tensor(out=eq2[:, :], in0=eq1[:, :],
                                                   scalar=-1e30, in1=lgt[:, :],
                                                   op0=ALU.mult, op1=ALU.add)
                    p.barrier()
                    nc.vector.reduce_max(m2[:, 0:1], eq2[:, :], axis=mybir.AxisListType.X)
                    p.barrier()
                    nc.vector.tensor_sub(gw1[:, 0:1], m1[:, 0:1], m2[:, 0:1])
                    p.barrier()
                    nc.scalar.activation(out=gw1[:, 0:1], in_=gw1[:, 0:1],
                                         func=AF.Sigmoid)
                    p.barrier()
                    nc.vector.tensor_scalar(out=gw2[:, 0:1], in0=gw1[:, 0:1],
                                            scalar1=-1.0, scalar2=1.0,
                                            op0=ALU.mult, op1=ALU.add)
                    p.barrier()
                    nc.vector.tensor_scalar(out=eq2[:, :], in0=eq2[:, :],
                                            scalar1=m2[:, 0:1], scalar2=None,
                                            op0=ALU.is_equal)
                    p.barrier()
                    nc.vector.tensor_scalar_mul(eq1[:, :], eq1[:, :], gw1[:, 0:1])
                    nc.vector.tensor_scalar_mul(eq2[:, :], eq2[:, :], gw2[:, 0:1])
                    p.barrier()
                    nc.vector.tensor_add(ew[:, tt, :], eq1[:, :], eq2[:, :])
                    p.barrier()

        with (
            nc.sbuf_tensor(N("macc"), [128, 4, D], F32) as macc,
            nc.sbuf_tensor(N("hidb"), [128, NH, 512], BF16) as hidb,
            nc.sbuf_tensor(N("w1c"), [128, 8, 512], BF16) as w1c,
            nc.sbuf_tensor(N("w2c"), [128, 4, D], BF16) as w2c,
            nc.sbuf_tensor(N("b1s"), [128, NH], F32) as b1s,
            nc.sbuf_tensor(N("b2s"), [1, D], BF16) as b2s,
            nc.sbuf_tensor(N("mtb"), [128, D], BF16) as mtb,
        ):
            for sup in range(S // 512):
                scols = slice(sup * 512, (sup + 1) * 512)
                nc.vector.memset(macc.ap().rearrange("p a b -> p (a b)")[:, :], 0.0)
                p.barrier()
                for e in range(NE):
                    p.dma(nc.sync, b1s[:, :], b1c[e])
                    p.dma(nc.sync, stgf[0:1, 0:1024], b2r[e])
                    p.barrier()
                    nc.vector.tensor_copy(out=b2s[0:1, :], in_=stgf[0:1, 0:1024])
                    p.barrier()
                    with nc.psum_tensor(_nm("hps"), [128, 512], F32) as hps:
                        for hg in range(NH // 4):
                            p.dma(nc.sync, w1c[:, :, :],
                                  w1b[e, :, :, hg * 512:(hg + 1) * 512]
                                  .rearrange("a p b -> p a b"))
                            p.barrier()
                            for hl in range(4):
                                hc = hg * 4 + hl
                                for c in range(8):
                                    nc.tensor.matmul(hps[:, :],
                                                     w1c[:, c, hl * 128:(hl + 1) * 128],
                                                     h2[:, c, scols],
                                                     start=(c == 0), stop=(c == 7),
                                                     skip_group_check=True)
                                p.barrier()
                                nc.scalar.activation(out=hidb[:, hc], in_=hps[:, :],
                                                     func=AF.Relu, bias=b1s[:, hc:hc + 1])
                                p.barrier()
                    with nc.psum_tensor(_nm("ops"), [128, 4, 2, 512], F32) as ops:
                        for hg in range(NH // 4):
                            p.dma(nc.sync, w2c[:, :, :], w2b[e, hg * 4:(hg + 1) * 4]
                                  .rearrange("a p b -> p a b"))
                            p.barrier()
                            for hl in range(4):
                                hc = hg * 4 + hl
                                for tl in range(4):
                                    lcols = slice(tl * 128, (tl + 1) * 128)
                                    for oc in range(2):
                                        nc.tensor.matmul(ops[:, tl, oc],
                                                         hidb[:, hc, lcols],
                                                         w2c[:, hl, oc * 512:(oc + 1) * 512],
                                                         start=(hc == 0), stop=False,
                                                         skip_group_check=True)
                            p.barrier()
                        for tl in range(4):
                            for oc in range(2):
                                nc.tensor.matmul(ops[:, tl, oc], onec[0:1, 0:128],
                                                 b2s[0:1, oc * 512:(oc + 1) * 512],
                                                 start=False, stop=True,
                                                 skip_group_check=True)
                        p.barrier()
                        for tl in range(4):
                            tg = sup * 4 + tl
                            for oc in range(2):
                                nc.vector.scalar_tensor_tensor(
                                    out=macc[:, tl, oc * 512:(oc + 1) * 512],
                                    in0=ops[:, tl, oc], scalar=ew[:, tg, e:e + 1],
                                    in1=macc[:, tl, oc * 512:(oc + 1) * 512],
                                    op0=ALU.mult, op1=ALU.add)
                        p.barrier()
                for tl in range(4):
                    nc.vector.tensor_copy(out=mtb[:, :], in_=macc[:, tl])
                    p.barrier()
                    with nc.psum_tensor(_nm("tps"), [128, 8, 128], BF16) as tps:
                        for dc in range(8):
                            nc.tensor.transpose(tps[:, dc],
                                                mtb[:, dc * 128:(dc + 1) * 128],
                                                idb[:, :])
                        p.barrier()
                        gcols = slice(sup * 512 + tl * 128, sup * 512 + (tl + 1) * 128)
                        for dc in range(8):
                            nc.vector.tensor_add(ybuf[:, dc, gcols], tps[:, dc],
                                                 xa1[:, dc, gcols])
                        p.barrier()

        with (
            nc.sbuf_tensor(N("fsq"), [128, S], BF16) as fsq,
            nc.sbuf_tensor(N("fr"), [1, S], F32) as fr,
            nc.sbuf_tensor(N("fmr"), [1, S], F32) as fmr,
            nc.sbuf_tensor(N("frbf"), [1, S], BF16) as frbf,
        ):
          emit_ln_fm(p, None, fsq, fr[0:1], fmr[0:1], S,
                     x_src_chunks=[ybuf[:, c] for c in range(8)])
          with (
            nc.psum_tensor(N("rps2"), [128, 512], F32) as rps2,
            nc.psum_tensor(N("mrps2"), [128, 512], F32) as mrps2,
          ):
            for tq in range(S // 512):
                cols = slice(tq * 512, (tq + 1) * 512)
                emit_bcast_row(p, fr[0:1, cols], rps2, 512, frbf)
                emit_bcast_row(p, fmr[0:1, cols], mrps2, 512, frbf)
                for c in range(8):
                    p.dma(nc.sync, stgg[:, :], xin[c, :, cols])
                    p.barrier()
                    nc.vector.tensor_mul(stgf[:, 0:512], ybuf[:, c, cols], rps2[:, :])
                    p.barrier()
                    nc.vector.tensor_sub(stgf[:, 0:512], stgf[:, 0:512], mrps2[:, :])
                    p.barrier()
                    nc.vector.tensor_add(stgf[:, 0:512], stgf[:, 0:512], stgg[:, :])
                    p.barrier()
                    p.dma(nc.sync, xout_ap[c, :, cols], stgf[:, 0:512])
                    p.barrier()


def emit_l1ffn(nc, p, pfx, xin, wgT, wuT, wdT, xout):
    import contextlib
    N = lambda s: pfx + s
    HID = 3072
    NH = HID // 128
    SC = 1.0 / 127.0
    SH = S // 2
    es = contextlib.ExitStack()
    with es:
        xp = es.enter_context(nc.sbuf_tensor(N("xp"), [128, 8, S], BF16))
        ybuf = es.enter_context(nc.sbuf_tensor(N("ybuf"), [128, 8, S], BF16))
        wd = es.enter_context(nc.sbuf_tensor(N("wd"), [128, NH, D], BF16))
        stg = es.enter_context(nc.sbuf_tensor(N("stg"), [128, 8, 128], I32))
        sq = es.enter_context(nc.sbuf_tensor(N("sq"), [128, S], BF16))
        rowr = es.enter_context(nc.sbuf_tensor(N("rowr"), [1, S], F32))
        rowmr = es.enter_context(nc.sbuf_tensor(N("rowmr"), [1, S], F32))
        rowbf = es.enter_context(nc.sbuf_tensor(N("rowbf"), [1, S], BF16))
        xst = es.enter_context(nc.sbuf_tensor(N("xst"), [128, S], F32))
        whc = es.enter_context(nc.sbuf_tensor(N("whc"), [128, 2, 8, 128], BF16))
        gbuf = es.enter_context(nc.sbuf_tensor(N("gbuf"), [128, SH], BF16))
        with nc.sbuf_tensor(N("xb"), [128, 8, S], BF16) as xb:
            for c in range(8):
                p.dma(nc.sync, xst[:, :], xin[c])
                p.barrier()
                nc.vector.tensor_copy(out=xb[:, c], in_=xst[:, :])
                p.barrier()
            emit_ln_fm(p, None, sq, rowr[0:1], rowmr[0:1], S,
                       x_src_chunks=[xb[:, c] for c in range(8)])
            with (
                nc.psum_tensor(N("rps"), [128, 512], F32) as rps,
                nc.psum_tensor(N("mrps"), [128, 512], F32) as mrps,
            ):
                for tt in range(S // 512):
                    cols = slice(tt * 512, (tt + 1) * 512)
                    emit_bcast_row(p, rowr[0:1, cols], rps, 512, rowbf)
                    emit_bcast_row(p, rowmr[0:1, cols], mrps, 512, rowbf)
                    for c in range(8):
                        nc.vector.tensor_mul(xp[:, c, cols], xb[:, c, cols], rps[:, :])
                    p.barrier()
                    for c in range(8):
                        nc.vector.tensor_sub(xp[:, c, cols], xp[:, c, cols], mrps[:, :])
                    p.barrier()

        with nc.sbuf_tensor(N("hbuf"), [128, NH, SH], BF16) as hbuf:
            for hc in range(NH):
                p.dma(nc.sync, stg.ap().rearrange("p a b -> p (a b)")[:, 0:D], wdT[hc])
                p.barrier()
                nc.vector.tensor_copy(out=wd[:, hc],
                                      in_=stg.ap().rearrange("p a b -> p (a b)")[:, 0:D])
                p.barrier()
            for half in range(2):
                cols = slice(half * SH, (half + 1) * SH)
                with (
                    nc.psum_tensor(_nm("gps"), [128, SH], F32) as gps,
                    nc.psum_tensor(_nm("ups"), [128, SH], F32) as ups,
                ):
                    for hc in range(NH):
                        hs = slice(hc * 128, (hc + 1) * 128)
                        p.dma(nc.sync, stg[:, :, :], wgT[:, :, hs].rearrange("a p b -> p a b"))
                        p.barrier()
                        nc.vector.tensor_copy(out=whc[:, 0], in_=stg[:, :, :])
                        p.barrier()
                        p.dma(nc.sync, stg[:, :, :], wuT[:, :, hs].rearrange("a p b -> p a b"))
                        p.barrier()
                        nc.vector.tensor_copy(out=whc[:, 1], in_=stg[:, :, :])
                        p.barrier()
                        for dc in range(8):
                            mm(nc, gps[:, :], whc[:, 0, dc], xp[:, dc, cols],
                               start=(dc == 0), stop=(dc == 7))
                            mm(nc, ups[:, :], whc[:, 1, dc], xp[:, dc, cols],
                               start=(dc == 0), stop=(dc == 7))
                        p.barrier()
                        nc.scalar.activation(out=gbuf[:, :], in_=gps[:, :],
                                             func=AF.Silu, scale=SC)
                        nc.scalar.activation(out=hbuf[:, hc], in_=ups[:, :],
                                             func=AF.Copy, scale=SC * SC)
                        p.barrier()
                        nc.vector.tensor_mul(hbuf[:, hc], gbuf[:, :], hbuf[:, hc])
                    p.barrier()
                for dq in range(2):
                    with nc.psum_tensor(_nm("dps"), [128, 4, SH], F32) as dps:
                        for i in range(4):
                            dc = dq * 4 + i
                            for hc in range(NH):
                                mm(nc, dps[:, i], wd[:, hc, dc * 128:(dc + 1) * 128],
                                   hbuf[:, hc], start=(hc == 0), stop=(hc == NH - 1))
                        p.barrier()
                        for i in range(4):
                            nc.vector.tensor_copy(out=ybuf[:, dq * 4 + i, cols],
                                                  in_=dps[:, i])
                        p.barrier()

        emit_ln_fm(p, None, sq, rowr[0:1], rowmr[0:1], S,
                   x_src_chunks=[ybuf[:, c] for c in range(8)])
        with (
            nc.psum_tensor(N("rps2"), [128, SH], F32) as rps2,
            nc.psum_tensor(N("mrps2"), [128, SH], F32) as mrps2,
        ):
            for half in range(2):
                cols = slice(half * SH, (half + 1) * SH)
                emit_bcast_row(p, rowr[0:1, cols], rps2, SH, rowbf)
                emit_bcast_row(p, rowmr[0:1, cols], mrps2, SH, rowbf)
                with nc.sbuf_tensor(_nm("swo"), [128, SH], F32) as tmp:
                    for c in range(8):
                        p.dma(nc.sync, xst[:, 0:SH],
                              xin[c].rearrange("p (h t) -> p h t", h=2)[:, half])
                        p.barrier()
                        nc.vector.tensor_mul(tmp[:, :], ybuf[:, c, cols], rps2[:, :])
                        p.barrier()
                        nc.vector.tensor_sub(tmp[:, :], tmp[:, :], mrps2[:, :])
                        p.barrier()
                        nc.vector.tensor_add(tmp[:, :], tmp[:, :], xst[:, 0:SH])
                        p.barrier()
                        p.dma(nc.sync,
                              xout[c].rearrange("p (h t) -> p h t", h=2)[:, half],
                              tmp[:, :])
                        p.barrier()
        p.barrier()


def build_full(upto=None):
    nc = bass.Bass("TRN2", target_bir_lowering=False)
    EI, EO = "ExternalInput", "ExternalOutput"
    a_xin = nc.dram_tensor("a_xin", [8, 128, T], F32, kind=EI).ap()
    a_wqp = nc.dram_tensor("a_wqp", [8, 128, 512], F32, kind=EI).ap()
    a_wdn = nc.dram_tensor("a_wdn", [8, 128, 512], F32, kind=EI).ap()
    a_wkup = nc.dram_tensor("a_wkup", [4, 128, 512], F32, kind=EI).ap()
    a_wvup = nc.dram_tensor("a_wvup", [4, 128, 512], F32, kind=EI).ap()
    a_wop = nc.dram_tensor("a_wop", [4, 128, D], F32, kind=EI).ap()
    rc2 = nc.dram_tensor("rc2", [128, T], F32, kind=EI).ap()
    rs2 = nc.dram_tensor("rs2", [128, T], F32, kind=EI).ap()
    x_tok = nc.dram_tensor("x_tok", [8, 128, S], F32, kind=EI).ap()
    cw1 = nc.dram_tensor("cw1", [8, 128, 4096], F32, kind=EI).ap()
    cw2 = nc.dram_tensor("cw2", [32, 128, 1024], F32, kind=EI).ap()
    f_wqs = nc.dram_tensor("f_wqs", [8, 128, 512], F32, kind=EI).ap()
    f_wks = nc.dram_tensor("f_wks", [8, 128, 512], F32, kind=EI).ap()
    f_wvs = nc.dram_tensor("f_wvs", [8, 128, 512], F32, kind=EI).ap()
    f_bq = nc.dram_tensor("f_bq", [128, 4], F32, kind=EI).ap()
    f_bk = nc.dram_tensor("f_bk", [128, 4], F32, kind=EI).ap()
    f_bv = nc.dram_tensor("f_bv", [1, 512], F32, kind=EI).ap()
    f_l1g = nc.dram_tensor("f_l1g", [128, 8], F32, kind=EI).ap()
    f_l1b = nc.dram_tensor("f_l1b", [128, 8], F32, kind=EI).ap()
    f_wop = nc.dram_tensor("f_wop", [4, 128, D], F32, kind=EI).ap()
    m_bo = nc.dram_tensor("m_bo", [128, 8], F32, kind=EI).ap()
    m_l2g = nc.dram_tensor("m_l2g", [128, 8], F32, kind=EI).ap()
    m_l2b = nc.dram_tensor("m_l2b", [128, 8], F32, kind=EI).ap()
    m_gw = nc.dram_tensor("m_gw", [8, 128, 8], F32, kind=EI).ap()
    m_gbr = nc.dram_tensor("m_gbr", [1, 8], F32, kind=EI).ap()
    m_b1c = nc.dram_tensor("m_b1c", [8, 128, 32], F32, kind=EI).ap()
    m_b2r = nc.dram_tensor("m_b2r", [8, 1, D], F32, kind=EI).ap()
    ident = nc.dram_tensor("ident", [128, 128], F32, kind=EI).ap()
    p1_wT = nc.dram_tensor("p1_wT", [8, 128, 4096], F32, kind=EI).ap()
    p1_b0 = nc.dram_tensor("p1_b0", [1, 4096], F32, kind=EI).ap()
    p1_b1 = nc.dram_tensor("p1_b1", [1, 4096], F32, kind=EI).ap()
    p2_wT = nc.dram_tensor("p2_wT", [8, 128, 4160], F32, kind=EI).ap()
    p2_b0 = nc.dram_tensor("p2_b0", [1, 4096], F32, kind=EI).ap()
    p2_b1 = nc.dram_tensor("p2_b1", [1, 4096], F32, kind=EI).ap()
    s1_u0 = nc.dram_tensor("s1_u0", [8, 128, 2048], F32, kind=EI).ap()
    s1_u1 = nc.dram_tensor("s1_u1", [8, 128, 2112], F32, kind=EI).ap()
    s1_pt = nc.dram_tensor("s1_pt", [64, 1024], F32, kind=EI).ap()
    s1_al = nc.dram_tensor("s1_al", [1, 1024], F32, kind=EI).ap()
    s2_u0 = nc.dram_tensor("s2_u0", [8, 128, 2048], F32, kind=EI).ap()
    s2_u1 = nc.dram_tensor("s2_u1", [8, 128, 2112], F32, kind=EI).ap()
    s2_pt = nc.dram_tensor("s2_pt", [64, 1024], F32, kind=EI).ap()
    s2_al = nc.dram_tensor("s2_al", [1, 1024], F32, kind=EI).ap()
    xaz = nc.dram_tensor("xaz", [T + 1, 64], BF16, kind=EI).ap()
    l1_wgT = nc.dram_tensor("l1_wgT", [8, 128, 3072], I32, kind=EI).ap()
    l1_wuT = nc.dram_tensor("l1_wuT", [8, 128, 3072], I32, kind=EI).ap()
    l1_wdT = nc.dram_tensor("l1_wdT", [24, 128, 1024], I32, kind=EI).ap()
    yp1 = nc.dram_tensor("yp1", [8, 128, T], F32).ap()
    yp1g = nc.dram_tensor("yp1g", [8 * 1024, T], F32).ap()
    x1 = nc.dram_tensor("x1", [8, 128, S], F32).ap()
    x1g = nc.dram_tensor("x1g", [8 * 1024, S], F32).ap()
    w1o = nc.dram_tensor("w1o", [8, 128, 4096], BF16).ap()
    w1g = nc.dram_tensor("w1g", [8 * 1024, 4096], BF16).ap()
    w2o = nc.dram_tensor("w2o", [32, 128, 1024], BF16).ap()
    w2g = nc.dram_tensor("w2g", [8 * 4096, 1024], BF16).ap()
    yp2 = nc.dram_tensor("yp2", [8, 128, T], F32).ap()
    yp2g = nc.dram_tensor("yp2g", [8 * 1024, T], F32).ap()
    x2 = nc.dram_tensor("x2", [8, 128, S], F32).ap()
    x2g = nc.dram_tensor("x2g", [8 * 1024, S], F32).ap()
    xw1 = nc.dram_tensor("xw1", [T + 1, 4096], BF16).ap()
    sh1 = nc.dram_tensor("sh1", [T, 1024], BF16).ap()
    xw2 = nc.dram_tensor("xw2", [T + 1, 4160], BF16).ap()
    sh2 = nc.dram_tensor("sh2", [T, 1024], BF16).ap()
    x3 = nc.dram_tensor("x3", [8, 128, S], F32).ap()
    ya1l = nc.dram_tensor("ya1l", [8, 128, S], F32).ap()
    yb1l = nc.dram_tensor("yb1l", [8, 128, S], F32).ap()
    ya2l = nc.dram_tensor("ya2l", [8, 128, S], F32).ap()
    yb2l = nc.dram_tensor("yb2l", [8, 128, S], F32).ap()
    x1full = nc.dram_tensor("x1full", [8, 128, T], F32).ap()
    x2full = nc.dram_tensor("x2full", [8, 128, T], F32).ap()
    hml = nc.dram_tensor("hml", [S, 1024], BF16).ap()
    xout = nc.dram_tensor("xout", [8, 128, S], F32, kind=EO).ap()

    pid = nc.partition_id()
    r0 = (pid >> 1) * 2
    r1 = r0 + 1
    hh = pid & 1
    p = Prog(nc)

    def fin(tag, src, shp, dt):
        dbg = nc.dram_tensor("dbg", shp, dt, kind=EO).ap()
        p.barrier()
        p.dma(nc.sync, dbg, src)
        p.barrier()
        return nc

    emit_wcast(nc, p, "wc_", cw1, cw2, w1o, w2o)
    p.barrier()
    sem_w1 = p.allgather(w1o.rearrange("a p h -> (a p) h"), w1g)
    sem_w2 = p.allgather(w2o.rearrange("a p h -> (a p) h"), w2g)

    emit_l0attn(nc, p, "a_", a_xin, a_wqp, a_wdn, a_wkup, a_wvup, a_wop,
                rc2, rs2, yp1)
    p.barrier()
    sem_y1 = p.allgather(yp1.rearrange("a p t -> (a p) t"), yp1g)
    p.cc_wait(sem_y1)

    yp1g_v = yp1g.rearrange("(r q) (h t) -> r h q t", r=8, h=2)
    p.dma(nc.sync, ya1l.rearrange("a p t -> (a p) t"), yp1g_v[r0, hh])
    p.dma(nc.sync, yb1l.rearrange("a p t -> (a p) t"), yp1g_v[r1, hh])
    p.barrier()
    emit_comb(nc, p, "cb_", x_tok,
              lambda c: ya1l[c], lambda c: yb1l[c], x1)
    p.barrier()
    if upto == "x1":
        return fin("x1", x1.rearrange("a p t -> (a p) t"), [1024, S], F32)
    sem_x1 = p.allgather(x1.rearrange("a p t -> (a p) t"), x1g)
    p.cc_wait(sem_x1)

    x1g_v = x1g.rearrange("(r q) t -> r q t", r=8)
    x1f_v = x1full.rearrange("a p (h t) -> (a p) h t", h=2)
    p.dma(nc.sync, x1f_v[:, 0], x1g_v[r0])
    p.dma(nc.sync, x1f_v[:, 1], x1g_v[r1])
    p.barrier()
    emit_l0fattn(nc, p, "f_", x1full, f_wqs, f_wks, f_wvs, f_bq, f_bk, f_bv,
                 f_l1g, f_l1b, f_wop, yp2)
    p.barrier()
    sem_y2 = p.allgather(yp2.rearrange("a p t -> (a p) t"), yp2g)
    p.cc_wait(sem_y2)
    p.cc_wait(sem_w1)
    p.cc_wait(sem_w2)

    yp2g_v = yp2g.rearrange("(r q) (h t) -> r h q t", r=8, h=2)
    p.dma(nc.sync, ya2l.rearrange("a p t -> (a p) t"), yp2g_v[r0, hh])
    p.dma(nc.sync, yb2l.rearrange("a p t -> (a p) t"), yp2g_v[r1, hh])
    p.barrier()
    w1b_v = w1g.rearrange("(e a p) h -> e a p h", e=8, a=8)
    w2b_v = w2g.rearrange("(e n p) d -> e n p d", e=8, n=32)
    emit_l0fmoe(nc, p, "m_", x1,
                lambda c: ya2l[c], lambda c: yb2l[c],
                m_bo, m_l2g, m_l2b, m_gw, m_gbr, w1b_v, w2b_v, m_b1c, m_b2r,
                ident, x2)
    p.barrier()
    if upto == "x2":
        return fin("x2", x2.rearrange("a p t -> (a p) t"), [1024, S], F32)
    sem_x2 = p.allgather(x2.rearrange("a p t -> (a p) t"), x2g)
    p.cc_wait(sem_x2)

    x2g_v = x2g.rearrange("(r q) t -> r q t", r=8)
    x2f_v = x2full.rearrange("a p (h t) -> (a p) h t", h=2)
    p.dma(nc.sync, x2f_v[:, 0], x2g_v[r0])
    p.dma(nc.sync, x2f_v[:, 1], x2g_v[r1])
    p.barrier()
    emit_premid(nc, p, "p1_", 4096, "fm", x2full, p1_wT, p1_b0, p1_b1, xw1)
    p.barrier()
    if upto == "xw1":
        return fin("xw1", xw1[0:T], [T, 4096], BF16)

    xaz_v = xaz.rearrange("t (x o) -> t x o", o=1)
    emit_scan(nc, p, "s1_", xw1, xaz_v, s1_u0, s1_u1, s1_pt, s1_al, sh1)
    if upto == "sh1":
        return fin("sh1", sh1, [T, 1024], BF16)

    emit_premid(nc, p, "p2_", 4160, "sh", sh1, p2_wT, p2_b0, p2_b1, xw2,
                ident_in=ident)
    p.barrier()

    xa2_v = xw2.rearrange("t (x o) -> t x o", o=1)[:, 4096:4160]
    emit_scan(nc, p, "s2_", xw2[:, 0:4096], xa2_v,
              s2_u0, s2_u1, s2_pt, s2_al, sh2)
    if upto == "sh2":
        return fin("sh2", sh2, [T, 1024], BF16)

    sh2_v = sh2.rearrange("(h t) d -> h t d", h=2)
    p.dma(nc.sync, hml, sh2_v[hh])
    p.barrier()
    emit_post(nc, p, "po_", x2, hml, ident, x3)
    p.barrier()
    if upto == "x3":
        return fin("x3", x3.rearrange("a p t -> (a p) t"), [1024, S], F32)

    emit_l1ffn(nc, p, "l1_", x3, l1_wgT, l1_wuT, l1_wdT, xout)
    p.barrier()
    return nc


def build_inmaps(inp, x_fm):
    perm = []
    for h in range(16):
        perm += [64 * h + 2 * i for i in range(32)]
        perm += [64 * h + 2 * i + 1 for i in range(32)]
    perm = np.asarray(perm)
    wq_p = inp["L0_attn_wq"][:, perm]
    wk_p = inp["L0_attn_wk_up"][:, perm]
    inv = 1.0 / 10000.0 ** (np.arange(0, 64, 2, dtype=np.float32) / 64)
    ang = np.outer(np.arange(T, dtype=np.float32), inv)
    cos2 = np.ascontiguousarray(np.concatenate([np.cos(ang).T] * 4, axis=0)).astype(np.float32)
    sin2 = np.ascontiguousarray(np.concatenate([np.sin(ang).T] * 4, axis=0)).astype(np.float32)

    w_qkv, b_qkv = inp["L0_ffn_w_qkv"], inp["L0_ffn_b_qkv"]
    identm = np.eye(128, dtype=np.float32)

    def scan_prep(U, Bm, P, alpha):
        UT = np.ascontiguousarray(U.T)
        u0 = np.ascontiguousarray(UT.reshape(8, 128, 4096)[:, :, 0:2048])
        u1z = UT.reshape(8, 128, 4096)[:, :, 2048:4096]
        BmT = np.ascontiguousarray(Bm.T).reshape(8, 128, 64)
        u1 = np.ascontiguousarray(np.concatenate([u1z, BmT], axis=2))
        return (u0, u1, np.ascontiguousarray(P.T),
                np.ascontiguousarray(alpha.reshape(1, 1024)))

    z64 = np.zeros((64, 1024), np.float32)
    zP = np.zeros((1024, 64), np.float32)
    s1 = scan_prep(inp["L1_attn_Us"], z64, zP, inp["L1_attn_alpha"])
    s2 = scan_prep(inp["L1_attn_Um"], inp["L1_attn_Bm"], inp["L1_attn_P"],
                   np.ones(1024, np.float32))
    p1_wT = np.ascontiguousarray(inp["L1_attn_Ws"].T).reshape(8, 128, 4096)
    p2_wT = np.ascontiguousarray(
        np.concatenate([inp["L1_attn_Wm"].T, inp["L1_attn_A"].T], axis=1)
    ).reshape(8, 128, 4160)
    xaz = np.zeros((T + 1, 64), dtype=BF)
    wgT = np.ascontiguousarray(inp["L1_ffn_w_gate"].T).reshape(8, 128, 3072)
    wuT = np.ascontiguousarray(inp["L1_ffn_w_up"].T).reshape(8, 128, 3072)
    wdT = np.ascontiguousarray(inp["L1_ffn_w_down"].T).reshape(24, 128, 1024)

    maps = []
    for c in range(NCORES):
        b2, hh = c // 2, c % 2
        hcols = slice(512 * hh, 512 * (hh + 1))
        qcols = slice(512 * hh, 512 * (hh + 1))
        kcols = slice(1024 + 512 * hh, 1024 + 512 * (hh + 1))
        vcols = slice(2048 + 512 * hh, 2048 + 512 * (hh + 1))
        m = {
            "a_xin": np.ascontiguousarray(x_fm[:, b2 * T:(b2 + 1) * T]).reshape(8, 128, T),
            "a_wqp": np.ascontiguousarray(wq_p[:, hcols]).reshape(8, 128, 512),
            "a_wdn": np.ascontiguousarray(inp["L0_attn_w_down"]).reshape(8, 128, 512),
            "a_wkup": np.ascontiguousarray(wk_p[:, hcols]).reshape(4, 128, 512),
            "a_wvup": np.ascontiguousarray(
                inp["L0_attn_wv_up"][:, 512 * hh:512 * (hh + 1)]).reshape(4, 128, 512),
            "a_wop": np.ascontiguousarray(
                inp["L0_attn_wo"][512 * hh:512 * (hh + 1), :]).reshape(4, 128, D),
            "rc2": cos2, "rs2": sin2,
            "x_tok": np.ascontiguousarray(x_fm[:, c * S:(c + 1) * S]).reshape(8, 128, S),
            "cw1": np.ascontiguousarray(inp["L0_ffn_W1"][c]).reshape(8, 128, 4096),
            "cw2": np.ascontiguousarray(inp["L0_ffn_W2"][c]).reshape(32, 128, 1024),
            "f_wqs": np.ascontiguousarray(w_qkv[:, qcols]).reshape(8, 128, 512),
            "f_wks": np.ascontiguousarray(w_qkv[:, kcols]).reshape(8, 128, 512),
            "f_wvs": np.ascontiguousarray(w_qkv[:, vcols]).reshape(8, 128, 512),
            "f_bq": np.ascontiguousarray(b_qkv[qcols].reshape(4, 128).T),
            "f_bk": np.ascontiguousarray(b_qkv[kcols].reshape(4, 128).T),
            "f_bv": np.ascontiguousarray(b_qkv[vcols]).reshape(1, 512),
            "f_l1g": np.ascontiguousarray(inp["L0_ffn_ln1_g"].reshape(8, 128).T),
            "f_l1b": np.ascontiguousarray(inp["L0_ffn_ln1_b"].reshape(8, 128).T),
            "f_wop": np.ascontiguousarray(
                inp["L0_ffn_w_o"][512 * hh:512 * (hh + 1), :]).reshape(4, 128, D),
            "m_bo": np.ascontiguousarray(inp["L0_ffn_b_o"].reshape(8, 128).T),
            "m_l2g": np.ascontiguousarray(inp["L0_ffn_ln2_g"].reshape(8, 128).T),
            "m_l2b": np.ascontiguousarray(inp["L0_ffn_ln2_b"].reshape(8, 128).T),
            "m_gw": np.ascontiguousarray(inp["L0_ffn_gate_w"]).reshape(8, 128, 8),
            "m_gbr": np.ascontiguousarray(inp["L0_ffn_gate_b"]).reshape(1, 8),
            "m_b1c": np.ascontiguousarray(
                inp["L0_ffn_b1"].reshape(8, 32, 128).transpose(0, 2, 1)),
            "m_b2r": np.ascontiguousarray(inp["L0_ffn_b2"]).reshape(8, 1, 1024),
            "ident": identm,
            "p1_wT": p1_wT,
            "p1_b0": np.ascontiguousarray(inp["L1_attn_bWs"]).reshape(1, 4096),
            "p1_b1": np.ascontiguousarray(inp["L1_attn_bUs"]).reshape(1, 4096),
            "p2_wT": p2_wT,
            "p2_b0": np.ascontiguousarray(inp["L1_attn_bWm"]).reshape(1, 4096),
            "p2_b1": np.ascontiguousarray(inp["L1_attn_bUm"]).reshape(1, 4096),
            "s1_u0": s1[0], "s1_u1": s1[1], "s1_pt": s1[2], "s1_al": s1[3],
            "s2_u0": s2[0], "s2_u1": s2[1], "s2_pt": s2[2], "s2_al": s2[3],
            "xaz": xaz,
            "l1_wgT": wgT, "l1_wuT": wuT, "l1_wdT": wdT,
        }
        maps.append(m)
    return maps


def kernel(**inputs):
    inp = {k: np.asarray(v) for k, v in inputs.items()}
    x = inp["L0_attn_x"]
    x_fm = np.ascontiguousarray(x.reshape(NTOK, D).T)
    maps = build_inmaps(inp, x_fm)
    nc = build_full()
    res = run_bass_kernel_spmd(nc, maps, core_ids=list(range(NCORES)))
    out_fm = np.concatenate([res.results[c]["xout"].reshape(D, S)
                             for c in range(NCORES)], axis=1)
    return np.ascontiguousarray(out_fm.T).reshape(B, T, D)


# version 08 (final)
# speedup vs baseline: 1.3925x; 1.0007x over previous
import numpy as np
import ml_dtypes
import concourse.bass as bass
import concourse.mybir as mybir
from concourse.bass_utils import run_bass_kernel_spmd

F32 = mybir.dt.float32
BF16 = mybir.dt.bfloat16
I32 = mybir.dt.int32
AF = mybir.ActivationFunctionType
ALU = mybir.AluOpType
BF = ml_dtypes.bfloat16

NCORES = 8
B, T, D = 4, 4096, 1024
NTOK = B * T
S = NTOK // NCORES
EPS = 1e-5


def ceil_div(a, b):
    return -(-a // b)

_UID = [0]


def _nm(base):
    _UID[0] += 1
    return f"{base}_{_UID[0]}"


def mm(nc, out, lhsT, rhs, start, stop, tile_position=None):
    n = rhs.shape[-1]
    kw = {} if tile_position is None else {"tile_position": tile_position}
    for j in range(ceil_div(n, 512)):
        cols = slice(j * 512, min((j + 1) * 512, n))
        nc.tensor.matmul(out[..., cols], lhsT, rhs[..., cols],
                         start=start, stop=stop, skip_group_check=True, **kw)


class Prog:

    def __init__(self, nc):
        self.nc = nc
        self.dma_sems = {}
        self.dma_counts = {}
        self.cc_uid = 0

    def dma(self, qeng, out, in_, **kw):
        key = qeng.engine.name
        if key not in self.dma_sems:
            self.dma_sems[key] = (qeng, self.nc.alloc_semaphore(f"dmasem_{key}"))
            self.dma_counts[key] = 0
        _, sem = self.dma_sems[key]
        qeng.dma_start(out=out, in_=in_, **kw).then_inc(sem, 16)
        self.dma_counts[key] += 16
        return sem

    def dma_wait_all(self):
        for key, (eng, sem) in self.dma_sems.items():
            eng.wait_ge(sem, self.dma_counts[key])

    def barrier(self):
        self.dma_wait_all()
        self.nc.all_engine_barrier()

    def allgather(self, in_ap2d, out_ap2d):
        self.cc_uid += 1
        sem = self.nc.alloc_semaphore(f"ccsem_{self.cc_uid}")
        self.nc.gpsimd.collective_compute(
            "AllGather", mybir.AluOpType.bypass,
            replica_groups=[list(range(8))],
            ins=[in_ap2d.opt()],
            outs=[out_ap2d.opt()],
        ).then_inc(sem)
        return sem

    def cc_wait(self, sem):
        self.nc.gpsimd.wait_ge(sem, 1)
        self.nc.all_engine_barrier()


def emit_ln_fm(p, xb_out, sq_buf, row_r, row_mr, t_cols, nchunk=8,
               x_src_chunks=None):
    nc = p.nc
    dmodel = 128 * nchunk
    with (
        nc.sbuf_tensor(_nm("ln_ones"), [128, 1], BF16) as ones,
        nc.sbuf_tensor(_nm("ln_m"), [1, t_cols], F32) as row_m,
        nc.sbuf_tensor(_nm("ln_v"), [1, t_cols], F32) as row_v,
        nc.psum_tensor(_nm("ln_ps"), [1, t_cols], F32) as ps_s,
        nc.psum_tensor(_nm("ln_ps2"), [1, t_cols], F32) as ps_q,
    ):
        nc.vector.memset(ones[:, :], 1.0)
        p.barrier()
        ncol = ceil_div(t_cols, 512)
        src = x_src_chunks if x_src_chunks is not None else [
            xb_out[:, c] for c in range(nchunk)]
        for c in range(nchunk):
            nc.scalar.activation(out=sq_buf[:, :], in_=src[c], func=AF.Square)
            p.barrier()
            for j in range(ncol):
                cols = slice(j * 512, min((j + 1) * 512, t_cols))
                nc.tensor.matmul(ps_s[0:1, cols], ones[:, :], src[c][:, cols],
                                 start=(c == 0), stop=(c == nchunk - 1),
                                 skip_group_check=True)
                nc.tensor.matmul(ps_q[0:1, cols], ones[:, :], sq_buf[:, cols],
                                 start=(c == 0), stop=(c == nchunk - 1),
                                 skip_group_check=True)
            p.barrier()
        m = row_m[0:1, :]
        var = row_v[0:1, :]
        nc.vector.tensor_scalar_mul(m, ps_s[0:1, :], 1.0 / dmodel)
        p.barrier()
        nc.vector.tensor_mul(var, m, m)
        p.barrier()
        nc.vector.scalar_tensor_tensor(out=var, in0=ps_q[0:1, :], scalar=1.0 / dmodel,
                                       in1=var, op0=ALU.mult, op1=ALU.subtract)
        p.barrier()
        nc.vector.tensor_scalar_add(var, var, EPS)
        p.barrier()
        nc.scalar.activation(out=var, in_=var, func=AF.Sqrt)
        p.barrier()
        nc.vector.reciprocal(row_r[0:1, :], var)
        p.barrier()
        nc.vector.tensor_mul(row_mr[0:1, :], m, row_r[0:1, :])
        p.barrier()


def emit_bcast_row(p, row_f32, ps_out, t_cols, row_bf):
    nc = p.nc
    with nc.sbuf_tensor(_nm("bc_ones"), [1, 128], BF16) as ones1:
        nc.vector.memset(ones1[:, :], 1.0)
        nc.vector.tensor_copy(out=row_bf[0:1, 0:t_cols], in_=row_f32)
        p.barrier()
        for j in range(ceil_div(t_cols, 512)):
            cols = slice(j * 512, min((j + 1) * 512, t_cols))
            nc.tensor.matmul(ps_out[:, cols], ones1[:, :], row_bf[0:1, cols],
                             start=True, stop=True)
        p.barrier()


def emit_scan(nc, p, pfx, xw_ap, xa_ap, u0, u1, pT, alpha, sh_ap, T_steps=T):
    import contextlib
    N = lambda s: pfx + s
    es = contextlib.ExitStack()
    with es:
        u0b = es.enter_context(nc.sbuf_tensor(N("u0b"), [128, 8, 2048], BF16))
        u1b = es.enter_context(nc.sbuf_tensor(N("u1b"), [128, 8, 2112], BF16))
        bmb = es.enter_context(nc.sbuf_tensor(N("bmb"), [128, 8, 64], BF16))
        stgf = es.enter_context(nc.sbuf_tensor(N("stgf"), [128, 2112], F32))
        pTb = es.enter_context(nc.sbuf_tensor(N("pTb"), [64, 1024], BF16))
        alpr = es.enter_context(nc.sbuf_tensor(N("alpr"), [1, 1024], F32))
        xwrow = es.enter_context(nc.sbuf_tensor(N("xwrow"), [1, 4096], BF16))
        xac = es.enter_context(nc.sbuf_tensor(N("xac"), [64, 1], BF16))
        one1 = es.enter_context(nc.sbuf_tensor(N("one1"), [1, 1], BF16))
        ident = es.enter_context(nc.sbuf_tensor(N("ident"), [1, 1], BF16))
        hT = es.enter_context(nc.sbuf_tensor(N("hT"), [128, 8], BF16))
        hb = es.enter_context(nc.sbuf_tensor(N("hb"), [1, 1024], BF16))
        cst = es.enter_context(nc.sbuf_tensor(N("cst"), [1, 1024], F32))
        sif = es.enter_context(nc.sbuf_tensor(N("sif"), [1, 2048], F32))
        sog = es.enter_context(nc.sbuf_tensor(N("sog"), [1, 2048], F32))
        rb = es.enter_context(nc.sbuf_tensor(N("rb"), [64, 1], BF16))
        tm1 = es.enter_context(nc.sbuf_tensor(N("tm1"), [1, 1024], F32))
        tm2 = es.enter_context(nc.sbuf_tensor(N("tm2"), [1, 1024], F32))
        tc = es.enter_context(nc.sbuf_tensor(N("tc"), [1, 1024], F32))
        Zp = es.enter_context(nc.psum_tensor(N("Z"), [128, 2048], F32))
        psM = es.enter_context(nc.psum_tensor(N("psM"), [1, 1024], F32))
        pst = es.enter_context(nc.psum_tensor(N("pst"), [128, 8, 2], BF16))
        scr = es.enter_context(nc.psum_tensor(N("scr"), [1, 512], F32))
        ld = nc.alloc_semaphore(N("ld"))
        st = nc.alloc_semaphore(N("st"))
        spe = nc.alloc_semaphore(N("spe"))
        sact = nc.alloc_semaphore(N("sact"))
        sdve = nc.alloc_semaphore(N("sdve"))
        nc.vector.memset(one1[:, :], 1.0)
        nc.vector.memset(ident[:, :], 1.0)
        nc.vector.memset(hT[:, :], 0.0)
        nc.vector.memset(cst[:, :], 0.0)
        p.dma(nc.sync, stgf[0:64, 0:1024], pT[:, :])
        p.barrier()
        nc.vector.tensor_copy(out=pTb[:, :], in_=stgf[0:64, 0:1024])
        p.barrier()
        p.dma(nc.sync, alpr[0:1, :], alpha[:, :])
        p.barrier()
        for k in range(8):
            p.dma(nc.sync, stgf[:, 0:2048], u0[k])
            p.barrier()
            nc.vector.tensor_copy(out=u0b[:, k], in_=stgf[:, 0:2048])
            p.barrier()
            p.dma(nc.sync, stgf[:, 0:2112], u1[k])
            p.barrier()
            nc.vector.tensor_copy(out=u1b[:, k], in_=stgf[:, 0:2112])
            p.barrier()
            nc.vector.tensor_copy(out=bmb[:, k], in_=stgf[:, 2048:2112])
            p.barrier()
        nc.sync.dma_start(out=xwrow[0:1, :], in_=xw_ap[0]).then_inc(ld, 16)
        nc.sync.dma_start(out=xac[:, :], in_=xa_ap[0]).then_inc(ld, 16)
        p.barrier()

        with nc.Fori(0, T_steps) as i:
            nc.sync.wait_ge(spe, i * 4 + 1)
            nc.sync.dma_start(out=xwrow[0:1, :], in_=xw_ap[i + 1]).then_inc(ld, 16)
            nc.sync.wait_ge(sdve, i * 8 + 1)
            nc.sync.dma_start(out=xac[:, :], in_=xa_ap[i + 1]).then_inc(ld, 16)
            nc.tensor.wait_ge(sdve, i * 8)
            nc.tensor.wait_ge(ld, i * 32 + 32)
            for k in range(8):
                mm(nc, Zp[0:1, 0:2048], hT[:, k:k + 1], u0b[:, k],
                   start=(k == 0), stop=False, tile_position=(0, 0))
                mm(nc, Zp[32:33, 0:2048], hT[:, k:k + 1], u1b[:, k, 0:2048],
                   start=(k == 0), stop=False, tile_position=(0, 32))
            mm(nc, Zp[0:1, 0:2048], one1[0:1, :], xwrow[0:1, 0:2048],
               start=False, stop=True, tile_position=(0, 0))
            for j in range(4):
                cols = slice(j * 512, (j + 1) * 512)
                ins = nc.tensor.matmul(Zp[32:33, cols], one1[0:1, :],
                                       xwrow[0:1, 2048 + j * 512:2048 + (j + 1) * 512],
                                       start=False, stop=True,
                                       skip_group_check=True,
                                       tile_position=(0, 32))
            ins.then_inc(spe, 1)
            for k in range(8):
                ins = nc.tensor.matmul(Zp[64:128, 2047:2048], bmb[:, k],
                                       hT[:, k:k + 1], start=(k == 0),
                                       stop=(k == 7), skip_group_check=True,
                                       tile_position=(0, 64))
            ins.then_inc(spe, 1)
            nc.tensor.wait_ge(sdve, i * 8 + 1)
            ins = None
            for j in range(2):
                ins = nc.tensor.matmul(psM[0:1, j * 512:(j + 1) * 512], rb[:, 0:1],
                                       pTb[:, j * 512:(j + 1) * 512],
                                       start=True, stop=True, skip_group_check=True)
            ins.then_inc(spe, 1)
            for dk in range(12):
                mm(nc, scr[0:1, :], hT[:, 0:1],
                   u0b[:, dk % 8, (dk % 4) * 512:(dk % 4) * 512 + 512],
                   start=True, stop=True)
            nc.tensor.wait_ge(sdve, i * 8 + 7)
            for k in range(8):
                ins = nc.tensor.transpose(pst[:, k, 0:1],
                                          hb[0:1, k * 128:(k + 1) * 128],
                                          ident[0:1, 0:1])
            ins.then_inc(spe, 1)
            nc.scalar.wait_ge(spe, i * 4 + 1)
            nc.scalar.activation(out=sif[0:1, :], in_=Zp[0:1, 0:2048],
                                 func=AF.Sigmoid).then_inc(sact, 1)
            nc.scalar.activation(out=sog[0:1, 0:1024], in_=Zp[32:33, 0:1024],
                                 func=AF.Sigmoid)
            nc.scalar.activation(out=sog[0:1, 1024:2048], in_=Zp[32:33, 1024:2048],
                                 func=AF.Tanh).then_inc(sact, 1)
            nc.scalar.wait_ge(sdve, i * 8 + 6)
            nc.scalar.activation(out=tc[0:1, :], in_=cst[0:1, :],
                                 func=AF.Tanh).then_inc(sact, 1)
            nc.vector.wait_ge(spe, i * 4 + 2)
            nc.vector.wait_ge(ld, i * 32 + 32)
            nc.vector.tensor_mul(rb[:, 0:1], xac[:, 0:1],
                                 Zp[64:128, 2047:2048]).then_inc(sdve, 1)
            nc.vector.wait_ge(sact, i * 3 + 1)
            nc.vector.tensor_mul(tm2[0:1, :], sif[0:1, 1024:2048],
                                 cst[0:1, :]).then_inc(sdve, 1)
            nc.vector.wait_ge(sact, i * 3 + 2)
            nc.vector.tensor_mul(tm1[0:1, :], sif[0:1, 0:1024],
                                 sog[0:1, 1024:2048]).then_inc(sdve, 1)
            nc.vector.wait_ge(sdve, i * 8 + 3)
            nc.vector.tensor_add(tm1[0:1, :], tm1[0:1, :],
                                 tm2[0:1, :]).then_inc(sdve, 1)
            nc.vector.wait_ge(sdve, i * 8 + 4)
            nc.vector.tensor_mul(tm1[0:1, :], tm1[0:1, :],
                                 alpr[0:1, :]).then_inc(sdve, 1)
            nc.vector.wait_ge(spe, i * 4 + 3)
            nc.vector.wait_ge(sdve, i * 8 + 5)
            nc.vector.scalar_tensor_tensor(out=cst[0:1, :], in0=psM[0:1, :],
                                           scalar=0.1, in1=tm1[0:1, :],
                                           op0=ALU.mult,
                                           op1=ALU.add).then_inc(sdve, 1)
            nc.vector.wait_ge(sact, i * 3 + 3)
            nc.vector.wait_ge(st, i * 16)
            nc.vector.tensor_mul(hb[0:1, :], sog[0:1, 0:1024],
                                 tc[0:1, :]).then_inc(sdve, 1)
            nc.vector.wait_ge(spe, i * 4 + 4)
            nc.vector.tensor_copy(out=hT[:, :],
                                  in_=pst[:, 0:8, 0]).then_inc(sdve, 1)
            nc.gpsimd.wait_ge(sdve, i * 8 + 7)
            nc.gpsimd.dma_start(out=sh_ap[i], in_=hb[0:1, :]).then_inc(st, 16)
        nc.sync.wait_ge(ld, 32 * (T_steps + 1))
        nc.gpsimd.wait_ge(st, 16 * T_steps)
        nc.tensor.wait_ge(sdve, 8 * T_steps)
        nc.all_engine_barrier()
    p.barrier()


def emit_premid(nc, p, pfx, zw, src_kind, src, wT, b0, b1, xwout, ident_in=None):
    import contextlib
    N = lambda s: pfx + s
    es = contextlib.ExitStack()
    with es:
        xp = es.enter_context(nc.sbuf_tensor(N("xp"), [128, 8, T], BF16))
        if src_kind == "fm":
            xfull = src
            with (
                nc.sbuf_tensor(N("stgi"), [128, S], F32) as stgi,
                nc.sbuf_tensor(N("sq"), [128, S], BF16) as sq,
                nc.sbuf_tensor(N("rowr"), [1, S], F32) as rowr,
                nc.sbuf_tensor(N("rowmr"), [1, S], F32) as rowmr,
                nc.sbuf_tensor(N("rowbf"), [1, S], BF16) as rowbf,
            ):
                for half in range(2):
                    hcols = slice(half * S, (half + 1) * S)
                    for c in range(8):
                        p.dma(nc.sync, stgi[:, :], xfull[c, :, hcols])
                        p.barrier()
                        nc.vector.tensor_copy(out=xp[:, c, hcols], in_=stgi[:, :])
                        p.barrier()
                    emit_ln_fm(p, None, sq, rowr[0:1], rowmr[0:1], S,
                               x_src_chunks=[xp[:, c, hcols] for c in range(8)])
                    with (
                        nc.psum_tensor(_nm("rps"), [128, 512], F32) as rps,
                        nc.psum_tensor(_nm("mrps"), [128, 512], F32) as mrps,
                    ):
                        for tt in range(S // 512):
                            cols = slice(half * S + tt * 512, half * S + (tt + 1) * 512)
                            emit_bcast_row(p, rowr[0:1, tt * 512:(tt + 1) * 512],
                                           rps, 512, rowbf)
                            emit_bcast_row(p, rowmr[0:1, tt * 512:(tt + 1) * 512],
                                           mrps, 512, rowbf)
                            for c in range(8):
                                nc.vector.tensor_mul(xp[:, c, cols], xp[:, c, cols],
                                                     rps[:, :])
                            p.barrier()
                            for c in range(8):
                                nc.vector.tensor_sub(xp[:, c, cols], xp[:, c, cols],
                                                     mrps[:, :])
                            p.barrier()
        else:
            sh_src = src
            with (
                nc.sbuf_tensor(N("stg"), [128, 1024], BF16) as stg,
                nc.sbuf_tensor(N("stgi"), [128, 128], F32) as stgi,
                nc.sbuf_tensor(N("idb"), [128, 128], BF16) as idb,
            ):
                p.dma(nc.sync, stgi[:, 0:128], ident_in[:, :])
                p.barrier()
                nc.vector.tensor_copy(out=idb[:, :], in_=stgi[:, 0:128])
                p.barrier()
                for tt in range(T // 128):
                    p.dma(nc.sync, stg[:, :], sh_src[tt * 128:(tt + 1) * 128, :])
                    p.barrier()
                    with nc.psum_tensor(_nm("tps"), [128, 8, 128], BF16) as tps:
                        for dc in range(8):
                            nc.tensor.transpose(tps[:, dc],
                                                stg[:, dc * 128:(dc + 1) * 128],
                                                idb[:, :])
                        p.barrier()
                        for dc in range(8):
                            nc.vector.tensor_copy(
                                out=xp[:, dc, tt * 128:(tt + 1) * 128],
                                in_=tps[:, dc])
                        p.barrier()
        wTb = es.enter_context(nc.sbuf_tensor(N("wTb"), [128, 8, zw], BF16))
        brf = es.enter_context(nc.sbuf_tensor(N("brf"), [1, 4096], BF16))
        onec = es.enter_context(nc.sbuf_tensor(N("onec"), [1, 128], BF16))
        xwt = es.enter_context(nc.sbuf_tensor(N("xwt"), [128, zw], BF16))
        nc.vector.memset(onec[:, :], 1.0)
        with (
            nc.sbuf_tensor(N("stgw"), [128, zw], F32) as stgw,
            nc.sbuf_tensor(N("b1row"), [1, 4096], F32) as b1row,
        ):
            p.dma(nc.sync, stgw[0:1, 0:4096], b0[:, :])
            p.dma(nc.sync, b1row[0:1, :], b1[:, :])
            p.barrier()
            nc.vector.tensor_add(stgw[0:1, 0:4096], stgw[0:1, 0:4096],
                                 b1row[0:1, :])
            p.barrier()
            nc.vector.tensor_copy(out=brf[0:1, :], in_=stgw[0:1, 0:4096])
            p.barrier()
            for dc in range(8):
                p.dma(nc.sync, stgw[:, 0:zw], wT[dc])
                p.barrier()
                nc.vector.tensor_copy(out=wTb[:, dc], in_=stgw[:, 0:zw])
                p.barrier()
        for tt in range(T // 128):
            ts = slice(tt * 128, (tt + 1) * 128)
            with nc.psum_tensor(_nm("gps"), [128, 8, 512], F32) as ps:
                for zc in range(8):
                    zs = slice(zc * 512, (zc + 1) * 512)
                    for dc in range(8):
                        nc.tensor.matmul(ps[:, zc], xp[:, dc, ts], wTb[:, dc, zs],
                                         start=(dc == 0), stop=False,
                                         skip_group_check=True)
                    nc.tensor.matmul(ps[:, zc], onec[0:1, :], brf[0:1, zs],
                                     start=False, stop=True,
                                     skip_group_check=True)
                p.barrier()
                for zc in range(8):
                    nc.vector.tensor_copy(out=xwt[:, zc * 512:(zc + 1) * 512],
                                          in_=ps[:, zc])
                p.barrier()
            if zw > 4096:
                with nc.psum_tensor(_nm("xps"), [128, zw - 4096], F32) as ps2:
                    for dc in range(8):
                        nc.tensor.matmul(ps2[:, :], xp[:, dc, ts], wTb[:, dc, 4096:zw],
                                         start=(dc == 0), stop=(dc == 7),
                                         skip_group_check=True)
                    p.barrier()
                    nc.vector.tensor_copy(out=xwt[:, 4096:zw], in_=ps2[:, :])
                    p.barrier()
            p.dma(nc.sync, xwout[ts], xwt[:, :])
            p.barrier()


def emit_post(nc, p, pfx, xin, hm_src, ident_in, xout_ap):
    import contextlib
    N = lambda s: pfx + s
    es = contextlib.ExitStack()
    with es:
        ymf = es.enter_context(nc.sbuf_tensor(N("ymf"), [128, 8, S], BF16))
        idb = es.enter_context(nc.sbuf_tensor(N("idb"), [128, 128], BF16))
        stg = es.enter_context(nc.sbuf_tensor(N("stg"), [128, 1024], BF16))
        stf = es.enter_context(nc.sbuf_tensor(N("stf"), [128, 1024], F32))
        sq = es.enter_context(nc.sbuf_tensor(N("sq"), [128, S], BF16))
        rowr = es.enter_context(nc.sbuf_tensor(N("rowr"), [1, S], F32))
        rowmr = es.enter_context(nc.sbuf_tensor(N("rowmr"), [1, S], F32))
        rowbf = es.enter_context(nc.sbuf_tensor(N("rowbf"), [1, S], BF16))
        xst = es.enter_context(nc.sbuf_tensor(N("xst"), [128, S // 2], F32))
        p.dma(nc.sync, stf[:, 0:128], ident_in[:, :])
        p.barrier()
        nc.vector.tensor_copy(out=idb[:, :], in_=stf[:, 0:128])
        p.barrier()
        for tt in range(S // 128):
            p.dma(nc.sync, stg[:, :], hm_src[tt * 128:(tt + 1) * 128, :])
            p.barrier()
            with nc.psum_tensor(_nm("tps"), [128, 8, 128], BF16) as tps:
                for dc in range(8):
                    nc.tensor.transpose(tps[:, dc], stg[:, dc * 128:(dc + 1) * 128],
                                        idb[:, :])
                p.barrier()
                for dc in range(8):
                    nc.vector.tensor_copy(out=ymf[:, dc, tt * 128:(tt + 1) * 128],
                                          in_=tps[:, dc])
                p.barrier()
        emit_ln_fm(p, None, sq, rowr[0:1], rowmr[0:1], S,
                   x_src_chunks=[ymf[:, c] for c in range(8)])
        with (
            nc.psum_tensor(N("rps2"), [128, S // 2], F32) as rps2,
            nc.psum_tensor(N("mrps2"), [128, S // 2], F32) as mrps2,
        ):
            for half in range(2):
                cols = slice(half * (S // 2), (half + 1) * (S // 2))
                emit_bcast_row(p, rowr[0:1, cols], rps2, S // 2, rowbf)
                emit_bcast_row(p, rowmr[0:1, cols], mrps2, S // 2, rowbf)
                with nc.sbuf_tensor(_nm("swo"), [128, S // 2], F32) as tmp:
                    for c in range(8):
                        p.dma(nc.sync, xst[:, :],
                              xin[c].rearrange("p (h t) -> p h t", h=2)[:, half])
                        p.barrier()
                        nc.vector.tensor_mul(tmp[:, :], ymf[:, c, cols], rps2[:, :])
                        p.barrier()
                        nc.vector.tensor_sub(tmp[:, :], tmp[:, :], mrps2[:, :])
                        p.barrier()
                        nc.vector.tensor_add(tmp[:, :], tmp[:, :], xst[:, :])
                        p.barrier()
                        p.dma(nc.sync,
                              xout_ap[c].rearrange("p (h t) -> p h t", h=2)[:, half],
                              tmp[:, :])
                        p.barrier()


def emit_l0attn(nc, p, pfx, xin, wqp, wdn, wkup, wvup, wop, rc2, rs2, yp):
    import contextlib
    N = lambda s: pfx + s
    NT = T // 128
    es = contextlib.ExitStack()
    with es:
        qT = es.enter_context(nc.sbuf_tensor(N("qT"), [128, 4, T], BF16))
        kT = es.enter_context(nc.sbuf_tensor(N("kT"), [128, 4, T], BF16))
        vh = es.enter_context(nc.sbuf_tensor(N("vh"), [128, NT, 8, 65], BF16))
        OT = es.enter_context(nc.sbuf_tensor(N("OT"), [128, 4, T], BF16))
        nb = es.enter_context(nc.sbuf_tensor(N("nb"), [128, 1], BF16))
        onec = es.enter_context(nc.sbuf_tensor(N("onec"), [1, 128], BF16))
        onek = es.enter_context(nc.sbuf_tensor(N("onek"), [128, 1], BF16))
        nc.vector.memset(onec[:, :], 1.0)
        nc.vector.memset(onek[:, :], 1.0)
        nc.vector.memset(vh.ap().rearrange("p a b c -> p (a b c)")[:, :], 0.0)
        nc.vector.memset(
            vh.ap().rearrange("p a b c -> p (a b) c", c=65)[:, :, 64:65], 1.0)
        p.barrier()

        with nc.sbuf_tensor(N("latT"), [128, 4, T], BF16) as latT:
            with (
                nc.sbuf_tensor(N("wqb"), [128, 8, 512], BF16) as wqb,
                nc.sbuf_tensor(N("wdnb"), [128, 8, 512], BF16) as wdnb,
                nc.sbuf_tensor(N("stgf"), [128, 2048], F32) as stgf,
                nc.sbuf_tensor(N("sqb"), [128, 512], BF16) as sqb,
                nc.sbuf_tensor(N("rowr"), [1, 512], F32) as rowr,
                nc.sbuf_tensor(N("rowmr"), [1, 512], F32) as rowmr,
                nc.sbuf_tensor(N("rowbf"), [1, 512], BF16) as rowbf,
            ):
                for wsrc, wdst in ((wqp, wqb), (wdn, wdnb)):
                    for c2 in range(2):
                        p.dma(nc.sync,
                              stgf.ap().rearrange("p (a b) -> p a b", a=4),
                              wsrc[c2 * 4:(c2 + 1) * 4].rearrange("a p b -> p a b"))
                        p.barrier()
                        nc.vector.tensor_copy(
                            out=wdst[:, c2 * 4:(c2 + 1) * 4],
                            in_=stgf.ap().rearrange("p (a b) -> p a b", a=4))
                        p.barrier()
                for tq8 in range(8):
                    tcols = slice(tq8 * 512, (tq8 + 1) * 512)
                    with nc.sbuf_tensor(_nm("xpq"), [128, 8, 512], BF16) as xp:
                        for c in range(8):
                            p.dma(nc.sync, stgf[:, 0:512], xin[c, :, tcols])
                            p.barrier()
                            nc.vector.tensor_copy(out=xp[:, c], in_=stgf[:, 0:512])
                            p.barrier()
                        emit_ln_fm(p, None, sqb, rowr[0:1], rowmr[0:1], 512,
                                   x_src_chunks=[xp[:, c] for c in range(8)])
                        with (
                            nc.psum_tensor(_nm("rps"), [128, 512], F32) as rps,
                            nc.psum_tensor(_nm("mrps"), [128, 512], F32) as mrps,
                        ):
                            emit_bcast_row(p, rowr[0:1, 0:512], rps, 512, rowbf)
                            emit_bcast_row(p, rowmr[0:1, 0:512], mrps, 512, rowbf)
                            for c in range(8):
                                nc.vector.tensor_mul(xp[:, c], xp[:, c], rps[:, :])
                            p.barrier()
                            for c in range(8):
                                nc.vector.tensor_sub(xp[:, c], xp[:, c], mrps[:, :])
                            p.barrier()
                        for wdst, dst in ((wqb, qT), (wdnb, latT)):
                            with nc.psum_tensor(_nm("qps"), [128, 4, 512], F32) as ps:
                                for oc in range(4):
                                    for c in range(8):
                                        nc.tensor.matmul(
                                            ps[:, oc],
                                            wdst[:, c, oc * 128:(oc + 1) * 128],
                                            xp[:, c],
                                            start=(c == 0), stop=(c == 7),
                                            skip_group_check=True)
                                p.barrier()
                                for oc in range(4):
                                    nc.vector.tensor_copy(
                                        out=dst[:, oc, tcols], in_=ps[:, oc])
                                p.barrier()
            with (
                nc.sbuf_tensor(N("wkb"), [128, 4, 512], BF16) as wkb,
                nc.sbuf_tensor(N("wvb"), [128, 4, 512], BF16) as wvb,
                nc.sbuf_tensor(N("stg2"), [128, 2048], F32) as stg2,
            ):
                for wsrc, wdst in ((wkup, wkb), (wvup, wvb)):
                    p.dma(nc.sync,
                          stg2.ap().rearrange("p (a b) -> p a b", a=4),
                          wsrc[:, :, :].rearrange("a p b -> p a b"))
                    p.barrier()
                    nc.vector.tensor_copy(
                        out=wdst[:, :, :],
                        in_=stg2.ap().rearrange("p (a b) -> p a b", a=4))
                    p.barrier()
                with nc.psum_tensor(_nm("kps"), [128, 4, 512], F32) as ps:
                    for oc in range(4):
                        for tt in range(8):
                            cols = slice(tt * 512, (tt + 1) * 512)
                            for c in range(4):
                                nc.tensor.matmul(ps[:, tt % 4],
                                                 wkb[:, c, oc * 128:(oc + 1) * 128],
                                                 latT[:, c, cols],
                                                 start=(c == 0), stop=(c == 3),
                                                 skip_group_check=True)
                            p.barrier()
                            nc.vector.tensor_copy(out=kT[:, oc, cols], in_=ps[:, tt % 4])
                            p.barrier()
                with nc.psum_tensor(_nm("vps"), [128, 512], F32) as ps:
                    for tk in range(NT):
                        cols = slice(tk * 128, (tk + 1) * 128)
                        for c in range(4):
                            nc.tensor.matmul(ps[:, :], latT[:, c, cols], wvb[:, c, :],
                                             start=(c == 0), stop=(c == 3),
                                             skip_group_check=True)
                        p.barrier()
                        nc.vector.tensor_copy(
                            out=vh[:, tk, :, 0:64],
                            in_=ps.ap().rearrange("p (a b) -> p a b", a=8))
                        p.barrier()
        with (
            nc.sbuf_tensor(N("cosb"), [128, T], BF16) as cosb,
            nc.sbuf_tensor(N("sinb"), [128, T], BF16) as sinb,
            nc.sbuf_tensor(N("stg3"), [128, T], F32) as stg3,
            nc.sbuf_tensor(N("rtmp"), [64, 4, 2048], BF16) as rtmp,
        ):
            p.dma(nc.sync, stg3[:, :], rc2[:, :])
            p.barrier()
            nc.vector.tensor_copy(out=cosb[:, :], in_=stg3[:, :])
            p.barrier()
            p.dma(nc.sync, stg3[:, :], rs2[:, :])
            p.barrier()
            nc.vector.tensor_copy(out=sinb[:, :], in_=stg3[:, :])
            p.barrier()
            for buf in (qT, kT):
                for oc in range(4):
                    for th in range(2):
                        cols = slice(th * 2048, (th + 1) * 2048)
                        for hl in range(2):
                            E = buf[64 * hl:64 * hl + 32, oc, cols]
                            O = buf[64 * hl + 32:64 * hl + 64, oc, cols]
                            cE = cosb[64 * hl:64 * hl + 32, cols]
                            cO = cosb[64 * hl + 32:64 * hl + 64, cols]
                            sE = sinb[64 * hl:64 * hl + 32, cols]
                            sO = sinb[64 * hl + 32:64 * hl + 64, cols]
                            nc.vector.tensor_mul(rtmp[0:32, 0], E, cE)
                            nc.vector.tensor_mul(rtmp[0:32, 1], O, sO)
                            nc.vector.tensor_mul(rtmp[0:32, 2], E, sE)
                            nc.vector.tensor_mul(rtmp[0:32, 3], O, cO)
                            p.barrier()
                            nc.vector.tensor_sub(E, rtmp[0:32, 0], rtmp[0:32, 1])
                            nc.vector.tensor_add(O, rtmp[0:32, 2], rtmp[0:32, 3])
                            p.barrier()
        with (
            nc.sbuf_tensor(N("sq4"), [128, T], BF16) as sq4,
            nc.sbuf_tensor(N("srow"), [1, T], F32) as srow,
            nc.sbuf_tensor(N("sml"), [1, 8], F32) as sml,
            nc.sbuf_tensor(N("smlb"), [1, 1], BF16) as smlb,
        ):
          with nc.psum_tensor(_nm("nps"), [1, T], F32) as nps:
              for idx, buf in ((0, qT), (1, kT)):
                  for oc in range(4):
                      nc.scalar.activation(out=sq4[:, :], in_=buf[:, oc], func=AF.Square)
                      p.barrier()
                      for j in range(T // 512):
                          cols = slice(j * 512, (j + 1) * 512)
                          nc.tensor.matmul(nps[0:1, cols], onek[:, 0:1],
                                           sq4[:, cols], start=(oc == 0), stop=(oc == 3),
                                           skip_group_check=True)
                      p.barrier()
                  nc.vector.reduce_max(sml[0:1, idx:idx + 1], nps[0:1, :],
                                       axis=mybir.AxisListType.X)
                  p.barrier()
              nc.scalar.activation(out=sml[0:1, 2:4], in_=sml[0:1, 0:2], func=AF.Sqrt)
              p.barrier()
              nc.vector.tensor_mul(sml[0:1, 4:5], sml[0:1, 2:3], sml[0:1, 3:4])
              p.barrier()
              nc.vector.tensor_scalar(out=sml[0:1, 5:6], in0=sml[0:1, 4:5],
                                      scalar1=-0.125, scalar2=60.0,
                                      op0=ALU.mult, op1=ALU.add)
              p.barrier()
          nc.vector.tensor_copy(out=smlb[0:1, :], in_=sml[0:1, 5:6])
          p.barrier()
          with nc.psum_tensor(_nm("nbp"), [128, 1], F32) as nbp:
              nc.tensor.matmul(nbp[:, :], onec[0:1, :], smlb[0:1, :],
                               start=True, stop=True)
              p.barrier()
              nc.vector.tensor_copy(out=nb[:, :], in_=nbp[:, :])
              p.barrier()

        sem_pe = nc.alloc_semaphore(N("att_pe"))
        sem_act = nc.alloc_semaphore(N("att_act"))
        cnt_pe = 0
        cnt_act = 0
        with (
            nc.sbuf_tensor(N("ebuf"), [128, 2, 32, 512], BF16) as ebuf,
            nc.sbuf_tensor(N("r512"), [1, 512], F32) as r512,
            nc.sbuf_tensor(N("pvs"), [64, 512], F32) as pvs,
            nc.sbuf_tensor(N("one64f"), [1, 64], F32) as one64f,
        ):
            nc.vector.memset(one64f[:, :], 1.0)
            p.barrier()
            for g in range(8):
                ntf = 4 * g + 2
                gc = slice(g * 512, (g + 1) * 512)
                for pair in range(4):
                    with (
                        nc.psum_tensor(_nm("sA"), [128, 2, 512], F32) as sA,
                        nc.psum_tensor(_nm("sB"), [128, 2, 512], F32) as sB,
                    ):
                        for hloc in range(2):
                            for tkb in (ntf, ntf + 1):
                                nc.vector.memset(ebuf[:, hloc, tkb, 0:256], 0.0)
                        p.barrier()
                        for tk in range(ntf + 2):
                            par = tk % 2
                            tkc = slice(tk * 128, (tk + 1) * 128)
                            bnd = tk >= ntf
                            if tk >= 2:
                                nc.tensor.wait_ge(sem_act, cnt_act + tk - 1)
                            nc.tensor.matmul(sA[:, par], kT[0:64, pair, tkc],
                                             qT[0:64, pair, gc], start=True, stop=True,
                                             skip_group_check=True)
                            nc.tensor.matmul(sB[:, par], kT[64:128, pair, tkc],
                                             qT[64:128, pair, gc], start=True, stop=True,
                                             skip_group_check=True,
                                             tile_position=(64, 0)).then_inc(sem_pe, 1)
                            if True:
                                nc.scalar.wait_ge(sem_pe, cnt_pe + tk + 1)
                            ecols = slice(256, 512) if bnd else slice(0, 512)
                            nc.scalar.activation(out=ebuf[:, 0, tk, ecols],
                                                 in_=sA[:, par, ecols], func=AF.Exp,
                                                 bias=nb[:, 0:1], scale=0.125)
                            nc.scalar.activation(out=ebuf[:, 1, tk, ecols],
                                                 in_=sB[:, par, ecols], func=AF.Exp,
                                                 bias=nb[:, 0:1],
                                                 scale=0.125).then_inc(sem_act, 1)
                        cnt_pe += ntf + 2
                        cnt_act += ntf + 2
                        p.barrier()
                    for hloc in range(2):
                        h = pair * 2 + hloc
                        with (
                            nc.psum_tensor(_nm("pv"), [128, 512], F32) as pv,
                            nc.psum_tensor(_nm("rbc"), [64, 512], F32) as rbc,
                        ):
                            for tk in range(ntf + 2):
                                nc.tensor.matmul(pv[0:65, :], vh[:, tk, h],
                                                 ebuf[:, hloc, tk], start=(tk == 0),
                                                 stop=(tk == ntf + 1),
                                                 skip_group_check=True)
                            p.barrier()
                            nc.vector.reciprocal(r512[0:1, :], pv[64:65, :])
                            p.barrier()
                            nc.tensor.matmul(rbc[:, :], one64f[0:1, :], r512[0:1, :],
                                             start=True, stop=True)
                            nc.vector.tensor_copy(out=pvs[:, :], in_=pv[0:64, :])
                            p.barrier()
                            nc.vector.tensor_mul(
                                OT[64 * hloc:64 * (hloc + 1), pair, gc],
                                pvs[:, :], rbc[:, :])
                            p.barrier()
        with (
            nc.sbuf_tensor(N("wob"), [128, 4, D], BF16) as wob,
            nc.sbuf_tensor(N("stg6"), [128, 2048], F32) as stg6,
            nc.sbuf_tensor(N("yst"), [128, 512], F32) as yst,
        ):
            for c2 in range(2):
                p.dma(nc.sync,
                      stg6.ap().rearrange("p (a b) -> p a b", a=2),
                      wop[c2 * 2:(c2 + 1) * 2].rearrange("a p b -> p a b"))
                p.barrier()
                nc.vector.tensor_copy(
                    out=wob[:, c2 * 2:(c2 + 1) * 2],
                    in_=stg6.ap().rearrange("p (a b) -> p a b", a=2))
                p.barrier()
            with nc.psum_tensor(_nm("yps"), [128, 512], F32) as ps:
                for dc in range(8):
                    for tt in range(T // 512):
                        cols = slice(tt * 512, (tt + 1) * 512)
                        for oc in range(4):
                            nc.tensor.matmul(ps[:, :],
                                             wob[:, oc, dc * 128:(dc + 1) * 128],
                                             OT[:, oc, cols], start=(oc == 0),
                                             stop=(oc == 3), skip_group_check=True)
                        p.barrier()
                        nc.vector.tensor_copy(out=yst[:, :], in_=ps[:, :])
                        p.barrier()
                        p.dma(nc.sync, yp[dc, :, cols], yst[:, :])
                        p.barrier()


def emit_comb(nc, p, pfx, xin, ya_chunks, yb_chunks, xout_ap):
    import contextlib
    N = lambda s: pfx + s
    es = contextlib.ExitStack()
    with es:
        ysum = es.enter_context(nc.sbuf_tensor(N("ysum"), [128, 8, S], BF16))
        stga = es.enter_context(nc.sbuf_tensor(N("stga"), [128, S], F32))
        stgb = es.enter_context(nc.sbuf_tensor(N("stgb"), [128, S], F32))
        sq = es.enter_context(nc.sbuf_tensor(N("sq"), [128, S], BF16))
        rowr = es.enter_context(nc.sbuf_tensor(N("rowr"), [1, S], F32))
        rowmr = es.enter_context(nc.sbuf_tensor(N("rowmr"), [1, S], F32))
        rowbf = es.enter_context(nc.sbuf_tensor(N("rowbf"), [1, S], BF16))
        for c in range(8):
            p.dma(nc.sync, stga[:, :], ya_chunks(c))
            p.dma(nc.sync, stgb[:, :], yb_chunks(c))
            p.barrier()
            nc.vector.tensor_add(ysum[:, c], stga[:, :], stgb[:, :])
            p.barrier()
        emit_ln_fm(p, None, sq, rowr[0:1], rowmr[0:1], S,
                   x_src_chunks=[ysum[:, c] for c in range(8)])
        with (
            nc.psum_tensor(N("rps"), [128, S // 2], F32) as rps,
            nc.psum_tensor(N("mrps"), [128, S // 2], F32) as mrps,
        ):
            for half in range(2):
                cols = slice(half * (S // 2), (half + 1) * (S // 2))
                emit_bcast_row(p, rowr[0:1, cols], rps, S // 2, rowbf)
                emit_bcast_row(p, rowmr[0:1, cols], mrps, S // 2, rowbf)
                for c in range(8):
                    p.dma(nc.sync, stga[:, 0:S // 2],
                          xin[c].rearrange("p (h t) -> p h t", h=2)[:, half])
                    p.barrier()
                    nc.vector.tensor_mul(stgb[:, 0:S // 2], ysum[:, c, cols], rps[:, :])
                    p.barrier()
                    nc.vector.tensor_sub(stgb[:, 0:S // 2], stgb[:, 0:S // 2], mrps[:, :])
                    p.barrier()
                    nc.vector.tensor_add(stgb[:, 0:S // 2], stgb[:, 0:S // 2],
                                         stga[:, 0:S // 2])
                    p.barrier()
                    p.dma(nc.sync,
                          xout_ap[c].rearrange("p (h t) -> p h t", h=2)[:, half],
                          stgb[:, 0:S // 2])
                    p.barrier()


def emit_wcast(nc, p, pfx, w1, w2, w1o, w2o):
    N = lambda s: pfx + s
    with (
        nc.sbuf_tensor(N("st"), [128, 4096], F32) as st,
        nc.sbuf_tensor(N("bt"), [128, 4096], BF16) as bt,
    ):
        for c in range(8):
            p.dma(nc.sync, st[:, :], w1[c])
            p.barrier()
            nc.vector.tensor_copy(out=bt[:, :], in_=st[:, :])
            p.barrier()
            p.dma(nc.sync, w1o[c], bt[:, :])
            p.barrier()
        for c in range(32):
            p.dma(nc.sync, st[:, 0:1024], w2[c])
            p.barrier()
            nc.vector.tensor_copy(out=bt[:, 0:1024], in_=st[:, 0:1024])
            p.barrier()
            p.dma(nc.sync, w2o[c], bt[:, 0:1024])
            p.barrier()


def emit_l0fattn(nc, p, pfx, xin, wqs, wks, wvs, bq, bk, bv, l1g, l1b,
                 wop, yp):
    import contextlib
    N = lambda s: pfx + s
    NT = T // 128
    es = contextlib.ExitStack()
    with es:
        qT = es.enter_context(nc.sbuf_tensor(N("qT"), [128, 4, T], BF16))
        kT = es.enter_context(nc.sbuf_tensor(N("kT"), [128, 4, T], BF16))
        vh = es.enter_context(nc.sbuf_tensor(N("vh"), [128, NT, 8, 65], BF16))
        OT = es.enter_context(nc.sbuf_tensor(N("OT"), [128, 4, T], BF16))
        nb = es.enter_context(nc.sbuf_tensor(N("nb"), [128, 1], BF16))
        onec = es.enter_context(nc.sbuf_tensor(N("onec"), [1, 128], BF16))
        onek = es.enter_context(nc.sbuf_tensor(N("onek"), [128, 1], BF16))
        gbg = es.enter_context(nc.sbuf_tensor(N("gbg"), [128, 8], F32))
        gbb = es.enter_context(nc.sbuf_tensor(N("gbb"), [128, 8], F32))
        bqr = es.enter_context(nc.sbuf_tensor(N("bqr"), [128, 4], F32))
        bkr = es.enter_context(nc.sbuf_tensor(N("bkr"), [128, 4], F32))
        bvr = es.enter_context(nc.sbuf_tensor(N("bvr"), [1, 512], BF16))
        nc.vector.memset(onec[:, :], 1.0)
        nc.vector.memset(onek[:, :], 1.0)
        nc.vector.memset(vh.ap().rearrange("p a b c -> p (a b c)")[:, :], 0.0)
        nc.vector.memset(
            vh.ap().rearrange("p a b c -> p (a b) c", c=65)[:, :, 64:65], 1.0)
        p.barrier()
        p.dma(nc.sync, gbg[:, :], l1g[:, :])
        p.dma(nc.sync, gbb[:, :], l1b[:, :])
        p.dma(nc.sync, bqr[:, :], bq[:, :])
        p.dma(nc.sync, bkr[:, :], bk[:, :])
        p.barrier()
        with nc.sbuf_tensor(N("bvf"), [1, 512], F32) as bvf:
            p.dma(nc.sync, bvf[0:1, :], bv[:, :])
            p.barrier()
            nc.vector.tensor_copy(out=bvr[0:1, :], in_=bvf[0:1, :])
            p.barrier()

        for th2 in range(2):
          with (
            nc.sbuf_tensor(_nm("h1"), [128, 8, 2048], BF16) as h1,
            nc.sbuf_tensor(_nm("wstg"), [128, 8, 512], BF16) as wstg,
            nc.sbuf_tensor(_nm("stgf"), [128, 2048], F32) as stgf,
              nc.sbuf_tensor(_nm("sqb"), [128, 512], BF16) as sqb,
              nc.sbuf_tensor(_nm("rowr"), [1, 512], F32) as rowr,
              nc.sbuf_tensor(_nm("rowmr"), [1, 512], F32) as rowmr,
              nc.sbuf_tensor(_nm("rowbf"), [1, 512], BF16) as rowbf,
          ):
              for tq8 in range(th2 * 4, th2 * 4 + 4):
                  tcols = slice(tq8 * 512, (tq8 + 1) * 512)
                  with nc.sbuf_tensor(_nm("xpq"), [128, 8, 512], BF16) as xp:
                      for c in range(8):
                          p.dma(nc.sync, stgf[:, 0:512], xin[c, :, tcols])
                          p.barrier()
                          nc.vector.tensor_copy(out=xp[:, c], in_=stgf[:, 0:512])
                          p.barrier()
                      for rnd in range(2):
                          emit_ln_fm(p, None, sqb, rowr[0:1], rowmr[0:1], 512,
                                     x_src_chunks=[xp[:, c] for c in range(8)])
                          with (
                              nc.psum_tensor(_nm("rps"), [128, 512], F32) as rps,
                              nc.psum_tensor(_nm("mrps"), [128, 512], F32) as mrps,
                          ):
                              emit_bcast_row(p, rowr[0:1, 0:512], rps, 512, rowbf)
                              emit_bcast_row(p, rowmr[0:1, 0:512], mrps, 512, rowbf)
                              for c in range(8):
                                  nc.vector.tensor_mul(xp[:, c], xp[:, c], rps[:, :])
                              p.barrier()
                              for c in range(8):
                                  nc.vector.tensor_sub(xp[:, c], xp[:, c], mrps[:, :])
                              p.barrier()
                              if rnd == 1:
                                  for c in range(8):
                                      nc.vector.tensor_scalar(
                                          out=h1[:, c, tcols.start - th2 * 2048:tcols.stop - th2 * 2048], in0=xp[:, c],
                                          scalar1=gbg[:, c:c + 1], scalar2=gbb[:, c:c + 1],
                                          op0=ALU.mult, op1=ALU.add)
                                  p.barrier()
              for wsrc, dst, bcol in ((wqs, qT, 0), (wks, kT, 1)):
                  for c2 in range(2):
                      p.dma(nc.sync,
                            stgf.ap().rearrange("p (a b) -> p a b", a=4)[:, :, 0:512],
                            wsrc[c2 * 4:(c2 + 1) * 4].rearrange("a p b -> p a b"))
                      p.barrier()
                      nc.vector.tensor_copy(
                          out=wstg[:, c2 * 4:(c2 + 1) * 4],
                          in_=stgf.ap().rearrange("p (a b) -> p a b", a=4)[:, :, 0:512])
                      p.barrier()
                  with nc.psum_tensor(_nm("qps"), [128, 512], F32) as ps:
                      for oc in range(4):
                          for tt in range(th2 * 4, th2 * 4 + 4):
                              cols = slice(tt * 512, (tt + 1) * 512)
                              for c in range(8):
                                  nc.tensor.matmul(ps[:, :],
                                                   wstg[:, c, oc * 128:(oc + 1) * 128],
                                                   h1[:, c, cols.start - th2 * 2048:cols.stop - th2 * 2048],
                                                   start=(c == 0), stop=(c == 7),
                                                   skip_group_check=True)
                              p.barrier()
                              nc.vector.tensor_scalar_add(
                                  dst[:, oc, cols], ps[:, :],
                                  (bqr if bcol == 0 else bkr)[:, oc:oc + 1])
                              p.barrier()
              for c2 in range(2):
                  p.dma(nc.sync,
                        stgf.ap().rearrange("p (a b) -> p a b", a=4)[:, :, 0:512],
                        wvs[c2 * 4:(c2 + 1) * 4].rearrange("a p b -> p a b"))
                  p.barrier()
                  nc.vector.tensor_copy(
                      out=wstg[:, c2 * 4:(c2 + 1) * 4],
                      in_=stgf.ap().rearrange("p (a b) -> p a b", a=4)[:, :, 0:512])
                  p.barrier()
              with nc.psum_tensor(_nm("vps"), [128, 512], F32) as ps:
                  for tk in range(th2 * 16, th2 * 16 + 16):
                      cols = slice(tk * 128 - th2 * 2048, (tk + 1) * 128 - th2 * 2048)
                      for c in range(8):
                          nc.tensor.matmul(ps[:, :], h1[:, c, cols], wstg[:, c, :],
                                           start=(c == 0), stop=False,
                                           skip_group_check=True)
                      nc.tensor.matmul(ps[:, :], onec[0:1, :], bvr[0:1, :],
                                       start=False, stop=True, skip_group_check=True)
                      p.barrier()
                      nc.vector.tensor_copy(
                          out=vh[:, tk, :, 0:64],
                          in_=ps.ap().rearrange("p (a b) -> p a b", a=8))
                      p.barrier()
        with (
            nc.sbuf_tensor(N("sq4"), [128, T], BF16) as sq4,
            nc.sbuf_tensor(N("sml"), [1, 8], F32) as sml,
            nc.sbuf_tensor(N("smlb"), [1, 1], BF16) as smlb,
        ):
            with nc.psum_tensor(_nm("nps"), [1, T], F32) as nps:
                for idx, buf in ((0, qT), (1, kT)):
                    for oc in range(4):
                        nc.scalar.activation(out=sq4[:, :], in_=buf[:, oc], func=AF.Square)
                        p.barrier()
                        for j in range(T // 512):
                            cols = slice(j * 512, (j + 1) * 512)
                            nc.tensor.matmul(nps[0:1, cols], onek[:, 0:1],
                                             sq4[:, cols], start=(oc == 0), stop=(oc == 3),
                                             skip_group_check=True)
                        p.barrier()
                    nc.vector.reduce_max(sml[0:1, idx:idx + 1], nps[0:1, :],
                                         axis=mybir.AxisListType.X)
                    p.barrier()
                nc.scalar.activation(out=sml[0:1, 2:4], in_=sml[0:1, 0:2], func=AF.Sqrt)
                p.barrier()
                nc.vector.tensor_mul(sml[0:1, 4:5], sml[0:1, 2:3], sml[0:1, 3:4])
                p.barrier()
                nc.vector.tensor_scalar(out=sml[0:1, 5:6], in0=sml[0:1, 4:5],
                                        scalar1=-0.125, scalar2=60.0,
                                        op0=ALU.mult, op1=ALU.add)
                p.barrier()
            nc.vector.tensor_copy(out=smlb[0:1, :], in_=sml[0:1, 5:6])
            p.barrier()
            with nc.psum_tensor(_nm("nbp"), [128, 1], F32) as nbp:
                nc.tensor.matmul(nbp[:, :], onec[0:1, :], smlb[0:1, :],
                                 start=True, stop=True)
                p.barrier()
                nc.vector.tensor_copy(out=nb[:, :], in_=nbp[:, :])
                p.barrier()
        sem_pe = nc.alloc_semaphore(N("fat_pe"))
        sem_act = nc.alloc_semaphore(N("fat_act"))
        cnt_pe = 0
        cnt_act = 0
        with (
            nc.sbuf_tensor(N("ebuf"), [128, 2, NT, 512], BF16) as ebuf,
            nc.sbuf_tensor(N("r512"), [1, 512], F32) as r512,
            nc.sbuf_tensor(N("pvs"), [64, 512], F32) as pvs,
            nc.sbuf_tensor(N("one64f"), [1, 64], F32) as one64f,
        ):
            nc.vector.memset(one64f[:, :], 1.0)
            p.barrier()
            for g in range(8):
                gc = slice(g * 512, (g + 1) * 512)
                for pair in range(4):
                    with (
                        nc.psum_tensor(_nm("sA"), [128, 2, 512], F32) as sA,
                        nc.psum_tensor(_nm("sB"), [128, 2, 512], F32) as sB,
                    ):
                        for tk in range(NT):
                            par = tk % 2
                            tkc = slice(tk * 128, (tk + 1) * 128)
                            if tk >= 2:
                                nc.tensor.wait_ge(sem_act, cnt_act + tk - 1)
                            nc.tensor.matmul(sA[:, par], kT[0:64, pair, tkc],
                                             qT[0:64, pair, gc], start=True, stop=True,
                                             skip_group_check=True)
                            nc.tensor.matmul(sB[:, par], kT[64:128, pair, tkc],
                                             qT[64:128, pair, gc], start=True, stop=True,
                                             skip_group_check=True,
                                             tile_position=(64, 0)).then_inc(sem_pe, 1)
                            nc.scalar.wait_ge(sem_pe, cnt_pe + tk + 1)
                            nc.scalar.activation(out=ebuf[:, 0, tk], in_=sA[:, par],
                                                 func=AF.Exp, bias=nb[:, 0:1], scale=0.125)
                            nc.scalar.activation(out=ebuf[:, 1, tk], in_=sB[:, par],
                                                 func=AF.Exp, bias=nb[:, 0:1],
                                                 scale=0.125).then_inc(sem_act, 1)
                        cnt_pe += NT
                        cnt_act += NT
                        p.barrier()
                    for hloc in range(2):
                        h = pair * 2 + hloc
                        with (
                            nc.psum_tensor(_nm("pv"), [128, 512], F32) as pv,
                            nc.psum_tensor(_nm("rbc"), [64, 512], F32) as rbc,
                        ):
                            for tk in range(NT):
                                nc.tensor.matmul(pv[0:65, :], vh[:, tk, h],
                                                 ebuf[:, hloc, tk], start=(tk == 0),
                                                 stop=(tk == NT - 1),
                                                 skip_group_check=True)
                            p.barrier()
                            nc.vector.reciprocal(r512[0:1, :], pv[64:65, :])
                            p.barrier()
                            nc.tensor.matmul(rbc[:, :], one64f[0:1, :], r512[0:1, :],
                                             start=True, stop=True)
                            nc.vector.tensor_copy(out=pvs[:, :], in_=pv[0:64, :])
                            p.barrier()
                            nc.vector.tensor_mul(
                                OT[64 * hloc:64 * (hloc + 1), pair, gc],
                                pvs[:, :], rbc[:, :])
                            p.barrier()
        with (
            nc.sbuf_tensor(N("wob"), [128, 4, D], BF16) as wob,
            nc.sbuf_tensor(N("stg6"), [128, 2048], F32) as stg6,
            nc.sbuf_tensor(N("yst"), [128, 512], F32) as yst,
        ):
            for c2 in range(2):
                p.dma(nc.sync,
                      stg6.ap().rearrange("p (a b) -> p a b", a=2),
                      wop[c2 * 2:(c2 + 1) * 2].rearrange("a p b -> p a b"))
                p.barrier()
                nc.vector.tensor_copy(
                    out=wob[:, c2 * 2:(c2 + 1) * 2],
                    in_=stg6.ap().rearrange("p (a b) -> p a b", a=2))
                p.barrier()
            with nc.psum_tensor(_nm("yps"), [128, 512], F32) as ps:
                for dc in range(8):
                    for tt in range(T // 512):
                        cols = slice(tt * 512, (tt + 1) * 512)
                        for oc in range(4):
                            nc.tensor.matmul(ps[:, :],
                                             wob[:, oc, dc * 128:(dc + 1) * 128],
                                             OT[:, oc, cols], start=(oc == 0),
                                             stop=(oc == 3), skip_group_check=True)
                        p.barrier()
                        nc.vector.tensor_copy(out=yst[:, :], in_=ps[:, :])
                        p.barrier()
                        p.dma(nc.sync, yp[dc, :, cols], yst[:, :])
                        p.barrier()


def emit_l0fmoe(nc, p, pfx, xin, ya_chunks, yb_chunks, bo, l2g, l2b, gw, gbr,
                w1b, w2b, b1c, b2r, ident_in, xout_ap):
    import contextlib
    N = lambda s: pfx + s
    NE, HID, NH = 8, 4096, 32
    es = contextlib.ExitStack()
    with es:
        xa1 = es.enter_context(nc.sbuf_tensor(N("xa1"), [128, 8, S], BF16))
        h2 = es.enter_context(nc.sbuf_tensor(N("h2"), [128, 8, S], BF16))
        ybuf = es.enter_context(nc.sbuf_tensor(N("ybuf"), [128, 8, S], BF16))
        ew = es.enter_context(nc.sbuf_tensor(N("ew"), [128, 16, 8], F32))
        bor = es.enter_context(nc.sbuf_tensor(N("bor"), [128, 8], F32))
        g2r = es.enter_context(nc.sbuf_tensor(N("g2r"), [128, 8], F32))
        b2rr = es.enter_context(nc.sbuf_tensor(N("b2rr"), [128, 8], F32))
        onec = es.enter_context(nc.sbuf_tensor(N("onec"), [1, 128], BF16))
        idb = es.enter_context(nc.sbuf_tensor(N("idb"), [128, 128], BF16))
        stgf = es.enter_context(nc.sbuf_tensor(N("stgf"), [128, 1024], F32))
        stgg = es.enter_context(nc.sbuf_tensor(N("stgg"), [128, 512], F32))
        sqb = es.enter_context(nc.sbuf_tensor(N("sqb"), [128, 512], BF16))
        rowr = es.enter_context(nc.sbuf_tensor(N("rowr"), [1, 512], F32))
        rowmr = es.enter_context(nc.sbuf_tensor(N("rowmr"), [1, 512], F32))
        rowbf = es.enter_context(nc.sbuf_tensor(N("rowbf"), [1, 512], BF16))
        nc.vector.memset(onec[:, :], 1.0)
        p.dma(nc.sync, bor[:, :], bo[:, :])
        p.dma(nc.sync, g2r[:, :], l2g[:, :])
        p.dma(nc.sync, b2rr[:, :], l2b[:, :])
        p.dma(nc.sync, stgf[:, 0:128], ident_in[:, :])
        p.barrier()
        nc.vector.tensor_copy(out=idb[:, :], in_=stgf[:, 0:128])
        p.barrier()

        for tq in range(S // 512):
            tcols = slice(tq * 512, (tq + 1) * 512)
            with nc.sbuf_tensor(_nm("xq"), [128, 8, 512], BF16) as xq:
                for c in range(8):
                    p.dma(nc.sync, stgg[:, :], xin[c, :, tcols])
                    p.barrier()
                    nc.vector.tensor_copy(out=xq[:, c], in_=stgg[:, :])
                    p.barrier()
                emit_ln_fm(p, None, sqb, rowr[0:1], rowmr[0:1], 512,
                           x_src_chunks=[xq[:, c] for c in range(8)])
                with (
                    nc.psum_tensor(_nm("rps"), [128, 512], F32) as rps,
                    nc.psum_tensor(_nm("mrps"), [128, 512], F32) as mrps,
                ):
                    emit_bcast_row(p, rowr[0:1, 0:512], rps, 512, rowbf)
                    emit_bcast_row(p, rowmr[0:1, 0:512], mrps, 512, rowbf)
                    for c in range(8):
                        nc.vector.tensor_mul(xq[:, c], xq[:, c], rps[:, :])
                    p.barrier()
                    for c in range(8):
                        nc.vector.tensor_sub(xq[:, c], xq[:, c], mrps[:, :])
                    p.barrier()
                for c in range(8):
                    p.dma(nc.sync, stgg[:, :], ya_chunks(c)[:, tcols])
                    p.dma(nc.sync, stgf[:, 0:512], yb_chunks(c)[:, tcols])
                    p.barrier()
                    nc.vector.tensor_add(stgg[:, :], stgg[:, :], stgf[:, 0:512])
                    p.barrier()
                    nc.vector.tensor_scalar_add(stgg[:, :], stgg[:, :], bor[:, c:c + 1])
                    p.barrier()
                    nc.vector.tensor_add(xa1[:, c, tcols], xq[:, c], stgg[:, :])
                    p.barrier()
            with nc.sbuf_tensor(_nm("hq"), [128, 8, 512], BF16) as hq:
                for c in range(8):
                    nc.vector.tensor_copy(out=hq[:, c], in_=xa1[:, c, tcols])
                p.barrier()
                emit_ln_fm(p, None, sqb, rowr[0:1], rowmr[0:1], 512,
                           x_src_chunks=[hq[:, c] for c in range(8)])
                with (
                    nc.psum_tensor(_nm("rps"), [128, 512], F32) as rps,
                    nc.psum_tensor(_nm("mrps"), [128, 512], F32) as mrps,
                ):
                    emit_bcast_row(p, rowr[0:1, 0:512], rps, 512, rowbf)
                    emit_bcast_row(p, rowmr[0:1, 0:512], mrps, 512, rowbf)
                    for c in range(8):
                        nc.vector.tensor_mul(hq[:, c], hq[:, c], rps[:, :])
                    p.barrier()
                    for c in range(8):
                        nc.vector.tensor_sub(hq[:, c], hq[:, c], mrps[:, :])
                    p.barrier()
                    for c in range(8):
                        nc.vector.tensor_scalar(out=h2[:, c, tcols], in0=hq[:, c],
                                                scalar1=g2r[:, c:c + 1],
                                                scalar2=b2rr[:, c:c + 1],
                                                op0=ALU.mult, op1=ALU.add)
                    p.barrier()

        with (
            nc.sbuf_tensor(N("gwb"), [128, 8, 8], BF16) as gwb,
            nc.sbuf_tensor(N("gbb"), [1, 8], BF16) as gbbv,
            nc.sbuf_tensor(N("lgt"), [128, 8], F32) as lgt,
            nc.sbuf_tensor(N("m1"), [128, 1], F32) as m1,
            nc.sbuf_tensor(N("m2"), [128, 1], F32) as m2,
            nc.sbuf_tensor(N("eq1"), [128, 8], F32) as eq1,
            nc.sbuf_tensor(N("eq2"), [128, 8], F32) as eq2,
            nc.sbuf_tensor(N("gw1"), [128, 1], F32) as gw1,
            nc.sbuf_tensor(N("gw2"), [128, 1], F32) as gw2,
        ):
            for c in range(8):
                p.dma(nc.sync, stgf[:, 0:8], gw[c])
                p.barrier()
                nc.vector.tensor_copy(out=gwb[:, c], in_=stgf[:, 0:8])
                p.barrier()
            p.dma(nc.sync, stgf[0:1, 0:8], gbr[:, :])
            p.barrier()
            nc.vector.tensor_copy(out=gbbv[0:1, :], in_=stgf[0:1, 0:8])
            p.barrier()
            with nc.psum_tensor(_nm("lps"), [128, 8], F32) as lps:
                for tt in range(16):
                    ts_ = slice(tt * 128, (tt + 1) * 128)
                    for c in range(8):
                        nc.tensor.matmul(lps[:, :], h2[:, c, ts_], gwb[:, c],
                                         start=(c == 0), stop=False,
                                         skip_group_check=True)
                    nc.tensor.matmul(lps[:, :], onec[0:1, :], gbbv[0:1, :],
                                     start=False, stop=True, skip_group_check=True)
                    p.barrier()
                    nc.vector.tensor_copy(out=lgt[:, :], in_=lps[:, :])
                    p.barrier()
                    nc.vector.reduce_max(m1[:, 0:1], lgt[:, :], axis=mybir.AxisListType.X)
                    p.barrier()
                    nc.vector.tensor_scalar(out=eq1[:, :], in0=lgt[:, :],
                                            scalar1=m1[:, 0:1], scalar2=None,
                                            op0=ALU.is_equal)
                    p.barrier()
                    nc.vector.scalar_tensor_tensor(out=eq2[:, :], in0=eq1[:, :],
                                                   scalar=-1e30, in1=lgt[:, :],
                                                   op0=ALU.mult, op1=ALU.add)
                    p.barrier()
                    nc.vector.reduce_max(m2[:, 0:1], eq2[:, :], axis=mybir.AxisListType.X)
                    p.barrier()
                    nc.vector.tensor_sub(gw1[:, 0:1], m1[:, 0:1], m2[:, 0:1])
                    p.barrier()
                    nc.scalar.activation(out=gw1[:, 0:1], in_=gw1[:, 0:1],
                                         func=AF.Sigmoid)
                    p.barrier()
                    nc.vector.tensor_scalar(out=gw2[:, 0:1], in0=gw1[:, 0:1],
                                            scalar1=-1.0, scalar2=1.0,
                                            op0=ALU.mult, op1=ALU.add)
                    p.barrier()
                    nc.vector.tensor_scalar(out=eq2[:, :], in0=eq2[:, :],
                                            scalar1=m2[:, 0:1], scalar2=None,
                                            op0=ALU.is_equal)
                    p.barrier()
                    nc.vector.tensor_scalar_mul(eq1[:, :], eq1[:, :], gw1[:, 0:1])
                    nc.vector.tensor_scalar_mul(eq2[:, :], eq2[:, :], gw2[:, 0:1])
                    p.barrier()
                    nc.vector.tensor_add(ew[:, tt, :], eq1[:, :], eq2[:, :])
                    p.barrier()

        with (
            nc.sbuf_tensor(N("macc"), [128, 4, D], F32) as macc,
            nc.sbuf_tensor(N("hidb"), [128, NH, 512], BF16) as hidb,
            nc.sbuf_tensor(N("w1c"), [128, 8, 512], BF16) as w1c,
            nc.sbuf_tensor(N("w2c"), [128, 4, D], BF16) as w2c,
            nc.sbuf_tensor(N("b1s"), [128, NH], F32) as b1s,
            nc.sbuf_tensor(N("b2s"), [1, D], BF16) as b2s,
            nc.sbuf_tensor(N("mtb"), [128, D], BF16) as mtb,
        ):
            for sup in range(S // 512):
                scols = slice(sup * 512, (sup + 1) * 512)
                nc.vector.memset(macc.ap().rearrange("p a b -> p (a b)")[:, :], 0.0)
                p.barrier()
                for e in range(NE):
                    p.dma(nc.sync, b1s[:, :], b1c[e])
                    p.dma(nc.sync, stgf[0:1, 0:1024], b2r[e])
                    p.barrier()
                    nc.vector.tensor_copy(out=b2s[0:1, :], in_=stgf[0:1, 0:1024])
                    p.barrier()
                    with nc.psum_tensor(_nm("hps"), [128, 512], F32) as hps:
                        for hg in range(NH // 4):
                            p.dma(nc.sync, w1c[:, :, :],
                                  w1b[e, :, :, hg * 512:(hg + 1) * 512]
                                  .rearrange("a p b -> p a b"))
                            p.barrier()
                            for hl in range(4):
                                hc = hg * 4 + hl
                                for c in range(8):
                                    nc.tensor.matmul(hps[:, :],
                                                     w1c[:, c, hl * 128:(hl + 1) * 128],
                                                     h2[:, c, scols],
                                                     start=(c == 0), stop=(c == 7),
                                                     skip_group_check=True)
                                p.barrier()
                                nc.scalar.activation(out=hidb[:, hc], in_=hps[:, :],
                                                     func=AF.Relu, bias=b1s[:, hc:hc + 1])
                                p.barrier()
                    with nc.psum_tensor(_nm("ops"), [128, 4, 2, 512], F32) as ops:
                        for hg in range(NH // 4):
                            p.dma(nc.sync, w2c[:, :, :], w2b[e, hg * 4:(hg + 1) * 4]
                                  .rearrange("a p b -> p a b"))
                            p.barrier()
                            for hl in range(4):
                                hc = hg * 4 + hl
                                for tl in range(4):
                                    lcols = slice(tl * 128, (tl + 1) * 128)
                                    for oc in range(2):
                                        nc.tensor.matmul(ops[:, tl, oc],
                                                         hidb[:, hc, lcols],
                                                         w2c[:, hl, oc * 512:(oc + 1) * 512],
                                                         start=(hc == 0), stop=False,
                                                         skip_group_check=True)
                            p.barrier()
                        for tl in range(4):
                            for oc in range(2):
                                nc.tensor.matmul(ops[:, tl, oc], onec[0:1, 0:128],
                                                 b2s[0:1, oc * 512:(oc + 1) * 512],
                                                 start=False, stop=True,
                                                 skip_group_check=True)
                        p.barrier()
                        for tl in range(4):
                            tg = sup * 4 + tl
                            for oc in range(2):
                                nc.vector.scalar_tensor_tensor(
                                    out=macc[:, tl, oc * 512:(oc + 1) * 512],
                                    in0=ops[:, tl, oc], scalar=ew[:, tg, e:e + 1],
                                    in1=macc[:, tl, oc * 512:(oc + 1) * 512],
                                    op0=ALU.mult, op1=ALU.add)
                        p.barrier()
                for tl in range(4):
                    nc.vector.tensor_copy(out=mtb[:, :], in_=macc[:, tl])
                    p.barrier()
                    with nc.psum_tensor(_nm("tps"), [128, 8, 128], BF16) as tps:
                        for dc in range(8):
                            nc.tensor.transpose(tps[:, dc],
                                                mtb[:, dc * 128:(dc + 1) * 128],
                                                idb[:, :])
                        p.barrier()
                        gcols = slice(sup * 512 + tl * 128, sup * 512 + (tl + 1) * 128)
                        for dc in range(8):
                            nc.vector.tensor_add(ybuf[:, dc, gcols], tps[:, dc],
                                                 xa1[:, dc, gcols])
                        p.barrier()

        with (
            nc.sbuf_tensor(N("fsq"), [128, S], BF16) as fsq,
            nc.sbuf_tensor(N("fr"), [1, S], F32) as fr,
            nc.sbuf_tensor(N("fmr"), [1, S], F32) as fmr,
            nc.sbuf_tensor(N("frbf"), [1, S], BF16) as frbf,
        ):
          emit_ln_fm(p, None, fsq, fr[0:1], fmr[0:1], S,
                     x_src_chunks=[ybuf[:, c] for c in range(8)])
          with (
            nc.psum_tensor(N("rps2"), [128, 512], F32) as rps2,
            nc.psum_tensor(N("mrps2"), [128, 512], F32) as mrps2,
          ):
            for tq in range(S // 512):
                cols = slice(tq * 512, (tq + 1) * 512)
                emit_bcast_row(p, fr[0:1, cols], rps2, 512, frbf)
                emit_bcast_row(p, fmr[0:1, cols], mrps2, 512, frbf)
                for c in range(8):
                    p.dma(nc.sync, stgg[:, :], xin[c, :, cols])
                    p.barrier()
                    nc.vector.tensor_mul(stgf[:, 0:512], ybuf[:, c, cols], rps2[:, :])
                    p.barrier()
                    nc.vector.tensor_sub(stgf[:, 0:512], stgf[:, 0:512], mrps2[:, :])
                    p.barrier()
                    nc.vector.tensor_add(stgf[:, 0:512], stgf[:, 0:512], stgg[:, :])
                    p.barrier()
                    p.dma(nc.sync, xout_ap[c, :, cols], stgf[:, 0:512])
                    p.barrier()


def emit_l1ffn(nc, p, pfx, xin, wgT, wuT, wdT, xout):
    import contextlib
    N = lambda s: pfx + s
    HID = 3072
    NH = HID // 128
    SC = 1.0 / 127.0
    SH = S // 2
    es = contextlib.ExitStack()
    with es:
        xp = es.enter_context(nc.sbuf_tensor(N("xp"), [128, 8, S], BF16))
        ybuf = es.enter_context(nc.sbuf_tensor(N("ybuf"), [128, 8, S], BF16))
        wd = es.enter_context(nc.sbuf_tensor(N("wd"), [128, NH, D], BF16))
        stg = es.enter_context(nc.sbuf_tensor(N("stg"), [128, 8, 128], I32))
        sq = es.enter_context(nc.sbuf_tensor(N("sq"), [128, S], BF16))
        rowr = es.enter_context(nc.sbuf_tensor(N("rowr"), [1, S], F32))
        rowmr = es.enter_context(nc.sbuf_tensor(N("rowmr"), [1, S], F32))
        rowbf = es.enter_context(nc.sbuf_tensor(N("rowbf"), [1, S], BF16))
        xst = es.enter_context(nc.sbuf_tensor(N("xst"), [128, S], F32))
        whc = es.enter_context(nc.sbuf_tensor(N("whc"), [128, 2, 8, 128], BF16))
        gbuf = es.enter_context(nc.sbuf_tensor(N("gbuf"), [128, SH], BF16))
        with nc.sbuf_tensor(N("xb"), [128, 8, S], BF16) as xb:
            for c in range(8):
                p.dma(nc.sync, xst[:, :], xin[c])
                p.barrier()
                nc.vector.tensor_copy(out=xb[:, c], in_=xst[:, :])
                p.barrier()
            emit_ln_fm(p, None, sq, rowr[0:1], rowmr[0:1], S,
                       x_src_chunks=[xb[:, c] for c in range(8)])
            with (
                nc.psum_tensor(N("rps"), [128, 512], F32) as rps,
                nc.psum_tensor(N("mrps"), [128, 512], F32) as mrps,
            ):
                for tt in range(S // 512):
                    cols = slice(tt * 512, (tt + 1) * 512)
                    emit_bcast_row(p, rowr[0:1, cols], rps, 512, rowbf)
                    emit_bcast_row(p, rowmr[0:1, cols], mrps, 512, rowbf)
                    for c in range(8):
                        nc.vector.tensor_mul(xp[:, c, cols], xb[:, c, cols], rps[:, :])
                    p.barrier()
                    for c in range(8):
                        nc.vector.tensor_sub(xp[:, c, cols], xp[:, c, cols], mrps[:, :])
                    p.barrier()

        with nc.sbuf_tensor(N("hbuf"), [128, NH, SH], BF16) as hbuf:
            for hc in range(NH):
                p.dma(nc.sync, stg.ap().rearrange("p a b -> p (a b)")[:, 0:D], wdT[hc])
                p.barrier()
                nc.vector.tensor_copy(out=wd[:, hc],
                                      in_=stg.ap().rearrange("p a b -> p (a b)")[:, 0:D])
                p.barrier()
            for half in range(2):
                cols = slice(half * SH, (half + 1) * SH)
                with (
                    nc.psum_tensor(_nm("gps"), [128, SH], F32) as gps,
                    nc.psum_tensor(_nm("ups"), [128, SH], F32) as ups,
                ):
                    for hc in range(NH):
                        hs = slice(hc * 128, (hc + 1) * 128)
                        p.dma(nc.sync, stg[:, :, :], wgT[:, :, hs].rearrange("a p b -> p a b"))
                        p.barrier()
                        nc.vector.tensor_copy(out=whc[:, 0], in_=stg[:, :, :])
                        p.barrier()
                        p.dma(nc.sync, stg[:, :, :], wuT[:, :, hs].rearrange("a p b -> p a b"))
                        p.barrier()
                        nc.vector.tensor_copy(out=whc[:, 1], in_=stg[:, :, :])
                        p.barrier()
                        for dc in range(8):
                            mm(nc, gps[:, :], whc[:, 0, dc], xp[:, dc, cols],
                               start=(dc == 0), stop=(dc == 7))
                            mm(nc, ups[:, :], whc[:, 1, dc], xp[:, dc, cols],
                               start=(dc == 0), stop=(dc == 7))
                        p.barrier()
                        nc.scalar.activation(out=gbuf[:, :], in_=gps[:, :],
                                             func=AF.Silu, scale=SC)
                        nc.scalar.activation(out=hbuf[:, hc], in_=ups[:, :],
                                             func=AF.Copy, scale=SC * SC)
                        p.barrier()
                        nc.vector.tensor_mul(hbuf[:, hc], gbuf[:, :], hbuf[:, hc])
                    p.barrier()
                for dq in range(2):
                    with nc.psum_tensor(_nm("dps"), [128, 4, SH], F32) as dps:
                        for i in range(4):
                            dc = dq * 4 + i
                            for hc in range(NH):
                                mm(nc, dps[:, i], wd[:, hc, dc * 128:(dc + 1) * 128],
                                   hbuf[:, hc], start=(hc == 0), stop=(hc == NH - 1))
                        p.barrier()
                        for i in range(4):
                            nc.vector.tensor_copy(out=ybuf[:, dq * 4 + i, cols],
                                                  in_=dps[:, i])
                        p.barrier()

        emit_ln_fm(p, None, sq, rowr[0:1], rowmr[0:1], S,
                   x_src_chunks=[ybuf[:, c] for c in range(8)])
        with (
            nc.psum_tensor(N("rps2"), [128, SH], F32) as rps2,
            nc.psum_tensor(N("mrps2"), [128, SH], F32) as mrps2,
        ):
            for half in range(2):
                cols = slice(half * SH, (half + 1) * SH)
                emit_bcast_row(p, rowr[0:1, cols], rps2, SH, rowbf)
                emit_bcast_row(p, rowmr[0:1, cols], mrps2, SH, rowbf)
                with nc.sbuf_tensor(_nm("swo"), [128, SH], F32) as tmp:
                    for c in range(8):
                        p.dma(nc.sync, xst[:, 0:SH],
                              xin[c].rearrange("p (h t) -> p h t", h=2)[:, half])
                        p.barrier()
                        nc.vector.tensor_mul(tmp[:, :], ybuf[:, c, cols], rps2[:, :])
                        p.barrier()
                        nc.vector.tensor_sub(tmp[:, :], tmp[:, :], mrps2[:, :])
                        p.barrier()
                        nc.vector.tensor_add(tmp[:, :], tmp[:, :], xst[:, 0:SH])
                        p.barrier()
                        p.dma(nc.sync,
                              xout[c].rearrange("p (h t) -> p h t", h=2)[:, half],
                              tmp[:, :])
                        p.barrier()
        p.barrier()


def build_full(upto=None):
    nc = bass.Bass("TRN2", target_bir_lowering=False)
    EI, EO = "ExternalInput", "ExternalOutput"
    a_xin = nc.dram_tensor("a_xin", [8, 128, T], F32, kind=EI).ap()
    a_wqp = nc.dram_tensor("a_wqp", [8, 128, 512], F32, kind=EI).ap()
    a_wdn = nc.dram_tensor("a_wdn", [8, 128, 512], F32, kind=EI).ap()
    a_wkup = nc.dram_tensor("a_wkup", [4, 128, 512], F32, kind=EI).ap()
    a_wvup = nc.dram_tensor("a_wvup", [4, 128, 512], F32, kind=EI).ap()
    a_wop = nc.dram_tensor("a_wop", [4, 128, D], F32, kind=EI).ap()
    rc2 = nc.dram_tensor("rc2", [128, T], F32, kind=EI).ap()
    rs2 = nc.dram_tensor("rs2", [128, T], F32, kind=EI).ap()
    x_tok = nc.dram_tensor("x_tok", [8, 128, S], F32, kind=EI).ap()
    cw1 = nc.dram_tensor("cw1", [8, 128, 4096], F32, kind=EI).ap()
    cw2 = nc.dram_tensor("cw2", [32, 128, 1024], F32, kind=EI).ap()
    f_wqs = nc.dram_tensor("f_wqs", [8, 128, 512], F32, kind=EI).ap()
    f_wks = nc.dram_tensor("f_wks", [8, 128, 512], F32, kind=EI).ap()
    f_wvs = nc.dram_tensor("f_wvs", [8, 128, 512], F32, kind=EI).ap()
    f_bq = nc.dram_tensor("f_bq", [128, 4], F32, kind=EI).ap()
    f_bk = nc.dram_tensor("f_bk", [128, 4], F32, kind=EI).ap()
    f_bv = nc.dram_tensor("f_bv", [1, 512], F32, kind=EI).ap()
    f_l1g = nc.dram_tensor("f_l1g", [128, 8], F32, kind=EI).ap()
    f_l1b = nc.dram_tensor("f_l1b", [128, 8], F32, kind=EI).ap()
    f_wop = nc.dram_tensor("f_wop", [4, 128, D], F32, kind=EI).ap()
    m_bo = nc.dram_tensor("m_bo", [128, 8], F32, kind=EI).ap()
    m_l2g = nc.dram_tensor("m_l2g", [128, 8], F32, kind=EI).ap()
    m_l2b = nc.dram_tensor("m_l2b", [128, 8], F32, kind=EI).ap()
    m_gw = nc.dram_tensor("m_gw", [8, 128, 8], F32, kind=EI).ap()
    m_gbr = nc.dram_tensor("m_gbr", [1, 8], F32, kind=EI).ap()
    m_b1c = nc.dram_tensor("m_b1c", [8, 128, 32], F32, kind=EI).ap()
    m_b2r = nc.dram_tensor("m_b2r", [8, 1, D], F32, kind=EI).ap()
    ident = nc.dram_tensor("ident", [128, 128], F32, kind=EI).ap()
    p1_wT = nc.dram_tensor("p1_wT", [8, 128, 4096], F32, kind=EI).ap()
    p1_b0 = nc.dram_tensor("p1_b0", [1, 4096], F32, kind=EI).ap()
    p1_b1 = nc.dram_tensor("p1_b1", [1, 4096], F32, kind=EI).ap()
    p2_wT = nc.dram_tensor("p2_wT", [8, 128, 4160], F32, kind=EI).ap()
    p2_b0 = nc.dram_tensor("p2_b0", [1, 4096], F32, kind=EI).ap()
    p2_b1 = nc.dram_tensor("p2_b1", [1, 4096], F32, kind=EI).ap()
    s1_u0 = nc.dram_tensor("s1_u0", [8, 128, 2048], F32, kind=EI).ap()
    s1_u1 = nc.dram_tensor("s1_u1", [8, 128, 2112], F32, kind=EI).ap()
    s1_pt = nc.dram_tensor("s1_pt", [64, 1024], F32, kind=EI).ap()
    s1_al = nc.dram_tensor("s1_al", [1, 1024], F32, kind=EI).ap()
    s2_u0 = nc.dram_tensor("s2_u0", [8, 128, 2048], F32, kind=EI).ap()
    s2_u1 = nc.dram_tensor("s2_u1", [8, 128, 2112], F32, kind=EI).ap()
    s2_pt = nc.dram_tensor("s2_pt", [64, 1024], F32, kind=EI).ap()
    s2_al = nc.dram_tensor("s2_al", [1, 1024], F32, kind=EI).ap()
    xaz = nc.dram_tensor("xaz", [T + 1, 64], BF16, kind=EI).ap()
    l1_wgT = nc.dram_tensor("l1_wgT", [8, 128, 3072], I32, kind=EI).ap()
    l1_wuT = nc.dram_tensor("l1_wuT", [8, 128, 3072], I32, kind=EI).ap()
    l1_wdT = nc.dram_tensor("l1_wdT", [24, 128, 1024], I32, kind=EI).ap()
    yp1 = nc.dram_tensor("yp1", [8, 128, T], F32).ap()
    yp1g = nc.dram_tensor("yp1g", [8 * 1024, T], F32).ap()
    x1 = nc.dram_tensor("x1", [8, 128, S], F32).ap()
    x1g = nc.dram_tensor("x1g", [8 * 1024, S], F32).ap()
    w1o = nc.dram_tensor("w1o", [8, 128, 4096], BF16).ap()
    w1g = nc.dram_tensor("w1g", [8 * 1024, 4096], BF16).ap()
    w2o = nc.dram_tensor("w2o", [32, 128, 1024], BF16).ap()
    w2g = nc.dram_tensor("w2g", [8 * 4096, 1024], BF16).ap()
    yp2 = nc.dram_tensor("yp2", [8, 128, T], F32).ap()
    yp2g = nc.dram_tensor("yp2g", [8 * 1024, T], F32).ap()
    x2 = nc.dram_tensor("x2", [8, 128, S], F32).ap()
    x2g = nc.dram_tensor("x2g", [8 * 1024, S], F32).ap()
    xw1 = nc.dram_tensor("xw1", [T + 1, 4096], BF16).ap()
    sh1 = nc.dram_tensor("sh1", [T, 1024], BF16).ap()
    xw2 = nc.dram_tensor("xw2", [T + 1, 4160], BF16).ap()
    sh2 = nc.dram_tensor("sh2", [T, 1024], BF16).ap()
    x3 = nc.dram_tensor("x3", [8, 128, S], F32).ap()
    ya1l = nc.dram_tensor("ya1l", [8, 128, S], F32).ap()
    yb1l = nc.dram_tensor("yb1l", [8, 128, S], F32).ap()
    ya2l = nc.dram_tensor("ya2l", [8, 128, S], F32).ap()
    yb2l = nc.dram_tensor("yb2l", [8, 128, S], F32).ap()
    x1full = nc.dram_tensor("x1full", [8, 128, T], F32).ap()
    x2full = nc.dram_tensor("x2full", [8, 128, T], F32).ap()
    hml = nc.dram_tensor("hml", [S, 1024], BF16).ap()
    xout = nc.dram_tensor("xout", [8, 128, S], F32, kind=EO).ap()

    pid = nc.partition_id()
    r0 = (pid >> 1) * 2
    r1 = r0 + 1
    hh = pid & 1
    p = Prog(nc)

    def fin(tag, src, shp, dt):
        dbg = nc.dram_tensor("dbg", shp, dt, kind=EO).ap()
        p.barrier()
        p.dma(nc.sync, dbg, src)
        p.barrier()
        return nc

    emit_wcast(nc, p, "wc_", cw1, cw2, w1o, w2o)
    p.barrier()
    sem_w1 = p.allgather(w1o.rearrange("a p h -> (a p) h"), w1g)
    sem_w2 = p.allgather(w2o.rearrange("a p h -> (a p) h"), w2g)

    emit_l0attn(nc, p, "a_", a_xin, a_wqp, a_wdn, a_wkup, a_wvup, a_wop,
                rc2, rs2, yp1)
    p.barrier()
    sem_y1 = p.allgather(yp1.rearrange("a p t -> (a p) t"), yp1g)
    p.cc_wait(sem_y1)

    yp1g_v = yp1g.rearrange("(r q) (h t) -> r h q t", r=8, h=2)
    p.dma(nc.sync, ya1l.rearrange("a p t -> (a p) t"), yp1g_v[r0, hh])
    p.dma(nc.sync, yb1l.rearrange("a p t -> (a p) t"), yp1g_v[r1, hh])
    p.barrier()
    emit_comb(nc, p, "cb_", x_tok,
              lambda c: ya1l[c], lambda c: yb1l[c], x1)
    p.barrier()
    if upto == "x1":
        return fin("x1", x1.rearrange("a p t -> (a p) t"), [1024, S], F32)
    sem_x1 = p.allgather(x1.rearrange("a p t -> (a p) t"), x1g)
    p.cc_wait(sem_x1)

    x1g_v = x1g.rearrange("(r q) t -> r q t", r=8)
    x1f_v = x1full.rearrange("a p (h t) -> (a p) h t", h=2)
    p.dma(nc.sync, x1f_v[:, 0], x1g_v[r0])
    p.dma(nc.sync, x1f_v[:, 1], x1g_v[r1])
    p.barrier()
    emit_l0fattn(nc, p, "f_", x1full, f_wqs, f_wks, f_wvs, f_bq, f_bk, f_bv,
                 f_l1g, f_l1b, f_wop, yp2)
    p.barrier()
    sem_y2 = p.allgather(yp2.rearrange("a p t -> (a p) t"), yp2g)
    p.cc_wait(sem_y2)
    p.cc_wait(sem_w1)
    p.cc_wait(sem_w2)

    yp2g_v = yp2g.rearrange("(r q) (h t) -> r h q t", r=8, h=2)
    p.dma(nc.sync, ya2l.rearrange("a p t -> (a p) t"), yp2g_v[r0, hh])
    p.dma(nc.sync, yb2l.rearrange("a p t -> (a p) t"), yp2g_v[r1, hh])
    p.barrier()
    w1b_v = w1g.rearrange("(e a p) h -> e a p h", e=8, a=8)
    w2b_v = w2g.rearrange("(e n p) d -> e n p d", e=8, n=32)
    emit_l0fmoe(nc, p, "m_", x1,
                lambda c: ya2l[c], lambda c: yb2l[c],
                m_bo, m_l2g, m_l2b, m_gw, m_gbr, w1b_v, w2b_v, m_b1c, m_b2r,
                ident, x2)
    p.barrier()
    if upto == "x2":
        return fin("x2", x2.rearrange("a p t -> (a p) t"), [1024, S], F32)
    sem_x2 = p.allgather(x2.rearrange("a p t -> (a p) t"), x2g)
    p.cc_wait(sem_x2)

    x2g_v = x2g.rearrange("(r q) t -> r q t", r=8)
    x2f_v = x2full.rearrange("a p (h t) -> (a p) h t", h=2)
    p.dma(nc.sync, x2f_v[:, 0], x2g_v[r0])
    p.dma(nc.sync, x2f_v[:, 1], x2g_v[r1])
    p.barrier()
    emit_premid(nc, p, "p1_", 4096, "fm", x2full, p1_wT, p1_b0, p1_b1, xw1)
    p.barrier()
    if upto == "xw1":
        return fin("xw1", xw1[0:T], [T, 4096], BF16)

    xaz_v = xaz.rearrange("t (x o) -> t x o", o=1)
    emit_scan(nc, p, "s1_", xw1, xaz_v, s1_u0, s1_u1, s1_pt, s1_al, sh1)
    if upto == "sh1":
        return fin("sh1", sh1, [T, 1024], BF16)

    emit_premid(nc, p, "p2_", 4160, "sh", sh1, p2_wT, p2_b0, p2_b1, xw2,
                ident_in=ident)
    p.barrier()

    xa2_v = xw2.rearrange("t (x o) -> t x o", o=1)[:, 4096:4160]
    emit_scan(nc, p, "s2_", xw2[:, 0:4096], xa2_v,
              s2_u0, s2_u1, s2_pt, s2_al, sh2)
    if upto == "sh2":
        return fin("sh2", sh2, [T, 1024], BF16)

    sh2_v = sh2.rearrange("(h t) d -> h t d", h=2)
    p.dma(nc.sync, hml, sh2_v[hh])
    p.barrier()
    emit_post(nc, p, "po_", x2, hml, ident, x3)
    p.barrier()
    if upto == "x3":
        return fin("x3", x3.rearrange("a p t -> (a p) t"), [1024, S], F32)

    emit_l1ffn(nc, p, "l1_", x3, l1_wgT, l1_wuT, l1_wdT, xout)
    p.barrier()
    return nc


def build_inmaps(inp, x_fm):
    perm = []
    for h in range(16):
        perm += [64 * h + 2 * i for i in range(32)]
        perm += [64 * h + 2 * i + 1 for i in range(32)]
    perm = np.asarray(perm)
    wq_p = inp["L0_attn_wq"][:, perm]
    wk_p = inp["L0_attn_wk_up"][:, perm]
    inv = 1.0 / 10000.0 ** (np.arange(0, 64, 2, dtype=np.float32) / 64)
    ang = np.outer(np.arange(T, dtype=np.float32), inv)
    cos2 = np.ascontiguousarray(np.concatenate([np.cos(ang).T] * 4, axis=0)).astype(np.float32)
    sin2 = np.ascontiguousarray(np.concatenate([np.sin(ang).T] * 4, axis=0)).astype(np.float32)

    w_qkv, b_qkv = inp["L0_ffn_w_qkv"], inp["L0_ffn_b_qkv"]
    identm = np.eye(128, dtype=np.float32)

    def scan_prep(U, Bm, P, alpha):
        UT = np.ascontiguousarray(U.T)
        u0 = np.ascontiguousarray(UT.reshape(8, 128, 4096)[:, :, 0:2048])
        u1z = UT.reshape(8, 128, 4096)[:, :, 2048:4096]
        BmT = np.ascontiguousarray(Bm.T).reshape(8, 128, 64)
        u1 = np.ascontiguousarray(np.concatenate([u1z, BmT], axis=2))
        return (u0, u1, np.ascontiguousarray(P.T),
                np.ascontiguousarray(alpha.reshape(1, 1024)))

    z64 = np.zeros((64, 1024), np.float32)
    zP = np.zeros((1024, 64), np.float32)
    s1 = scan_prep(inp["L1_attn_Us"], z64, zP, inp["L1_attn_alpha"])
    s2 = scan_prep(inp["L1_attn_Um"], inp["L1_attn_Bm"], inp["L1_attn_P"],
                   np.ones(1024, np.float32))
    p1_wT = np.ascontiguousarray(inp["L1_attn_Ws"].T).reshape(8, 128, 4096)
    p2_wT = np.ascontiguousarray(
        np.concatenate([inp["L1_attn_Wm"].T, inp["L1_attn_A"].T], axis=1)
    ).reshape(8, 128, 4160)
    xaz = np.zeros((T + 1, 64), dtype=BF)
    wgT = np.ascontiguousarray(inp["L1_ffn_w_gate"].T).reshape(8, 128, 3072)
    wuT = np.ascontiguousarray(inp["L1_ffn_w_up"].T).reshape(8, 128, 3072)
    wdT = np.ascontiguousarray(inp["L1_ffn_w_down"].T).reshape(24, 128, 1024)

    maps = []
    for c in range(NCORES):
        b2, hh = c // 2, c % 2
        hcols = slice(512 * hh, 512 * (hh + 1))
        qcols = slice(512 * hh, 512 * (hh + 1))
        kcols = slice(1024 + 512 * hh, 1024 + 512 * (hh + 1))
        vcols = slice(2048 + 512 * hh, 2048 + 512 * (hh + 1))
        m = {
            "a_xin": np.ascontiguousarray(x_fm[:, b2 * T:(b2 + 1) * T]).reshape(8, 128, T),
            "a_wqp": np.ascontiguousarray(wq_p[:, hcols]).reshape(8, 128, 512),
            "a_wdn": np.ascontiguousarray(inp["L0_attn_w_down"]).reshape(8, 128, 512),
            "a_wkup": np.ascontiguousarray(wk_p[:, hcols]).reshape(4, 128, 512),
            "a_wvup": np.ascontiguousarray(
                inp["L0_attn_wv_up"][:, 512 * hh:512 * (hh + 1)]).reshape(4, 128, 512),
            "a_wop": np.ascontiguousarray(
                inp["L0_attn_wo"][512 * hh:512 * (hh + 1), :]).reshape(4, 128, D),
            "rc2": cos2, "rs2": sin2,
            "x_tok": np.ascontiguousarray(x_fm[:, c * S:(c + 1) * S]).reshape(8, 128, S),
            "cw1": np.ascontiguousarray(inp["L0_ffn_W1"][c]).reshape(8, 128, 4096),
            "cw2": np.ascontiguousarray(inp["L0_ffn_W2"][c]).reshape(32, 128, 1024),
            "f_wqs": np.ascontiguousarray(w_qkv[:, qcols]).reshape(8, 128, 512),
            "f_wks": np.ascontiguousarray(w_qkv[:, kcols]).reshape(8, 128, 512),
            "f_wvs": np.ascontiguousarray(w_qkv[:, vcols]).reshape(8, 128, 512),
            "f_bq": np.ascontiguousarray(b_qkv[qcols].reshape(4, 128).T),
            "f_bk": np.ascontiguousarray(b_qkv[kcols].reshape(4, 128).T),
            "f_bv": np.ascontiguousarray(b_qkv[vcols]).reshape(1, 512),
            "f_l1g": np.ascontiguousarray(inp["L0_ffn_ln1_g"].reshape(8, 128).T),
            "f_l1b": np.ascontiguousarray(inp["L0_ffn_ln1_b"].reshape(8, 128).T),
            "f_wop": np.ascontiguousarray(
                inp["L0_ffn_w_o"][512 * hh:512 * (hh + 1), :]).reshape(4, 128, D),
            "m_bo": np.ascontiguousarray(inp["L0_ffn_b_o"].reshape(8, 128).T),
            "m_l2g": np.ascontiguousarray(inp["L0_ffn_ln2_g"].reshape(8, 128).T),
            "m_l2b": np.ascontiguousarray(inp["L0_ffn_ln2_b"].reshape(8, 128).T),
            "m_gw": np.ascontiguousarray(inp["L0_ffn_gate_w"]).reshape(8, 128, 8),
            "m_gbr": np.ascontiguousarray(inp["L0_ffn_gate_b"]).reshape(1, 8),
            "m_b1c": np.ascontiguousarray(
                inp["L0_ffn_b1"].reshape(8, 32, 128).transpose(0, 2, 1)),
            "m_b2r": np.ascontiguousarray(inp["L0_ffn_b2"]).reshape(8, 1, 1024),
            "ident": identm,
            "p1_wT": p1_wT,
            "p1_b0": np.ascontiguousarray(inp["L1_attn_bWs"]).reshape(1, 4096),
            "p1_b1": np.ascontiguousarray(inp["L1_attn_bUs"]).reshape(1, 4096),
            "p2_wT": p2_wT,
            "p2_b0": np.ascontiguousarray(inp["L1_attn_bWm"]).reshape(1, 4096),
            "p2_b1": np.ascontiguousarray(inp["L1_attn_bUm"]).reshape(1, 4096),
            "s1_u0": s1[0], "s1_u1": s1[1], "s1_pt": s1[2], "s1_al": s1[3],
            "s2_u0": s2[0], "s2_u1": s2[1], "s2_pt": s2[2], "s2_al": s2[3],
            "xaz": xaz,
            "l1_wgT": wgT, "l1_wuT": wuT, "l1_wdT": wdT,
        }
        maps.append(m)
    return maps


def kernel(**inputs):
    inp = {k: np.asarray(v) for k, v in inputs.items()}
    x = inp["L0_attn_x"]
    x_fm = np.ascontiguousarray(x.reshape(NTOK, D).T)
    maps = build_inmaps(inp, x_fm)
    nc = build_full()
    res = run_bass_kernel_spmd(nc, maps, core_ids=list(range(NCORES)))
    out_fm = np.concatenate([res.results[c]["xout"].reshape(D, S)
                             for c in range(NCORES)], axis=1)
    return np.ascontiguousarray(out_fm.T).reshape(B, T, D)
